# version 04
# baseline (speedup 1.0000x reference)
import sys
sys.path.insert(0, '/opt/trn_rl_repo')

import numpy as np
import concourse.bass as bass
import concourse.mybir as mybir
from concourse.alu_op_type import AluOpType
from concourse.bass_utils import run_bass_kernel_spmd

F32 = mybir.dt.float32
BF16 = mybir.dt.bfloat16
I8 = mybir.dt.int8
AF = mybir.ActivationFunctionType

B_ = 4
S = 4096
D = 1024
H = 16
FT = 2848
SPLIT = 2816
DEPTH = 6
W1OF = 2816
W1C = 22
NEG = -1.0e30
STAGE = 'full'
NLAYERS = DEPTH


def _tiles(total, step, start=0):
    o, out = start, []
    while o < total:
        out.append((o, min(step, total - o)))
        o += step
    return out


NT_FT = _tiles(FT, 512)
NT_S = _tiles(S, 512)
MEGAS = [(0, 1024), (1024, 1024), (1824, 1024)]
NT_OV = [(0, 512), (512, 512), (1024, 512), (1536, 512), (2048, 512), (2336, 512)]


def _tile_lhsT(W):
    K, N = W.shape
    Kc, Nc = (K + 127) // 128, (N + 127) // 128
    Wp = np.zeros((Kc * 128, Nc * 128), np.float32)
    Wp[:K, :N] = W
    return np.ascontiguousarray(
        Wp.reshape(Kc, 128, Nc, 128).transpose(1, 2, 0, 3), np.float32)


def _tile_rhs(W):
    K, N = W.shape
    Kc = (K + 127) // 128
    Wp = np.zeros((Kc * 128, N), np.float32)
    Wp[:K] = W
    return np.ascontiguousarray(Wp.reshape(Kc, 128, N).transpose(1, 0, 2), np.float32)


def _rotary_tables():
    HD = 64
    half32 = np.arange(0, HD, 2, dtype=np.float32)
    inv_freq = (np.float32(1.0) / np.float32(10000.0) ** (half32 / np.float32(HD)))
    fr32 = (np.arange(S, dtype=np.float32)[:, None] * inv_freq[None, :])
    freqs = np.concatenate([fr32.astype(np.float64)] * 2, -1)
    bs32 = ((half32 + np.float32(0.4 * HD)) / np.float32(1.4 * HD))
    power = ((np.arange(S, dtype=np.float32) - np.float32(S // 2)) / np.float32(128.0))
    sc32 = (bs32[None, :].astype(np.float64) ** power[:, None].astype(np.float64)
            ).astype(np.float32)
    scale = np.concatenate([sc32.astype(np.float64)] * 2, -1)
    tau = np.where(np.arange(HD) < 32, 1.0, -1.0)[None, :]
    cs_q = (np.cos(freqs) * scale).T
    sn_q = (np.sin(freqs) * scale * tau).T
    cs_k = (np.cos(freqs) / scale).T
    sn_k = (np.sin(freqs) / scale * tau).T
    def st(a):
        full = np.concatenate([a, a], 0)
        hi = full.astype(np.float32)
        lo = (full - hi.astype(np.float64)).astype(np.float32)
        return np.ascontiguousarray(hi), np.ascontiguousarray(lo)
    return st(cs_q), st(sn_q), st(cs_k), st(sn_k)


def _alibi_toep():
    base = (2.0 ** 8) ** (1.0 / H)
    m64 = np.asarray([1.0 / base ** (i + 1) for i in range(H)], np.float64)
    m32 = m64.astype(np.float32)
    x32 = np.arange(8192, dtype=np.float32) - np.float32(4095.0)
    d64 = np.arange(8192, dtype=np.float64) - 4095.0
    T = np.zeros((H, 8192), np.float32)
    for h in range(H):
        b32 = (m32[h] * x32).astype(np.float32).astype(np.float64)
        T[h] = np.exp(b32 - m64[h] * d64).astype(np.float32)
    return np.ascontiguousarray(T)


def _alibi_g():
    base = (2.0 ** 8) ** (1.0 / H)
    m = np.asarray([1.0 / base ** (i + 1) for i in range(H)], np.float64)
    j = np.arange(S, dtype=np.float64)
    g = np.exp(m[None, :] * (j[:, None] - (S - 1))).astype(np.float32)
    return np.ascontiguousarray(g.reshape(S // 128, 128, H).transpose(1, 0, 2)
                                .reshape(128, (S // 128) * H))


def _band_masks():
    ms = np.zeros((128, 4, 256), np.float32)
    kk = np.arange(128)[:, None]
    ii = np.arange(256)[None, :]
    for kco in range(4):
        jj = 128 * kco + kk
        ms[:, kco, :] = np.where((jj >= ii) & (jj <= ii + 256), 0.0, NEG)
    return np.ascontiguousarray(ms.reshape(128, 1024))


def _perm_rot():
    P = np.zeros((128, 128), np.float32)
    for c in range(128):
        P[(c - 32 if (c % 64) >= 32 else c + 32), c] = 1.0
    return P


def _indicators():
    ind_ss = np.zeros((128, 8, 16), np.float32)
    ind_bc = np.zeros((16, 8, 128), np.float32)
    for sx in range(8):
        for p in range(128):
            r = 2 * sx + (p // 64)
            ind_ss[p, sx, r] = 1.0
            ind_bc[r, sx, p] = 1.0
    return (np.ascontiguousarray(ind_ss.reshape(128, 128)),
            np.ascontiguousarray(ind_bc.reshape(16, 1024)))


def prep_inputs(inputs):
    x = np.asarray(inputs['L0_attn_x'], np.float32)
    Wkqv = np.asarray(inputs['L0_attn_W_kqv'], np.float32)
    iWqkv = np.asarray(inputs['L0_ffn_Wqkv'], np.float32)
    qs = np.asarray(inputs['L0_ffn_q_scale'], np.float32)
    ks = np.asarray(inputs['L0_ffn_k_scale'], np.float32)
    Wo = np.asarray(inputs['L0_ffn_Wo'], np.float32)
    lng = np.asarray(inputs['L0_ffn_ln_g'], np.float32)
    lnb = np.asarray(inputs['L0_ffn_ln_b'], np.float32)
    W1 = np.asarray(inputs['L0_ffn_W1'], np.float32)
    b1 = np.asarray(inputs['L0_ffn_b1'], np.float32)
    W2 = np.asarray(inputs['L0_ffn_W2'], np.float32)
    b2 = np.asarray(inputs['L0_ffn_b2'], np.float32)

    sh = {}
    sh['o_wk'] = _tile_lhsT(Wkqv[:, 0:D])
    sh['o_wq'] = _tile_lhsT(Wkqv[:, D:2 * D])
    sh['o_wv'] = _tile_rhs(Wkqv[:, 2 * D:3 * D])
    for i in range(DEPTH):
        sh[f'wq{i}'] = _tile_lhsT(iWqkv[i][:, 0:D])
        sh[f'wk{i}'] = _tile_lhsT(iWqkv[i][:, D:2 * D])
        sh[f'wv{i}'] = _tile_rhs(iWqkv[i][:, 2 * D:3 * D])
        sh[f'wo{i}'] = _tile_lhsT(Wo[i])
        W1a = np.zeros((D, W1OF), np.float32); W1a[:, :2730] = W1[i][:, :2730]
        W1g = np.zeros((D, W1OF), np.float32); W1g[:, :2730] = W1[i][:, 2730:]
        sh[f'w1a{i}'] = _tile_lhsT(W1a)
        sh[f'w1g{i}'] = _tile_lhsT(W1g)
        W2p = np.zeros((W1OF, D), np.float32); W2p[:2730] = W2[i]
        sh[f'w2{i}'] = _tile_lhsT(W2p)

    vecs = np.zeros((128, DEPTH, 70), np.float32)
    for i in range(DEPTH):
        vecs[:, i, 0] = np.tile(qs[i], 2)
        vecs[:, i, 1] = np.tile(ks[i], 2)
        for c in range(8):
            vecs[:, i, 2 + c] = lng[i][128 * c:128 * (c + 1)]
            vecs[:, i, 10 + c] = lnb[i][128 * c:128 * (c + 1)]
            vecs[:, i, 62 + c] = b2[i][128 * c:128 * (c + 1)]
        b1a = np.zeros(W1OF, np.float32); b1a[:2730] = b1[i][:2730]
        b1g = np.zeros(W1OF, np.float32); b1g[:2730] = b1[i][2730:]
        for c in range(W1C):
            vecs[:, i, 18 + c] = b1a[128 * c:128 * (c + 1)]
            vecs[:, i, 40 + c] = b1g[128 * c:128 * (c + 1)]
    sh['lvecs'] = np.ascontiguousarray(vecs.reshape(128, DEPTH * 70))
    sh['toep'] = _alibi_toep()
    sh['galibi'] = _alibi_g()
    sh['masks'] = _band_masks()
    sh['perm'] = _perm_rot()
    sh['ind_ss'], sh['ind_bc'] = _indicators()
    sh['i128'] = np.eye(128, dtype=np.float32)

    (cs_q, cs_ql), (sn_q, sn_ql), (cs_k, cs_kl), (sn_k, sn_kl) = _rotary_tables()
    in_maps = []
    for c in range(8):
        b = c // 2
        F0 = 0 if c % 2 == 0 else S - FT
        d = dict(sh)
        d['xT'] = np.ascontiguousarray(x[b].T)
        xtf = np.zeros((D, 3072), np.float32)
        xtf[:, 0:FT] = x[b].T[:, F0:F0 + FT]
        d['xTf'] = xtf
        d['rcq'] = np.ascontiguousarray(cs_q[:, F0:F0 + FT])
        d['rsq'] = np.ascontiguousarray(sn_q[:, F0:F0 + FT])
        d['rck'] = np.ascontiguousarray(cs_k[:, F0:F0 + FT])
        d['rsk'] = np.ascontiguousarray(sn_k[:, F0:F0 + FT])
        d['rcql'] = np.ascontiguousarray(cs_ql[:, F0:F0 + FT])
        d['rsql'] = np.ascontiguousarray(sn_ql[:, F0:F0 + FT])
        d['rckl'] = np.ascontiguousarray(cs_kl[:, F0:F0 + FT])
        d['rskl'] = np.ascontiguousarray(sn_kl[:, F0:F0 + FT])
        in_maps.append(d)
    return in_maps



class Ctr:
    def __init__(self, h):
        self.h, self.n = h, 0


class Bank:
    def __init__(self, t):
        self.t = t
        self.free = []

    def consumed(self, ctr, val):
        self.free = [(ctr, val)]

    def also_consumed(self, ctr, val):
        self.free.append((ctr, val))


def build_nc():
    nc = bass.Bass("TRN2", target_bir_lowering=False, debug=False,
                   detect_race_conditions=False)
    from contextlib import ExitStack
    stack = ExitStack()

    ap = {}
    def din(name, shape, dt=F32):
        ap[name] = nc.dram_tensor(name, list(shape), dt, kind="ExternalInput").ap()

    din('xT', (D, S)); din('xTf', (D, 3072))
    din('o_wk', (128, 8, 8, 128)); din('o_wq', (128, 8, 8, 128)); din('o_wv', (128, 8, D))
    for i in range(DEPTH):
        din(f'wq{i}', (128, 8, 8, 128)); din(f'wk{i}', (128, 8, 8, 128))
        din(f'wv{i}', (128, 8, D)); din(f'wo{i}', (128, 8, 8, 128))
        din(f'w1a{i}', (128, W1C, 8, 128)); din(f'w1g{i}', (128, W1C, 8, 128))
        din(f'w2{i}', (128, 8, W1C, 128))
    din('lvecs', (128, DEPTH * 70)); din('toep', (H, 8192))
    din('galibi', (128, (S // 128) * H))
    din('masks', (128, 1024)); din('perm', (128, 128)); din('i128', (128, 128))
    din('ind_ss', (128, 128)); din('ind_bc', (16, 1024))
    din('rcq', (128, FT)); din('rsq', (128, FT)); din('rck', (128, FT)); din('rsk', (128, FT))
    din('rcql', (128, FT)); din('rsql', (128, FT)); din('rckl', (128, FT)); din('rskl', (128, FT))

    out = nc.dram_tensor('out', [D, FT], F32, kind="ExternalOutput").ap()
    st_t = [nc.dram_tensor('st_a', [D, 1 + FT], F32).ap(),
            nc.dram_tensor('st_b', [D, 1 + FT], F32).ap()]
    x1k = nc.dram_tensor('x1k', [D, FT], F32).ap()

    stv = [t.rearrange('(c p) t -> p c t', p=128) for t in st_t]
    x1kv = x1k.rearrange('(c p) t -> p c t', p=128)
    outv = out.rearrange('(c p) t -> p c t', p=128)
    xTv = ap['xT'].rearrange('(c p) t -> p c t', p=128)
    xTfv = ap['xTf'].rearrange('(c p) t -> p c t', p=128)

    q = {e: [] for e in ('sync', 'vector', 'scalar', 'tensor', 'gpsimd')}
    def on(e, fn):
        q[e].append(fn)

    def sem(name):
        return Ctr(stack.enter_context(nc.semaphore(name)))
    NQS = 8
    ldp = [sem(f's_ld{j}') for j in range(NQS)]
    wlp = [sem(f's_wl{j}') for j in range(NQS)]
    stp = [sem(f's_st{j}') for j in range(NQS)]
    qrot = {'sync': [ldp, 0], 'gpsimd': [wlp, 0], 'scalar': [stp, 0]}
    nmp = [sem(f's_nm{j}') for j in range(40)]
    nmmap = {}
    def named(name):
        if name not in nmmap:
            nmmap[name] = nmp[len(nmmap)]
        return nmmap[name]
    s_pe = sem('s_pe'); s_ve = sem('s_ve'); s_ac = sem('s_ac')
    ALL = ldp + wlp + stp + nmp + [s_pe, s_ve, s_ac]
    def w(eng, pairs):
        for (ctr, n) in pairs:
            if n > 0:
                eng.wait_ge(ctr.h, n)

    def dma(queue, _ctr, dst, src, waits=(), slow=False, stream=None):
        if stream is None:
            pool, idx = qrot[queue]
            qrot[queue][1] = (idx + 1) % NQS
            ctr = pool[idx]
        else:
            ctr = named(stream)
        ctr.n += 16
        def f(eng, dst=dst, src=src, waits=tuple(waits), slow=slow, h=ctr.h):
            w(eng, waits)
            eng.dma_start(out=dst, in_=src,
                          allow_slow_non_contiguous=slow).then_inc(h, 16)
        on(queue, f)
        return (ctr, ctr.n)

    def ld(dst, src, waits=(), stream=None):
        return dma('sync', None, dst, src, waits, stream=stream)

    def wl(dst, src, waits=(), stream=None):
        return dma('gpsimd', None, dst, src, waits, stream=stream)

    def sto(dst, src, waits=(), stream=None):
        return dma('scalar', None, dst, src, waits, stream=stream)

    def _eng_emit(queue, ctr, fn, waits):
        ctr.n += 1
        def f(e, fn=fn, waits=tuple(waits), h=ctr.h):
            w(e, waits)
            fn(e).then_inc(h, 1)
        on(queue, f)
        return ctr.n

    def ve(fn, waits=()):
        return _eng_emit('vector', s_ve, fn, waits)

    def ac(fn, waits=()):
        return _eng_emit('scalar', s_ac, fn, waits)

    def pe(fn, waits=()):
        return _eng_emit('tensor', s_pe, fn, waits)

    def barrier():
        vals = tuple((c, c.n) for c in ALL)
        for e in q:
            def f(eng, vals=vals):
                w(eng, vals)
            on(e, f)

    def sb(name, shape, dt):
        return stack.enter_context(nc.sbuf_tensor(name, list(shape), dt))

    c_masks = sb('c_masks', (128, 1024), BF16)
    c_perm = sb('c_perm', (128, 128), BF16)
    c_i128 = sb('c_i128', (128, 128), BF16)
    c_indss = sb('c_indss', (128, 128), BF16)
    c_indbcf = sb('c_indbcf', (16, 1024), F32)
    c_onesf = sb('c_onesf', (128, 128), F32)
    c_eps = sb('c_eps', (128, 1), F32)
    c_lv = sb('c_lv', (128, DEPTH * 70), F32)
    c_g = sb('c_g', (128, (S // 128) * H), F32)

    _ab = (int(nc.sbuf_base) + 63) // 64 * 64 + 64
    ARENA = (int(nc.sbuf_top) - _ab) // 64 * 64

    class Arena:
        def __init__(self):
            self.off = 0
            self.marks = []
            self.base = _ab
        def al(self, cols, dt):
            esz = 2 if dt == BF16 else 4
            a = (self.off + 63) // 64 * 64
            self.off = a + cols * esz
            assert self.off <= ARENA, f"arena overflow {self.off} > {ARENA}"
            return nc.alloc_sbuf_tensor_at('ar', (128, cols), dt,
                                           offset=self.base + a)
        def push(self):
            self.marks.append(self.off)
        def pop(self):
            self.off = self.marks.pop()
    A = Arena()

    def psum(name, cols):
        return stack.enter_context(nc.psum_tensor(name, [128, cols], F32))
    pA = Bank(psum('pA', 1024))
    pB = Bank(psum('pB', 1024))
    p4 = Bank(psum('p4', 512)); p5 = Bank(psum('p5', 512))
    p6 = Bank(psum('p6', 512)); p7 = Bank(psum('p7', 512))

    def lvec(i, col):
        return c_lv[:, i * 70 + col:i * 70 + col + 1]

    wl(c_masks[:, :], ap['masks'])
    wl(c_perm[:, :], ap['perm'])
    wl(c_i128[:, :], ap['i128'])
    wl(c_indss[:, :], ap['ind_ss'])
    ld(c_indbcf[:, :], ap['ind_bc'])
    ld(c_lv[:, :], ap['lvecs'])
    ld(c_g[:, :], ap['galibi'])
    ve(lambda v: v.memset(c_onesf[:, :], 1.0))
    ve(lambda v: v.memset(c_eps[:, :], 1e-5))
    zbuf = A.al(8, F32)
    vz = ve(lambda v: v.memset(zbuf[:, :], 0.0))
    for t in range(2):
        dma('scalar', None, stv[t][:, :, 0:1], zbuf[:, :, None],
            waits=[(s_ve, vz)], slow=True)
    barrier()

    if STAGE == 'consts':
        return _finish(nc, q, ALL, w, on, stack)

    NSLOT = 6
    wslots = []
    wslot_free = [[] for _ in range(NSLOT)]
    wst = {'i': 0}

    def load_w(name, of, ni=8, sub=0):
        a = ap[name]
        si = wst['i']; wst['i'] = (si + 1) % NSLOT
        src = a[:, of, sub:sub + ni, :].rearrange('p a b -> p (a b)')
        lv = wl(wslots[si][:, 0:ni * 128], src, waits=wslot_free[si],
                stream=f'ws{si}')
        wslot_free[si] = []
        def sl(k, si=si):
            return wslots[si][:, 128 * k:128 * (k + 1)]
        def rel(si=si, ctr=s_pe):
            wslot_free[si] = [(ctr, ctr.n)]
        return sl, lv, rel

    def load_w_rhs(name, ci):
        a = ap[name]
        si = wst['i']; wst['i'] = (si + 1) % NSLOT
        lv = wl(wslots[si][:, 0:D], a[:, ci, :], waits=wslot_free[si],
                stream=f'ws{si}')
        wslot_free[si] = []
        def sl(c0, c1, si=si):
            return wslots[si][:, c0:c1]
        def rel(si=si):
            wslot_free[si] = [(s_pe, s_pe.n)]
        return sl, lv, rel

    def ln_stats_chain(srows_fn, nn):
        m1 = ve(lambda v: v.tensor_scalar_mul(out=srows_fn(0), in0=p4.t[0:1, 0:nn],
                                              scalar1=1.0 / D))
        m2 = ve(lambda v: v.tensor_scalar_mul(out=srows_fn(1), in0=p5.t[0:1, 0:nn],
                                              scalar1=1.0 / D))
        p4.consumed(s_ve, m1)
        p5.consumed(s_ve, m2)
        m3 = ac(lambda s: s.square(out=srows_fn(2), in_=srows_fn(0)), waits=[(s_ve, m2)])
        m4 = ve(lambda v: v.tensor_tensor(out=srows_fn(3), in0=srows_fn(1), in1=srows_fn(2),
                                          op=AluOpType.subtract), waits=[(s_ac, m3)])
        m5 = ac(lambda s: s.activation(out=srows_fn(4), in_=srows_fn(3), func=AF.Sqrt,
                                       bias=c_eps[0:1, 0:1]), waits=[(s_ve, m4)])
        m6 = ve(lambda v: v.reciprocal(out=srows_fn(5), in_=srows_fn(4)), waits=[(s_ac, m5)])
        m7 = ve(lambda v: v.tensor_tensor(out=srows_fn(6), in0=srows_fn(0), in1=srows_fn(5),
                                          op=AluOpType.mult))
        m8 = ve(lambda v: v.tensor_scalar_mul(out=srows_fn(7), in0=srows_fn(6), scalar1=-1.0))
        b1 = pe(lambda t: t.matmul(p6.t[:, 0:nn], c_onesf[0:1, :], srows_fn(5),
                                   start=True, stop=True), waits=[(s_ve, m8)] + p6.free)
        b2 = pe(lambda t: t.matmul(p7.t[:, 0:nn], c_onesf[0:1, :], srows_fn(7),
                                   start=True, stop=True), waits=p7.free)
        return b2


    ktd = nc.dram_tensor('ktd', [8, 128, S], BF16).ap()
    vtd = nc.dram_tensor('vtd', [32, 128, 1040], BF16).ap()
    tdd = nc.dram_tensor('tdd', [H, 128, 7040], BF16).ap()

    A.push()
    wkb = A.al(8192, BF16)
    A.push()
    wvb = A.al(8192, BF16)
    xsb = [A.al(4096, BF16), A.al(4096, BF16)]
    xsb_free = [[], []]
    kstg = [A.al(512, BF16), A.al(512, BF16)]
    kstg_free = [[], []]
    vstg_free = [[], []]
    stagB_free = []
    vstg = [A.al(1040, BF16), A.al(1040, BF16)]

    for h in range(H):
        wl(tdd[h], bass.AP(tensor=ap['toep'].tensor, offset=h * 8192 + 1024,
                           ap=[[1, 128], [1, 7040]]), stream='td')

    for of in range(8):
        wk_lv = wl(wkb[:, 1024 * of:1024 * (of + 1)],
                   ap['o_wk'][:, of, :, :].rearrange('p a b -> p (a b)'),
                   stream='wk')
    for ci in range(8):
        wv_lv = wl(wvb[:, 1024 * ci:1024 * (ci + 1)], ap['o_wv'][:, ci, :],
                   stream='wv')

    for nt_i, (o, n) in enumerate(NT_S):
        sl_ = nt_i % 2
        xv = wl(xsb[sl_][:, 0:8 * n].rearrange('p (c t) -> p c t', c=8),
                xTv[:, :, o:o + n],
                waits=xsb_free[sl_] + [wk_lv], stream=f'xs{sl_}')
        for of in range(8):
            bank = pA if of % 2 == 0 else pB
            ks_ = kstg[of % 2]
            def kmm(t, sl_=sl_, n=n, of=of, bank=bank):
                last = None
                for ci in range(8):
                    last = t.matmul(bank.t[:, 0:n],
                                    wkb[:, of * 1024 + ci * 128:of * 1024 + (ci + 1) * 128],
                                    xsb[sl_][:, ci * n:ci * n + n],
                                    start=(ci == 0), stop=(ci == 7))
                return last
            pv = pe(kmm, waits=[xv] + bank.free)
            av = ac(lambda s, n=n, bank=bank, ks_=ks_: s.copy(out=ks_[:, 0:n],
                                                              in_=bank.t[:, 0:n]),
                    waits=[(s_pe, pv)] + kstg_free[of % 2])
            bank.consumed(s_ac, av)
            kstg_free[of % 2] = []
            kv = sto(ktd[of][:, o:o + n], ks_[:, 0:n], waits=[(s_ac, av)],
                     stream=f'kst{of % 2}')
            kstg_free[of % 2] = [kv]
        for tb in range(0, n, 128):
            kc = (o + tb) // 128
            vs_ = vstg[kc % 2]
            ev2 = 0
            for half in range(2):
                bank = p4 if half == 0 else p5
                def vmm(t, sl_=sl_, n=n, tb=tb, half=half, bank=bank):
                    last = None
                    for ci in range(8):
                        last = t.matmul(bank.t[:, 0:512],
                                        xsb[sl_][:, ci * n + tb:ci * n + tb + 128],
                                        wvb[:, ci * 1024 + half * 512:ci * 1024 + half * 512 + 512],
                                        start=(ci == 0), stop=(ci == 7))
                    return last
                pv = pe(vmm, waits=[xv, wv_lv] + bank.free)
                gs = c_g[:, kc * H + half * 8:kc * H + half * 8 + 8]
                e1 = ve(lambda v, half=half, bank=bank, vs_=vs_, gs=gs:
                        v.tensor_tensor(
                            out=vs_[:, half * 520:half * 520 + 520]
                                .rearrange('p (h e) -> p h e', h=8)[:, :, 0:64],
                            in0=bank.t[:, 0:512].rearrange('p (h e) -> p h e', h=8),
                            in1=gs[:, :, None].broadcast_to((128, 8, 64)),
                            op=AluOpType.mult),
                        waits=[(s_pe, pv)] + (vstg_free[kc % 2] if half == 0 else []))
                ev2 = ve(lambda v, half=half, vs_=vs_, gs=gs:
                         v.tensor_copy(
                             out=vs_[:, half * 520:half * 520 + 520]
                                 .rearrange('p (h e) -> p h e', e=65)[:, :, 64:65],
                             in_=gs[:, :, None]))
                bank.consumed(s_ve, e1)
            vstg_free[kc % 2] = []
            vv_ = sto(vtd[kc], vs_[:, 0:1040], waits=[(s_ve, ev2)], stream=f'vst{kc % 2}')
            vstg_free[kc % 2] = [vv_]
        xsb_free[sl_] = [(s_pe, s_pe.n)]

    if STAGE.startswith('o1'):
        A.pop()
        A.pop()
        wslots.extend(A.al(1024, BF16) for _ in range(NSLOT))
        barrier()
        return _finish(nc, q, ALL, w, on, stack)

    A.pop()
    A.push()
    wqb = wkb
    NO2 = 256
    NT_O2 = _tiles(3072, NO2)
    QTf = A.al(8 * 3072, BF16)
    xsq = [A.al(8 * NO2, BF16), A.al(8 * NO2, BF16)]
    xsq_free = [[], []]
    for of in range(8):
        wq_lv = wl(wqb[:, 1024 * of:1024 * (of + 1)],
                   ap['o_wq'][:, of, :, :].rearrange('p a b -> p (a b)'),
                   waits=[(s_ac, s_ac.n), (s_pe, s_pe.n)], stream='wk')
    Pq = [Bank(pA.t[:, 0:256]), Bank(pA.t[:, 512:768]),
          Bank(pB.t[:, 0:256]), Bank(pB.t[:, 512:768])]
    for nt_i, (o, n) in enumerate(NT_O2):
        sl_ = nt_i % 2
        xv = wl(xsq[sl_][:, 0:8 * n].rearrange('p (c t) -> p c t', c=8),
                xTfv[:, :, o:o + n],
                waits=xsq_free[sl_] + ([wq_lv] if nt_i == 0 else []),
                stream=f'xq{sl_}')
        for of in range(8):
            bank = Pq[(nt_i * 8 + of) % 4]
            def qmm(t, sl_=sl_, n=n, of=of, bank=bank):
                last = None
                for ci in range(8):
                    last = t.matmul(bank.t[:, 0:n],
                                    wqb[:, of * 1024 + ci * 128:of * 1024 + (ci + 1) * 128],
                                    xsq[sl_][:, ci * n:ci * n + n],
                                    start=(ci == 0), stop=(ci == 7))
                return last
            pv = pe(qmm, waits=[xv] + bank.free)
            qrdy = ac(lambda s, of=of, o=o, n=n, bank=bank:
                      s.copy(out=QTf[:, of * 3072 + o:of * 3072 + o + n],
                             in_=bank.t[:, 0:n]),
                      waits=[(s_pe, pv)])
            bank.consumed(s_ac, qrdy)
        xsq_free[sl_] = [(s_pe, s_pe.n)]
    qf_rdy = s_ac.n

    KTp = A.al(S, BF16)
    vtp = A.al(32 * 130, BF16)
    TDs = A.al(2 * 7040, BF16)
    prA = A.al(1024, BF16); prB = A.al(1024, BF16)
    aob = A.al(8 * 3072, BF16)
    stagB = A.al(3072, BF16)
    rA = A.al(NO2, F32); rB = A.al(NO2, F32)
    RA = A.al(NO2, F32); RB = A.al(NO2, F32)

    _td0 = TDs[:, :]

    def tdap(h2, kq, q0):
        return bass.AP(tensor=_td0.tensor,
                       offset=int(_td0.offset) + h2 * 7040 + 3071 - q0 + 512 * kq,
                       ap=[[int(_td0.ap[0][0]), 128], [128, 4], [-1, 256]])

    for hc in range(8):
        ktv = wl(KTp[:, :], ktd[hc], waits=[(s_pe, s_pe.n)], stream='ktp')
        vtv = wl(vtp[:, :].rearrange('p (a b) -> p a b', b=130),
                 vtd[:, :, 130 * hc:130 * hc + 130].rearrange('a p b -> p a b'),
                 waits=[(s_pe, s_pe.n)], stream='vtp')
        tdv = wl(TDs[:, 0:7040], tdd[2 * hc], waits=[(s_ve, s_ve.n)], stream='tds')
        tdv = wl(TDs[:, 7040:14080], tdd[2 * hc + 1], stream='tds')
        for nt_i, (q0, n) in enumerate(NT_O2):
            for kq in range(8):
                pvsc = 0
                for kcl in range(4):
                    kc = kq * 4 + kcl
                    def smm(t, kc=kc, kcl=kcl, q0=q0, n=n, hc=hc):
                        t.matmul(pA.t[:, kcl * 256:kcl * 256 + n],
                                 KTp[0:64, kc * 128:(kc + 1) * 128],
                                 QTf[0:64, hc * 3072 + q0:hc * 3072 + q0 + n],
                                 start=True, stop=True)
                        return t.matmul(pB.t[:, kcl * 256:kcl * 256 + n],
                                        KTp[64:128, kc * 128:(kc + 1) * 128],
                                        QTf[64:128, hc * 3072 + q0:hc * 3072 + q0 + n],
                                        start=True, stop=True, tile_position=(64, 0))
                    wt = ([ktv, (s_ac, qf_rdy)] + pA.free + pB.free) \
                        if (kq == 0 and kcl == 0) else []
                    pvsc = pe(smm, waits=wt)
                eA = ac(lambda s: s.activation(out=prA[:, 0:1024], in_=pA.t[:, 0:1024],
                                               func=AF.Exp, scale=1.0 / 32.0),
                        waits=[(s_pe, pvsc)])
                eB = ac(lambda s: s.activation(out=prB[:, 0:1024], in_=pB.t[:, 0:1024],
                                               func=AF.Exp, scale=1.0 / 32.0))
                pA.consumed(s_ac, eA)
                pB.consumed(s_ac, eB)
                tm1 = ve(lambda v, kq=kq, q0=q0: v.tensor_tensor(
                            out=prA[:, 0:1024].rearrange('p (a b) -> p a b', a=4),
                            in0=prA[:, 0:1024].rearrange('p (a b) -> p a b', a=4),
                            in1=tdap(0, kq, q0), op=AluOpType.mult),
                         waits=[(s_ac, eA)] + ([tdv] if (nt_i == 0 and kq == 0) else []))
                tm2 = ve(lambda v, kq=kq, q0=q0: v.tensor_tensor(
                            out=prB[:, 0:1024].rearrange('p (a b) -> p a b', a=4),
                            in0=prB[:, 0:1024].rearrange('p (a b) -> p a b', a=4),
                            in1=tdap(1, kq, q0), op=AluOpType.mult),
                         waits=[(s_ac, eB)])
                def avmm(t, kq=kq, n=n):
                    last = None
                    for kcl in range(4):
                        kc = kq * 4 + kcl
                        st_ = (kc == 0)
                        sp = (kc == 31)
                        t.matmul(p4.t[0:65, 0:n], vtp[:, kc * 130:kc * 130 + 65],
                                 prA[:, kcl * 256:kcl * 256 + n], start=st_, stop=sp)
                        last = t.matmul(p5.t[0:65, 0:n],
                                        vtp[:, kc * 130 + 65:kc * 130 + 130],
                                        prB[:, kcl * 256:kcl * 256 + n],
                                        start=st_, stop=sp)
                    return last
                wt = [(s_ve, tm2), vtv] + ((p4.free + p5.free) if kq == 0 else [])
                pvav = pe(avmm, waits=wt)
            r1 = ve(lambda v, n=n: v.reciprocal(out=rA[64:65, 0:n], in_=p4.t[64:65, 0:n]),
                    waits=[(s_pe, pvav)])
            r2 = ve(lambda v, n=n: v.reciprocal(out=rB[64:65, 0:n], in_=p5.t[64:65, 0:n]))
            b1 = pe(lambda t, n=n: t.matmul(p6.t[0:64, 0:n], c_onesf[64:65, 0:64],
                                            rA[64:65, 0:n], start=True, stop=True),
                    waits=[(s_ve, r1)] + p6.free)
            b2 = pe(lambda t, n=n: t.matmul(p7.t[0:64, 0:n], c_onesf[64:65, 0:64],
                                            rB[64:65, 0:n], start=True, stop=True),
                    waits=[(s_ve, r2)] + p7.free)
            a1 = ac(lambda s, n=n: s.copy(out=RA[0:64, 0:n], in_=p6.t[0:64, 0:n]),
                    waits=[(s_pe, b1)])
            a2 = ac(lambda s, n=n: s.copy(out=RB[0:64, 0:n], in_=p7.t[0:64, 0:n]),
                    waits=[(s_pe, b2)])
            p6.consumed(s_ac, a1)
            p7.consumed(s_ac, a2)
            v1 = ve(lambda v, hc=hc, q0=q0, n=n: v.tensor_tensor(
                        out=aob[0:64, hc * 3072 + q0:hc * 3072 + q0 + n],
                        in0=p4.t[0:64, 0:n], in1=RA[0:64, 0:n], op=AluOpType.mult),
                    waits=[(s_ac, a1)])
            v2 = ve(lambda v, q0=q0, n=n: v.tensor_tensor(
                        out=stagB[0:64, q0:q0 + n],
                        in0=p5.t[0:64, 0:n], in1=RB[0:64, 0:n], op=AluOpType.mult),
                    waits=[(s_ac, a2)] + (stagB_free if nt_i == 0 else []))
            p4.consumed(s_ve, v1)
            p5.consumed(s_ve, v2)
        pk = ld(aob[64:128, hc * 3072:hc * 3072 + 3072], stagB[0:64, 0:3072],
                waits=[(s_ve, s_ve.n)], stream='pk')
        stagB_free = [pk]
    ao_rdy = [pk]

    x2f = A.al(NO2, F32)
    sqf = A.al(NO2, F32)
    tf = A.al(8 * NO2, F32)
    srow = A.al(8 * NO2, F32)
    for nt_i, (o, n) in enumerate(NT_O2):
        pvs = 0
        t1l = 0
        for c in range(8):
            xv2 = ld(x2f[:, 0:n], xTfv[:, c, o:o + n],
                     waits=[(s_ve, s_ve.n)], stream='x2f')
            t1 = ve(lambda v, c=c, o=o, n=n: v.tensor_tensor(
                        out=tf[:, c * NO2:c * NO2 + n],
                        in0=aob[:, c * 3072 + o:c * 3072 + o + n],
                        in1=x2f[:, 0:n], op=AluOpType.add),
                    waits=[xv2] + (ao_rdy if (nt_i == 0 and c == 0) else [])
                          + ([(s_pe, s_pe.n)] if c == 0 else []))
            sq = ac(lambda s, c=c, n=n: s.square(out=sqf[:, 0:n],
                                                 in_=tf[:, c * NO2:c * NO2 + n]),
                    waits=[(s_ve, t1), (s_pe, s_pe.n)])
            def summ(t, c=c, n=n):
                t.matmul(p4.t[0:1, 0:n], c_onesf[:, 0:1], tf[:, c * NO2:c * NO2 + n],
                         start=(c == 0), stop=(c == 7))
                return t.matmul(p5.t[0:1, 0:n], c_onesf[:, 0:1], sqf[:, 0:n],
                                start=(c == 0), stop=(c == 7))
            wt = [(s_ac, sq)] + ((p4.free + p5.free) if c == 0 else [])
            pvs = pe(summ, waits=wt)
        on('vector', lambda v, pvs=pvs: v.wait_ge(s_pe.h, pvs))
        bc = ln_stats_chain(lambda j, n=n: srow[0:1, j * NO2:j * NO2 + n], n)
        lastv = 0
        for c in range(8):
            ve(lambda v, c=c, n=n: v.tensor_tensor(
                    out=tf[:, c * NO2:c * NO2 + n], in0=tf[:, c * NO2:c * NO2 + n],
                    in1=p6.t[:, 0:n], op=AluOpType.mult),
               waits=[(s_pe, bc)] if c == 0 else [])
            lastv = ve(lambda v, c=c, n=n: v.tensor_tensor(
                    out=tf[:, c * NO2:c * NO2 + n], in0=tf[:, c * NO2:c * NO2 + n],
                    in1=p7.t[:, 0:n], op=AluOpType.add))
        p6.consumed(s_ve, lastv)
        p7.consumed(s_ve, lastv)
        ns = min(n, FT - o)
        if ns > 0:
            sto(stv[0][:, :, 1 + o:1 + o + ns],
                tf[:, 0:8 * NO2].rearrange('p (c t) -> p c t', c=8)[:, :, 0:ns],
                waits=[(s_ve, lastv)])
            sto(x1kv[:, :, o:o + ns],
                tf[:, 0:8 * NO2].rearrange('p (c t) -> p c t', c=8)[:, :, 0:ns])
            if STAGE == 'outer':
                sto(outv[:, :, o:o + ns],
                    tf[:, 0:8 * NO2].rearrange('p (c t) -> p c t', c=8)[:, :, 0:ns])
        barrier()
    A.pop()
    A.pop()
    wslots.extend(A.al(1024, BF16) for _ in range(NSLOT))

    SLmax = 1280

    def emit_layer(i):
        st_in, st_mid = stv[0], stv[1]
        A.push()
        xsb_l = A.al(8 * SLmax, BF16)
        QT = A.al(8 * 1024, BF16)
        KTl = A.al(8 * SLmax, BF16)
        VTl = A.al(10 * 1040, BF16)
        rn = A.al(2304, F32)
        wvbl = A.al(8 * D, BF16)
        for ci in range(8):
            wv_rdy = wl(wvbl[:, ci * D:(ci + 1) * D], ap[f'wv{i}'][:, ci, :],
                        waits=[(s_pe, s_pe.n)] if ci == 0 else [], stream='wvb')

        def emit_mega(m0, mn):
            s0 = max(0, m0 - 256)
            s1 = m0 + mn
            SLn = s1 - s0
            qoff = m0 - s0
            nkc = (SLn + 127) // 128
            A.push()
            rtab = A.al(4 * SLmax, F32)
            rlo = A.al(SLmax, F32)
            At = A.al(SLmax, F32)
            Bt = A.al(SLmax, F32)
            raw = A.al(8 * SLmax, F32)
            sqb = [A.al(512, BF16), A.al(512, BF16)]
            ts1 = [A.al(512, F32), A.al(512, F32)]
            vbs = [A.al(512, BF16), A.al(512, BF16)]

            wl(xsb_l[:, 0:4 * SLn].rearrange('p (c t) -> p c t', c=4),
               st_in[:, 0:4, 1 + s0:1 + s1],
               waits=[(s_pe, s_pe.n), (s_ac, s_ac.n)], stream='xspan')
            xv1 = wl(xsb_l[:, 4 * SLn:8 * SLn].rearrange('p (c t) -> p c t', c=4),
                     st_in[:, 4:8, s0:s1], stream='xspan')
            rtv = 0
            for j, nm in enumerate(('rcq', 'rsq', 'rck', 'rsk')):
                rtv = ld(rtab[:, j * SLmax:j * SLmax + SLn], ap[nm][:, s0:s1],
                         waits=[(s_ve, s_ve.n)] if j == 0 else [], stream='rtab')
            LONM = ('rcql', 'rsql', 'rckl', 'rskl')

            def rslice(j, a, b):
                return rtab[:, j * SLmax + a:j * SLmax + b]

            def side(sb_, wname, dest, dcols, r0, rl, tabj):
                tv1 = ve(lambda v: v.tensor_scalar_mul(out=At[:, 0:rl], in0=rslice(tabj[0], r0, r0 + rl),
                                                       scalar1=lvec(i, sb_ // 8)),
                         waits=[rtv])
                lv1 = ld(rlo[:, 0:rl], ap[LONM[tabj[0]]][:, s0 + r0:s0 + r0 + rl],
                         waits=[(s_ve, s_ve.n)], stream='rlo')
                tvx = ve(lambda v: v.scalar_tensor_tensor(
                       out=At[:, 0:rl], in0=rlo[:, 0:rl],
                       scalar=lvec(i, sb_ // 8), in1=At[:, 0:rl],
                       op0=AluOpType.mult, op1=AluOpType.add), waits=[lv1])
                tv2 = ve(lambda v: v.tensor_scalar_mul(out=Bt[:, 0:rl], in0=rslice(tabj[1], r0, r0 + rl),
                                                       scalar1=lvec(i, sb_ // 8)))
                lv2 = ld(rlo[:, 0:rl], ap[LONM[tabj[1]]][:, s0 + r0:s0 + r0 + rl],
                         waits=[(s_ve, s_ve.n)], stream='rlo')
                tv2 = ve(lambda v: v.scalar_tensor_tensor(
                       out=Bt[:, 0:rl], in0=rlo[:, 0:rl],
                       scalar=lvec(i, sb_ // 8), in1=Bt[:, 0:rl],
                       op0=AluOpType.mult, op1=AluOpType.add), waits=[lv2])
                ntl = _tiles(rl, 512)
                Pb = [Bank(pA.t[:, 0:512]), Bank(pA.t[:, 512:1024]),
                      Bank(pB.t[:, 0:512]), Bank(pB.t[:, 512:1024])]
                for bk in Pb:
                    bk.free = pA.free + pB.free
                SSb = [p4, p5, p6][:len(ntl)]
                wpre = []
                for of in range(2):
                    wpre.append(load_w(wname, of))
                pvlast = 0
                for of in range(8):
                    wsl, wlv, wrel = wpre[of]
                    if of + 2 < 8:
                        wpre.append(load_w(wname, of + 2))
                    for tix, (to, tn) in enumerate(ntl):
                        bk = Pb[(of * len(ntl) + tix) % 4]
                        def pmm(t, wsl=wsl, to=to, tn=tn, bk=bk, r0=r0):
                            last = None
                            for ci in range(8):
                                last = t.matmul(bk.t[:, 0:tn], wsl(ci),
                                                xsb_l[:, ci * SLn + r0 + to:ci * SLn + r0 + to + tn],
                                                start=(ci == 0), stop=(ci == 7))
                            return last
                        pv = pe(pmm, waits=[wlv, xv1] + bk.free)
                        a1 = ac(lambda s, of=of, to=to, tn=tn, bk=bk:
                                s.copy(out=raw[:, of * SLn + to:of * SLn + to + tn],
                                       in_=bk.t[:, 0:tn]), waits=[(s_pe, pv)])
                        sl2 = (of * len(ntl) + tix) % 2
                        a2 = ac(lambda s, of=of, to=to, tn=tn, sl2=sl2:
                                s.square(out=sqb[sl2][:, 0:tn],
                                         in_=raw[:, of * SLn + to:of * SLn + to + tn]),
                                waits=[(s_pe, s_pe.n)])
                        bk.consumed(s_ac, a1)
                        def ssm(t, of=of, tn=tn, sl2=sl2, sb_=sb_, tix=tix):
                            return t.matmul(SSb[tix].t[0:16, 0:tn],
                                            c_indss[:, of * 16:of * 16 + 16],
                                            sqb[sl2][:, 0:tn],
                                            start=(of == 0), stop=(of == 7))
                        wt = [(s_ac, a2)] + (SSb[tix].free if of == 0 else [])
                        pvlast = pe(ssm, waits=wt)
                    wrel()
                rnv = 0
                for tix, (to, tn) in enumerate(ntl):
                    roff = (sb_ // 8) * 1024
                    c1 = ac(lambda s, to=to, tn=tn, tix=tix, roff=roff:
                            s.activation(out=rn[0:16, roff + to:roff + to + tn],
                                         in_=SSb[tix].t[0:16, 0:tn],
                                         func=AF.Sqrt), waits=[(s_pe, pvlast)])
                    SSb[tix].consumed(s_ac, c1)
                    c2 = ve(lambda v, to=to, tn=tn, roff=roff:
                            v.tensor_scalar_max(out=rn[0:16, roff + to:roff + to + tn],
                                                in0=rn[0:16, roff + to:roff + to + tn],
                                                scalar1=1e-12),
                            waits=[(s_ac, c1)])
                    rnv = ve(lambda v, to=to, tn=tn, roff=roff:
                             v.reciprocal(out=rn[0:16, roff + to:roff + to + tn],
                                          in_=rn[0:16, roff + to:roff + to + tn]))
                Rb2 = [Bank(pA.t[:, 0:512]), Bank(pA.t[:, 512:1024])]
                Pp = [Bank(pB.t[:, 0:512]), Bank(pB.t[:, 512:1024])]
                for bk in Rb2 + Pp:
                    bk.free = [(s_ac, s_ac.n), (s_ve, s_ve.n)]
                for of in range(8):
                    for tix, (to, tn) in enumerate(ntl):
                        k2 = (of * len(ntl) + tix) % 2
                        v1 = ve(lambda v, of=of, to=to, tn=tn, k2=k2:
                                v.tensor_tensor(out=ts1[k2][:, 0:tn],
                                                in0=raw[:, of * SLn + to:of * SLn + to + tn],
                                                in1=At[:, to:to + tn], op=AluOpType.mult),
                                waits=[(s_ve, rnv)] if (of == 0 and tix == 0) else [])
                        v2 = ve(lambda v, of=of, to=to, tn=tn, k2=k2:
                                v.tensor_tensor(out=vbs[k2][:, 0:tn],
                                                in0=raw[:, of * SLn + to:of * SLn + to + tn],
                                                in1=Bt[:, to:to + tn], op=AluOpType.mult),
                                waits=[(s_pe, s_pe.n)])
                        pp = Pp[k2]
                        pv1 = pe(lambda t, tn=tn, k2=k2, pp=pp:
                                 t.matmul(pp.t[:, 0:tn], c_perm[:, :], vbs[k2][:, 0:tn],
                                          start=True, stop=True),
                                 waits=[(s_ve, v2)] + pp.free)
                        v3 = ve(lambda v, tn=tn, k2=k2, pp=pp:
                                v.tensor_tensor(out=ts1[k2][:, 0:tn], in0=ts1[k2][:, 0:tn],
                                                in1=pp.t[:, 0:tn], op=AluOpType.add),
                                waits=[(s_pe, pv1)])
                        pp.consumed(s_ve, v3)
                        rb = Rb2[k2]
                        roff2 = (sb_ // 8) * 1024
                        pv2 = pe(lambda t, of=of, to=to, tn=tn, rb=rb, roff2=roff2:
                                 t.matmul(rb.t[0:128, 0:tn],
                                          c_indbcf[0:16, of * 128:of * 128 + 128],
                                          rn[0:16, roff2 + to:roff2 + to + tn],
                                          start=True, stop=True),
                                 waits=[(s_ve, rnv)] + rb.free)
                        v4 = ve(lambda v, of=of, to=to, tn=tn, k2=k2, rb=rb:
                                v.tensor_tensor(out=dest(of, to, tn), in0=ts1[k2][:, 0:tn],
                                                in1=rb.t[:, 0:tn], op=AluOpType.mult),
                                waits=[(s_pe, pv2)])
                        rb.consumed(s_ve, v4)
                pA.free = [(s_ve, s_ve.n)]
                pB.free = [(s_ve, s_ve.n)]

            side(0, f'wq{i}',
                 lambda of, to, tn: QT[:, of * mn + to:of * mn + to + tn],
                 mn, qoff, mn, (0, 1))
            side(8, f'wk{i}',
                 lambda of, to, tn: KTl[:, of * SLn + to:of * SLn + to + tn],
                 SLn, 0, SLn, (2, 3))

            for tb in range(0, SLn, 128):
                kcr = tb // 128
                kw = min(128, SLn - tb)
                for half in range(2):
                    bk = p4 if half == 0 else p5
                    def vmm(t, tb=tb, kw=kw, half=half, bk=bk):
                        last = None
                        for ci in range(8):
                            last = t.matmul(bk.t[0:kw, 0:512],
                                            xsb_l[:, ci * SLn + tb:ci * SLn + tb + kw],
                                            wvbl[:, ci * D + half * 512:ci * D + half * 512 + 512],
                                            start=(ci == 0), stop=(ci == 7))
                        return last
                    pv = pe(vmm, waits=[wv_rdy, xv1] + bk.free)
                    e1 = ve(lambda v, kcr=kcr, kw=kw, half=half, bk=bk:
                            v.tensor_copy(
                                out=VTl[0:kw, kcr * 1040 + half * 520:kcr * 1040 + half * 520 + 520]
                                    .rearrange('p (h e) -> p h e', h=8)[:, :, 0:64],
                                in_=bk.t[0:kw, 0:512].rearrange('p (h e) -> p h e', h=8)),
                            waits=[(s_pe, pv)])
                    bk.consumed(s_ve, e1)
                ve(lambda v, kcr=kcr, kw=kw:
                   v.memset(VTl[0:kw, kcr * 1040:kcr * 1040 + 1040]
                            .rearrange('p (h e) -> p h e', e=65)[:, :, 64:65], 1.0))
            A.pop()

            A.push()
            prA = A.al(1024, BF16); prB = A.al(1024, BF16)
            rA = A.al(256, F32); rB = A.al(256, F32)
            RA = A.al(256, F32); RB = A.al(256, F32)
            aob = A.al(8 * 1024, BF16)
            stagBb = A.al(8 * 1024, BF16)
            res2 = [A.al(512, F32), A.al(512, F32)]
            yout = [A.al(512, F32), A.al(512, F32)]
            last_y = [[], []]
            qts = _tiles(mn, 256)
            for hc in range(8):
                for (qo, qn) in qts:
                    q0 = m0 + qo
                    kcos = [k for k in range(4) if q0 - 256 + 128 * k >= 0
                            and q0 - 256 + 128 * k < s1]
                    pvsc = 0
                    for kco in kcos:
                        kabs = q0 - 256 + 128 * kco
                        kcr = (kabs - s0) // 128
                        kw = min(128, SLn - kcr * 128)
                        def smm(t, hc=hc, kco=kco, kcr=kcr, kw=kw, qo=qo, qn=qn):
                            t.matmul(pA.t[0:kw, kco * 256:kco * 256 + qn],
                                     KTl[0:64, hc * SLn + kcr * 128:hc * SLn + kcr * 128 + kw],
                                     QT[0:64, hc * mn + qo:hc * mn + qo + qn],
                                     start=True, stop=False)
                            t.matmul(pA.t[0:kw, kco * 256:kco * 256 + qn],
                                     c_i128[0:kw, 0:kw],
                                     c_masks[0:kw, kco * 256:kco * 256 + qn],
                                     start=False, stop=True)
                            t.matmul(pB.t[0:kw, kco * 256:kco * 256 + qn],
                                     KTl[64:128, hc * SLn + kcr * 128:hc * SLn + kcr * 128 + kw],
                                     QT[64:128, hc * mn + qo:hc * mn + qo + qn],
                                     start=True, stop=False, tile_position=(64, 0))
                            return t.matmul(pB.t[0:kw, kco * 256:kco * 256 + qn],
                                            c_i128[0:kw, 0:kw],
                                            c_masks[0:kw, kco * 256:kco * 256 + qn],
                                            start=False, stop=True)
                        wt = (pA.free + pB.free) if kco == kcos[0] else []
                        pvsc = pe(smm, waits=wt)
                    eA = ac(lambda s: s.activation(out=prA[:, 0:1024], in_=pA.t[:, 0:1024],
                                                   func=AF.Exp, scale=8.0),
                            waits=[(s_pe, pvsc)])
                    eB = ac(lambda s: s.activation(out=prB[:, 0:1024], in_=pB.t[:, 0:1024],
                                                   func=AF.Exp, scale=8.0))
                    pA.consumed(s_ac, eA)
                    pB.consumed(s_ac, eB)
                    def avmm(t, kcos=tuple(kcos), q0=q0, qn=qn, s0=s0, hc=hc):
                        last = None
                        for ix, kco in enumerate(kcos):
                            kabs = q0 - 256 + 128 * kco
                            kcr = (kabs - s0) // 128
                            kw = min(128, SLn - kcr * 128)
                            st_ = (ix == 0)
                            sp = (ix == len(kcos) - 1)
                            t.matmul(p4.t[0:65, 0:qn],
                                     VTl[0:kw, kcr * 1040 + 65 * 2 * hc:kcr * 1040 + 65 * 2 * hc + 65],
                                     prA[0:kw, kco * 256:kco * 256 + qn], start=st_, stop=sp)
                            last = t.matmul(p5.t[0:65, 0:qn],
                                            VTl[0:kw, kcr * 1040 + 65 * (2 * hc + 1):kcr * 1040 + 65 * (2 * hc + 1) + 65],
                                            prB[0:kw, kco * 256:kco * 256 + qn], start=st_, stop=sp)
                        return last
                    pvav = pe(avmm, waits=[(s_ac, eB)] + p4.free + p5.free)
                    r1 = ve(lambda v, qn=qn: v.reciprocal(out=rA[64:65, 0:qn],
                                                          in_=p4.t[64:65, 0:qn]),
                            waits=[(s_pe, pvav)])
                    r2 = ve(lambda v, qn=qn: v.reciprocal(out=rB[64:65, 0:qn],
                                                          in_=p5.t[64:65, 0:qn]))
                    b1 = pe(lambda t, qn=qn: t.matmul(p6.t[0:64, 0:qn], c_onesf[64:65, 0:64],
                                                      rA[64:65, 0:qn], start=True, stop=True),
                            waits=[(s_ve, r1)] + p6.free)
                    b2 = pe(lambda t, qn=qn: t.matmul(p7.t[0:64, 0:qn], c_onesf[64:65, 0:64],
                                                      rB[64:65, 0:qn], start=True, stop=True),
                            waits=[(s_ve, r2)] + p7.free)
                    a1 = ac(lambda s, qn=qn: s.copy(out=RA[0:64, 0:qn], in_=p6.t[0:64, 0:qn]),
                            waits=[(s_pe, b1)])
                    a2 = ac(lambda s, qn=qn: s.copy(out=RB[0:64, 0:qn], in_=p7.t[0:64, 0:qn]),
                            waits=[(s_pe, b2)])
                    p6.consumed(s_ac, a1)
                    p7.consumed(s_ac, a2)
                    v1 = ve(lambda v, hc=hc, qo=qo, qn=qn: v.tensor_tensor(
                                out=aob[0:64, hc * mn + qo:hc * mn + qo + qn],
                                in0=p4.t[0:64, 0:qn], in1=RA[0:64, 0:qn], op=AluOpType.mult),
                            waits=[(s_ac, a1)])
                    v2 = ve(lambda v, hc=hc, qo=qo, qn=qn: v.tensor_tensor(
                                out=stagBb[0:64, hc * mn + qo:hc * mn + qo + qn],
                                in0=p5.t[0:64, 0:qn], in1=RB[0:64, 0:qn], op=AluOpType.mult),
                            waits=[(s_ac, a2)])
                    p4.consumed(s_ve, v1)
                    p5.consumed(s_ve, v2)
            pk = ld(aob[64:128, 0:8 * mn], stagBb[0:64, 0:8 * mn],
                    waits=[(s_ve, s_ve.n)], stream='pk')
            wpre = [load_w(f'wo{i}', 0), load_w(f'wo{i}', 1)]
            ntm = _tiles(mn, 512)
            for of in range(8):
                wsl, wlv, wrel = wpre[of]
                if of + 2 < 8:
                    wpre.append(load_w(f'wo{i}', of + 2))
                for tix, (to, tn) in enumerate(ntm):
                    bk = p6 if (of * len(ntm) + tix) % 2 == 0 else p7
                    slr = (of * len(ntm) + tix) % 2
                    xr = ld(res2[slr][:, 0:tn], st_in[:, of, 1 + m0 + to:1 + m0 + to + tn],
                            waits=[(s_ve, s_ve.n)], stream=f'res{slr}')
                    def womm(t, wsl=wsl, to=to, tn=tn, bk=bk):
                        last = None
                        for ci in range(8):
                            last = t.matmul(bk.t[:, 0:tn], wsl(ci),
                                            aob[:, ci * mn + to:ci * mn + to + tn],
                                            start=(ci == 0), stop=(ci == 7))
                        return last
                    pv = pe(womm, waits=[wlv, pk] + bk.free)
                    vv = ve(lambda v, tn=tn, bk=bk, slr=slr: v.tensor_tensor(
                                out=yout[slr][:, 0:tn], in0=bk.t[:, 0:tn],
                                in1=res2[slr][:, 0:tn], op=AluOpType.add),
                            waits=[(s_pe, pv), xr] + last_y[slr])
                    bk.consumed(s_ve, vv)
                    last_y[slr] = [sto(st_mid[:, of, 1 + m0 + to:1 + m0 + to + tn],
                                       yout[slr][:, 0:tn],
                                       waits=[(s_ve, vv)], stream=f'sy{slr}')]
                wrel()
            A.pop()
            barrier()

        for (m0_, mn_) in MEGAS:
            emit_mega(m0_, mn_)
        A.pop()

        A.push()
        xsf = A.al(8 * 512, F32)
        sqf = [A.al(512, F32), A.al(512, F32)]
        t1s = [A.al(512, F32), A.al(512, F32)]
        xnb = A.al(8 * 512, BF16)
        gels = [A.al(512, F32), A.al(512, F32)]
        prodb = A.al(W1C * 512, BF16)
        resf = [A.al(512, F32), A.al(512, F32)]
        zf = [A.al(512, F32), A.al(512, F32)]
        last_z = [[], []]
        srow2 = A.al(8 * 512, F32)


        for (o, n) in NT_OV:
            ld(xsf[:, 0:4 * n].rearrange('p (c t) -> p c t', c=4),
               st_mid[:, 0:4, 1 + o:1 + o + n],
               waits=[(s_ve, s_ve.n), (s_ac, s_ac.n)], stream='ffnx')
            xv1 = ld(xsf[:, 4 * n:8 * n].rearrange('p (c t) -> p c t', c=4),
                     st_mid[:, 4:8, o:o + n], stream='ffnx')
            pvs = 0
            for c in range(8):
                slc = c % 2
                sq = ac(lambda s, c=c, n=n, slc=slc: s.square(out=sqf[slc][:, 0:n],
                                                              in_=xsf[:, c * n:c * n + n]),
                        waits=([xv1] if c == 0 else []) + [(s_pe, s_pe.n)])
                def summ(t, c=c, n=n, slc=slc):
                    t.matmul(p4.t[0:1, 0:n], c_onesf[:, 0:1], xsf[:, c * n:c * n + n],
                             start=(c == 0), stop=(c == 7))
                    return t.matmul(p5.t[0:1, 0:n], c_onesf[:, 0:1], sqf[slc][:, 0:n],
                                    start=(c == 0), stop=(c == 7))
                wt = [(s_ac, sq)] + ((p4.free + p5.free) if c == 0 else [])
                pvs = pe(summ, waits=wt)
            on('vector', lambda v, pvs=pvs: v.wait_ge(s_pe.h, pvs))
            bc = ln_stats_chain(lambda j, n=n: srow2[0:1, j * 512:j * 512 + n], n)
            lnv = 0
            for c in range(8):
                slc = c % 2
                va = ve(lambda v, c=c, n=n, slc=slc: v.tensor_tensor(
                            out=t1s[slc][:, 0:n], in0=xsf[:, c * n:c * n + n],
                            in1=p6.t[:, 0:n], op=AluOpType.mult),
                        waits=[(s_pe, bc)] if c == 0 else [])
                vb = ve(lambda v, c=c, n=n, slc=slc: v.tensor_tensor(
                            out=t1s[slc][:, 0:n], in0=t1s[slc][:, 0:n],
                            in1=p7.t[:, 0:n], op=AluOpType.add))
                lnv = ve(lambda v, c=c, n=n, slc=slc: v.tensor_scalar(
                            out=xnb[:, c * n:c * n + n], in0=t1s[slc][:, 0:n],
                            scalar1=lvec(i, 2 + c), scalar2=lvec(i, 10 + c),
                            op0=AluOpType.mult, op1=AluOpType.add))
            p6.consumed(s_ve, lnv)
            p7.consumed(s_ve, lnv)
            wpre = [load_w(f'w1g{i}', 0), load_w(f'w1a{i}', 0),
                    load_w(f'w1g{i}', 1), load_w(f'w1a{i}', 1)]
            for af in range(W1C):
                wg, wglv, wgrel = wpre[2 * af]
                wa, walv, warel = wpre[2 * af + 1]
                if af + 2 < W1C:
                    wpre.append(load_w(f'w1g{i}', af + 2))
                    wpre.append(load_w(f'w1a{i}', af + 2))
                bkg = Bank(pA.t[:, 0:512]) if af % 2 == 0 else Bank(pB.t[:, 0:512])
                bka = Bank(pA.t[:, 512:1024]) if af % 2 == 0 else Bank(pB.t[:, 512:1024])
                src = pA if af % 2 == 0 else pB
                bkg.free = src.free
                bka.free = src.free
                def w1mm(t, wg=wg, wa=wa, n=n, bkg=bkg, bka=bka):
                    for ci in range(8):
                        t.matmul(bkg.t[:, 0:n], wg(ci), xnb[:, ci * n:ci * n + n],
                                 start=(ci == 0), stop=(ci == 7))
                    last = None
                    for ci in range(8):
                        last = t.matmul(bka.t[:, 0:n], wa(ci), xnb[:, ci * n:ci * n + n],
                                        start=(ci == 0), stop=(ci == 7))
                    return last
                pv = pe(w1mm, waits=[wglv, walv, (s_ve, lnv)] + bkg.free)
                slg = af % 2
                ga = ac(lambda s, af=af, n=n, slg=slg, bkg=bkg: s.activation(
                            out=gels[slg][:, 0:n], in_=bkg.t[:, 0:n], func=AF.Gelu,
                            bias=lvec(i, 40 + af)),
                        waits=[(s_pe, pv), (s_ve, s_ve.n)])
                pr = ve(lambda v, af=af, n=n, slg=slg, bka=bka: v.scalar_tensor_tensor(
                            out=prodb[:, af * n:af * n + n], in0=bka.t[:, 0:n],
                            scalar=lvec(i, 18 + af), in1=gels[slg][:, 0:n],
                            op0=AluOpType.add, op1=AluOpType.mult),
                        waits=[(s_ac, ga)])
                src.consumed(s_ve, pr)
                src.also_consumed(s_ac, ga)
                wgrel(); warel()
            w2pre = [load_w(f'w2{i}', 0, ni=8, sub=0), load_w(f'w2{i}', 0, ni=8, sub=8),
                     load_w(f'w2{i}', 0, ni=6, sub=16)]
            for of in range(8):
                if of + 1 < 8:
                    w2pre.append(load_w(f'w2{i}', of + 1, ni=8, sub=0))
                    w2pre.append(load_w(f'w2{i}', of + 1, ni=8, sub=8))
                    w2pre.append(load_w(f'w2{i}', of + 1, ni=6, sub=16))
                g0, g1, g2 = w2pre[3 * of], w2pre[3 * of + 1], w2pre[3 * of + 2]
                bk = p6 if of % 2 == 0 else p7
                slr = of % 2
                xr = ld(resf[slr][:, 0:n], st_mid[:, of, 1 + o:1 + o + n],
                        waits=[(s_ve, s_ve.n)], stream=f'fr{slr}')
                def w2mm(t, g0=g0, g1=g1, g2=g2, n=n, bk=bk):
                    last = None
                    for af in range(W1C):
                        grp, k = (g0, af) if af < 8 else ((g1, af - 8) if af < 16 else (g2, af - 16))
                        last = t.matmul(bk.t[:, 0:n], grp[0](k),
                                        prodb[:, af * n:af * n + n],
                                        start=(af == 0), stop=(af == W1C - 1))
                    return last
                pv = pe(w2mm, waits=[g0[1], g1[1], g2[1],
                                     (s_ve, s_ve.n)] + bk.free)
                vv = ve(lambda v, of=of, n=n, bk=bk, slr=slr: v.scalar_tensor_tensor(
                            out=zf[slr][:, 0:n], in0=bk.t[:, 0:n], scalar=lvec(i, 62 + of),
                            in1=resf[slr][:, 0:n], op0=AluOpType.add, op1=AluOpType.add),
                        waits=[(s_pe, pv), xr] + last_z[slr])
                bk.consumed(s_ve, vv)
                last_z[slr] = [sto(st_in[:, of, 1 + o:1 + o + n], zf[slr][:, 0:n],
                                   waits=[(s_ve, vv)], stream=f'sz{slr}')]
                g0[2](); g1[2](); g2[2]()
            barrier()
        A.pop()

    def emit_final():
        A.push()
        tb = A.al(8 * 512, F32)
        x1b = A.al(8 * 512, F32)
        sqf = [A.al(512, F32), A.al(512, F32)]
        srow3 = A.al(8 * 512, F32)
        outb = A.al(8 * 512, F32)


        for (o, n) in NT_OV:
            zv = ld(tb[:, 0:8 * n].rearrange('p (c t) -> p c t', c=8),
                    stv[0][:, :, 1 + o:1 + o + n], waits=[(s_ve, s_ve.n)], stream='fz')
            xv = ld(x1b[:, 0:8 * n].rearrange('p (c t) -> p c t', c=8),
                    x1kv[:, :, o:o + n], stream='fx')
            pvs = 0
            for c in range(8):
                slc = c % 2
                va = ve(lambda v, c=c, n=n: v.tensor_tensor(
                            out=tb[:, c * n:c * n + n], in0=tb[:, c * n:c * n + n],
                            in1=x1b[:, c * n:c * n + n], op=AluOpType.add),
                        waits=[zv, xv] if c == 0 else [])
                sq = ac(lambda s, c=c, n=n, slc=slc: s.square(out=sqf[slc][:, 0:n],
                                                              in_=tb[:, c * n:c * n + n]),
                        waits=[(s_ve, va), (s_pe, s_pe.n)])
                def summ(t, c=c, n=n, slc=slc):
                    t.matmul(p4.t[0:1, 0:n], c_onesf[:, 0:1], tb[:, c * n:c * n + n],
                             start=(c == 0), stop=(c == 7))
                    return t.matmul(p5.t[0:1, 0:n], c_onesf[:, 0:1], sqf[slc][:, 0:n],
                                    start=(c == 0), stop=(c == 7))
                wt = [(s_ac, sq)] + ((p4.free + p5.free) if c == 0 else [])
                pvs = pe(summ, waits=wt)
            on('vector', lambda v, pvs=pvs: v.wait_ge(s_pe.h, pvs))
            bc = ln_stats_chain(lambda j, n=n: srow3[0:1, j * 512:j * 512 + n], n)
            lastv = 0
            for c in range(8):
                ve(lambda v, c=c, n=n: v.tensor_tensor(
                        out=outb[:, c * n:c * n + n], in0=tb[:, c * n:c * n + n],
                        in1=p6.t[:, 0:n], op=AluOpType.mult),
                   waits=[(s_pe, bc)] if c == 0 else [])
                lastv = ve(lambda v, c=c, n=n: v.tensor_tensor(
                        out=outb[:, c * n:c * n + n], in0=outb[:, c * n:c * n + n],
                        in1=p7.t[:, 0:n], op=AluOpType.add))
            p6.consumed(s_ve, lastv)
            p7.consumed(s_ve, lastv)
            sto(outv[:, :, o:o + n],
                outb[:, 0:8 * n].rearrange('p (c t) -> p c t', c=8),
                waits=[(s_ve, lastv)])
            barrier()
        A.pop()

    if STAGE == 'full':
        for i in range(NLAYERS):
            emit_layer(i)
        emit_final()

    return _finish(nc, q, ALL, w, on, stack)


def _finish(nc, q, ALL, w, on, stack):
    final = tuple((c, c.n) for c in ALL)
    for e in q:
        def f(eng, final=final):
            w(eng, final)
        on(e, f)
    with nc.Block() as block:
        @block.sync
        def _(sy):
            for f in q['sync']:
                f(sy)
        @block.vector
        def _(v):
            for f in q['vector']:
                f(v)
        @block.scalar
        def _(s):
            for f in q['scalar']:
                f(s)
        @block.tensor
        def _(t):
            for f in q['tensor']:
                f(t)
        @block.gpsimd
        def _(g):
            for f in q['gpsimd']:
                f(g)
    stack.close()
    return nc


_CACHE = {}


def kernel(**inputs):
    in_maps = prep_inputs(inputs)
    if 'nc' not in _CACHE:
        _CACHE['nc'] = build_nc()
    res = run_bass_kernel_spmd(_CACHE['nc'], in_maps, core_ids=list(range(8)))
    full = np.zeros((B_, S, D), np.float32)
    for c in range(8):
        o = res.results[c]['out']
        b = c // 2
        if c % 2 == 0:
            full[b, 0:SPLIT, :] = o[:, 0:SPLIT].T
        else:
            full[b, SPLIT:S, :] = o[:, SPLIT - (S - FT):FT].T
    return full


# version 06
# speedup vs baseline: 1.0104x; 1.0104x over previous
import sys
sys.path.insert(0, '/opt/trn_rl_repo')

import numpy as np
import concourse.bass as bass
import concourse.mybir as mybir
from concourse.alu_op_type import AluOpType
from concourse.bass_utils import run_bass_kernel_spmd

F32 = mybir.dt.float32
BF16 = mybir.dt.bfloat16
I8 = mybir.dt.int8
AF = mybir.ActivationFunctionType

B_ = 4
S = 4096
D = 1024
H = 16
FT = 2848
SPLIT = 2816
DEPTH = 6
W1OF = 2816
W1C = 22
NEG = -1.0e30
STAGE = 'full'
NLAYERS = DEPTH


def _tiles(total, step, start=0):
    o, out = start, []
    while o < total:
        out.append((o, min(step, total - o)))
        o += step
    return out


NT_FT = _tiles(FT, 512)
NT_S = _tiles(S, 512)
MEGAS = [(0, 1024), (1024, 1024), (1824, 1024)]
NT_OV = [(0, 512), (512, 512), (1024, 512), (1536, 512), (2048, 512), (2336, 512)]


def _tile_lhsT(W):
    K, N = W.shape
    Kc, Nc = (K + 127) // 128, (N + 127) // 128
    Wp = np.zeros((Kc * 128, Nc * 128), np.float32)
    Wp[:K, :N] = W
    return np.ascontiguousarray(
        Wp.reshape(Kc, 128, Nc, 128).transpose(1, 2, 0, 3), np.float32)


def _tile_rhs(W):
    K, N = W.shape
    Kc = (K + 127) // 128
    Wp = np.zeros((Kc * 128, N), np.float32)
    Wp[:K] = W
    return np.ascontiguousarray(Wp.reshape(Kc, 128, N).transpose(1, 0, 2), np.float32)


def _rotary_tables():
    HD = 64
    half32 = np.arange(0, HD, 2, dtype=np.float32)
    inv_freq = (np.float32(1.0) / np.float32(10000.0) ** (half32 / np.float32(HD)))
    fr32 = (np.arange(S, dtype=np.float32)[:, None] * inv_freq[None, :])
    freqs = np.concatenate([fr32.astype(np.float64)] * 2, -1)
    bs32 = ((half32 + np.float32(0.4 * HD)) / np.float32(1.4 * HD))
    power = ((np.arange(S, dtype=np.float32) - np.float32(S // 2)) / np.float32(128.0))
    sc32 = (bs32[None, :].astype(np.float64) ** power[:, None].astype(np.float64)
            ).astype(np.float32)
    scale = np.concatenate([sc32.astype(np.float64)] * 2, -1)
    tau = np.where(np.arange(HD) < 32, 1.0, -1.0)[None, :]
    cs_q = (np.cos(freqs) * scale).T
    sn_q = (np.sin(freqs) * scale * tau).T
    cs_k = (np.cos(freqs) / scale).T
    sn_k = (np.sin(freqs) / scale * tau).T
    def st(a):
        full = np.concatenate([a, a], 0)
        hi = full.astype(np.float32)
        lo = (full - hi.astype(np.float64)).astype(np.float32)
        return np.ascontiguousarray(hi), np.ascontiguousarray(lo)
    return st(cs_q), st(sn_q), st(cs_k), st(sn_k)


def _alibi_toep(F0):
    base = (2.0 ** 8) ** (1.0 / H)
    m64 = np.asarray([1.0 / base ** (i + 1) for i in range(H)], np.float64)
    m32 = m64.astype(np.float32)
    d32 = np.arange(8192, dtype=np.float32) - np.float32(4095.0 + F0)
    d64 = np.arange(8192, dtype=np.float64) - (4095.0 + F0)
    keep = (d64 >= -4095.5) & (d64 <= 4095.5)
    T = np.ones((H, 8192), np.float32)
    for h in range(H):
        b32 = (m32[h] * d32).astype(np.float32).astype(np.float64)
        T[h, keep] = np.exp(b32 - m64[h] * d64).astype(np.float32)[keep]
    return np.ascontiguousarray(T)


def _alibi_g():
    base = (2.0 ** 8) ** (1.0 / H)
    m = np.asarray([1.0 / base ** (i + 1) for i in range(H)], np.float64)
    j = np.arange(S, dtype=np.float64)
    g = np.exp(m[None, :] * (j[:, None] - (S - 1))).astype(np.float32)
    return np.ascontiguousarray(g.reshape(S // 128, 128, H).transpose(1, 0, 2)
                                .reshape(128, (S // 128) * H))


def _band_masks():
    ms = np.zeros((128, 4, 256), np.float32)
    kk = np.arange(128)[:, None]
    ii = np.arange(256)[None, :]
    for kco in range(4):
        jj = 128 * kco + kk
        ms[:, kco, :] = np.where((jj >= ii) & (jj <= ii + 256), 0.0, NEG)
    return np.ascontiguousarray(ms.reshape(128, 1024))


def _perm_rot():
    P = np.zeros((128, 128), np.float32)
    for c in range(128):
        P[(c - 32 if (c % 64) >= 32 else c + 32), c] = 1.0
    return P


def _indicators():
    ind_ss = np.zeros((128, 8, 16), np.float32)
    ind_bc = np.zeros((16, 8, 128), np.float32)
    for sx in range(8):
        for p in range(128):
            r = 2 * sx + (p // 64)
            ind_ss[p, sx, r] = 1.0
            ind_bc[r, sx, p] = 1.0
    return (np.ascontiguousarray(ind_ss.reshape(128, 128)),
            np.ascontiguousarray(ind_bc.reshape(16, 1024)))


def prep_inputs(inputs):
    x = np.asarray(inputs['L0_attn_x'], np.float32)
    Wkqv = np.asarray(inputs['L0_attn_W_kqv'], np.float32)
    iWqkv = np.asarray(inputs['L0_ffn_Wqkv'], np.float32)
    qs = np.asarray(inputs['L0_ffn_q_scale'], np.float32)
    ks = np.asarray(inputs['L0_ffn_k_scale'], np.float32)
    Wo = np.asarray(inputs['L0_ffn_Wo'], np.float32)
    lng = np.asarray(inputs['L0_ffn_ln_g'], np.float32)
    lnb = np.asarray(inputs['L0_ffn_ln_b'], np.float32)
    W1 = np.asarray(inputs['L0_ffn_W1'], np.float32)
    b1 = np.asarray(inputs['L0_ffn_b1'], np.float32)
    W2 = np.asarray(inputs['L0_ffn_W2'], np.float32)
    b2 = np.asarray(inputs['L0_ffn_b2'], np.float32)

    sh = {}
    sh['o_wk'] = _tile_lhsT(Wkqv[:, 0:D])
    sh['o_wq'] = _tile_lhsT(Wkqv[:, D:2 * D])
    sh['o_wv'] = _tile_rhs(Wkqv[:, 2 * D:3 * D])
    for i in range(DEPTH):
        sh[f'wq{i}'] = _tile_lhsT(iWqkv[i][:, 0:D])
        sh[f'wk{i}'] = _tile_lhsT(iWqkv[i][:, D:2 * D])
        sh[f'wv{i}'] = _tile_rhs(iWqkv[i][:, 2 * D:3 * D])
        sh[f'wo{i}'] = _tile_lhsT(Wo[i])
        W1a = np.zeros((D, W1OF), np.float32); W1a[:, :2730] = W1[i][:, :2730]
        W1g = np.zeros((D, W1OF), np.float32); W1g[:, :2730] = W1[i][:, 2730:]
        sh[f'w1a{i}'] = _tile_lhsT(W1a)
        sh[f'w1g{i}'] = _tile_lhsT(W1g)
        W2p = np.zeros((W1OF, D), np.float32); W2p[:2730] = W2[i]
        sh[f'w2{i}'] = _tile_lhsT(W2p)

    vecs = np.zeros((128, DEPTH, 70), np.float32)
    for i in range(DEPTH):
        vecs[:, i, 0] = np.tile(qs[i], 2)
        vecs[:, i, 1] = np.tile(ks[i], 2)
        for c in range(8):
            vecs[:, i, 2 + c] = lng[i][128 * c:128 * (c + 1)]
            vecs[:, i, 10 + c] = lnb[i][128 * c:128 * (c + 1)]
            vecs[:, i, 62 + c] = b2[i][128 * c:128 * (c + 1)]
        b1a = np.zeros(W1OF, np.float32); b1a[:2730] = b1[i][:2730]
        b1g = np.zeros(W1OF, np.float32); b1g[:2730] = b1[i][2730:]
        for c in range(W1C):
            vecs[:, i, 18 + c] = b1a[128 * c:128 * (c + 1)]
            vecs[:, i, 40 + c] = b1g[128 * c:128 * (c + 1)]
    sh['lvecs'] = np.ascontiguousarray(vecs.reshape(128, DEPTH * 70))
    sh['galibi'] = _alibi_g()
    sh['masks'] = _band_masks()
    sh['perm'] = _perm_rot()
    sh['ind_ss'], sh['ind_bc'] = _indicators()
    sh['i128'] = np.eye(128, dtype=np.float32)

    (cs_q, cs_ql), (sn_q, sn_ql), (cs_k, cs_kl), (sn_k, sn_kl) = _rotary_tables()
    in_maps = []
    for c in range(8):
        b = c // 2
        F0 = 0 if c % 2 == 0 else S - FT
        d = dict(sh)
        d['toep'] = _alibi_toep(F0)
        d['xT'] = np.ascontiguousarray(x[b].T)
        xtf = np.zeros((D, 3072), np.float32)
        xtf[:, 0:FT] = x[b].T[:, F0:F0 + FT]
        d['xTf'] = xtf
        d['rcq'] = np.ascontiguousarray(cs_q[:, F0:F0 + FT])
        d['rsq'] = np.ascontiguousarray(sn_q[:, F0:F0 + FT])
        d['rck'] = np.ascontiguousarray(cs_k[:, F0:F0 + FT])
        d['rsk'] = np.ascontiguousarray(sn_k[:, F0:F0 + FT])
        d['rcql'] = np.ascontiguousarray(cs_ql[:, F0:F0 + FT])
        d['rsql'] = np.ascontiguousarray(sn_ql[:, F0:F0 + FT])
        d['rckl'] = np.ascontiguousarray(cs_kl[:, F0:F0 + FT])
        d['rskl'] = np.ascontiguousarray(sn_kl[:, F0:F0 + FT])
        in_maps.append(d)
    return in_maps



class Ctr:
    def __init__(self, h):
        self.h, self.n = h, 0


class Bank:
    def __init__(self, t):
        self.t = t
        self.free = []

    def consumed(self, ctr, val):
        self.free = [(ctr, val)]

    def also_consumed(self, ctr, val):
        self.free.append((ctr, val))


def build_nc():
    nc = bass.Bass("TRN2", target_bir_lowering=False, debug=False,
                   detect_race_conditions=False)
    from contextlib import ExitStack
    stack = ExitStack()

    ap = {}
    def din(name, shape, dt=F32):
        ap[name] = nc.dram_tensor(name, list(shape), dt, kind="ExternalInput").ap()

    din('xT', (D, S)); din('xTf', (D, 3072))
    din('o_wk', (128, 8, 8, 128)); din('o_wq', (128, 8, 8, 128)); din('o_wv', (128, 8, D))
    for i in range(DEPTH):
        din(f'wq{i}', (128, 8, 8, 128)); din(f'wk{i}', (128, 8, 8, 128))
        din(f'wv{i}', (128, 8, D)); din(f'wo{i}', (128, 8, 8, 128))
        din(f'w1a{i}', (128, W1C, 8, 128)); din(f'w1g{i}', (128, W1C, 8, 128))
        din(f'w2{i}', (128, 8, W1C, 128))
    din('lvecs', (128, DEPTH * 70)); din('toep', (H, 8192))
    din('galibi', (128, (S // 128) * H))
    din('masks', (128, 1024)); din('perm', (128, 128)); din('i128', (128, 128))
    din('ind_ss', (128, 128)); din('ind_bc', (16, 1024))
    din('rcq', (128, FT)); din('rsq', (128, FT)); din('rck', (128, FT)); din('rsk', (128, FT))
    din('rcql', (128, FT)); din('rsql', (128, FT)); din('rckl', (128, FT)); din('rskl', (128, FT))

    out = nc.dram_tensor('out', [D, FT], F32, kind="ExternalOutput").ap()
    st_t = [nc.dram_tensor('st_a', [D, 1 + FT], F32).ap(),
            nc.dram_tensor('st_b', [D, 1 + FT], F32).ap()]
    x1k = nc.dram_tensor('x1k', [D, FT], F32).ap()

    stv = [t.rearrange('(c p) t -> p c t', p=128) for t in st_t]
    x1kv = x1k.rearrange('(c p) t -> p c t', p=128)
    outv = out.rearrange('(c p) t -> p c t', p=128)
    xTv = ap['xT'].rearrange('(c p) t -> p c t', p=128)
    xTfv = ap['xTf'].rearrange('(c p) t -> p c t', p=128)

    q = {e: [] for e in ('sync', 'vector', 'scalar', 'tensor', 'gpsimd')}
    def on(e, fn):
        q[e].append(fn)

    def sem(name):
        return Ctr(stack.enter_context(nc.semaphore(name)))
    NQS = 8
    ldp = [sem(f's_ld{j}') for j in range(NQS)]
    wlp = [sem(f's_wl{j}') for j in range(NQS)]
    stp = [sem(f's_st{j}') for j in range(NQS)]
    qrot = {'sync': [ldp, 0], 'gpsimd': [wlp, 0], 'scalar': [stp, 0]}
    nmp = [sem(f's_nm{j}') for j in range(40)]
    nmmap = {}
    def named(name):
        if name not in nmmap:
            nmmap[name] = nmp[len(nmmap)]
        return nmmap[name]
    s_pe = sem('s_pe'); s_ve = sem('s_ve'); s_ac = sem('s_ac')
    ALL = ldp + wlp + stp + nmp + [s_pe, s_ve, s_ac]
    def w(eng, pairs):
        for (ctr, n) in pairs:
            if n > 0:
                eng.wait_ge(ctr.h, n)

    def dma(queue, _ctr, dst, src, waits=(), slow=False, stream=None):
        if stream is None:
            pool, idx = qrot[queue]
            qrot[queue][1] = (idx + 1) % NQS
            ctr = pool[idx]
        else:
            ctr = named(stream)
        ctr.n += 16
        def f(eng, dst=dst, src=src, waits=tuple(waits), slow=slow, h=ctr.h):
            w(eng, waits)
            eng.dma_start(out=dst, in_=src,
                          allow_slow_non_contiguous=slow).then_inc(h, 16)
        on(queue, f)
        return (ctr, ctr.n)

    def ld(dst, src, waits=(), stream=None):
        return dma('sync', None, dst, src, waits, stream=stream)

    def wl(dst, src, waits=(), stream=None):
        return dma('gpsimd', None, dst, src, waits, stream=stream)

    def sto(dst, src, waits=(), stream=None):
        return dma('scalar', None, dst, src, waits, stream=stream)

    def _eng_emit(queue, ctr, fn, waits):
        ctr.n += 1
        def f(e, fn=fn, waits=tuple(waits), h=ctr.h):
            w(e, waits)
            fn(e).then_inc(h, 1)
        on(queue, f)
        return ctr.n

    def ve(fn, waits=()):
        return _eng_emit('vector', s_ve, fn, waits)

    def ac(fn, waits=()):
        return _eng_emit('scalar', s_ac, fn, waits)

    def pe(fn, waits=()):
        return _eng_emit('tensor', s_pe, fn, waits)

    def barrier():
        vals = tuple((c, c.n) for c in ALL)
        for e in q:
            def f(eng, vals=vals):
                w(eng, vals)
            on(e, f)

    def sb(name, shape, dt):
        return stack.enter_context(nc.sbuf_tensor(name, list(shape), dt))

    c_masks = sb('c_masks', (128, 1024), BF16)
    c_perm = sb('c_perm', (128, 128), BF16)
    c_i128 = sb('c_i128', (128, 128), BF16)
    c_indss = sb('c_indss', (128, 128), BF16)
    c_indbcf = sb('c_indbcf', (16, 1024), F32)
    c_onesf = sb('c_onesf', (128, 128), F32)
    c_eps = sb('c_eps', (128, 1), F32)
    c_lv = sb('c_lv', (128, DEPTH * 70), F32)
    c_g = sb('c_g', (128, (S // 128) * H), F32)

    _ab = (int(nc.sbuf_base) + 63) // 64 * 64 + 64
    ARENA = (int(nc.sbuf_top) - _ab) // 64 * 64

    class Arena:
        def __init__(self):
            self.off = 0
            self.marks = []
            self.base = _ab
        def al(self, cols, dt):
            esz = 2 if dt == BF16 else 4
            a = (self.off + 63) // 64 * 64
            self.off = a + cols * esz
            assert self.off <= ARENA, f"arena overflow {self.off} > {ARENA}"
            return nc.alloc_sbuf_tensor_at('ar', (128, cols), dt,
                                           offset=self.base + a)
        def push(self):
            self.marks.append(self.off)
        def pop(self):
            self.off = self.marks.pop()
    A = Arena()

    def psum(name, cols):
        return stack.enter_context(nc.psum_tensor(name, [128, cols], F32))
    pA = Bank(psum('pA', 1024))
    pB = Bank(psum('pB', 1024))
    p4 = Bank(psum('p4', 512)); p5 = Bank(psum('p5', 512))
    p6 = Bank(psum('p6', 512)); p7 = Bank(psum('p7', 512))

    def lvec(i, col):
        return c_lv[:, i * 70 + col:i * 70 + col + 1]

    wl(c_masks[:, :], ap['masks'])
    wl(c_perm[:, :], ap['perm'])
    wl(c_i128[:, :], ap['i128'])
    wl(c_indss[:, :], ap['ind_ss'])
    ld(c_indbcf[:, :], ap['ind_bc'])
    ld(c_lv[:, :], ap['lvecs'])
    ld(c_g[:, :], ap['galibi'])
    ve(lambda v: v.memset(c_onesf[:, :], 1.0))
    ve(lambda v: v.memset(c_eps[:, :], 1e-5))
    zbuf = A.al(8, F32)
    vz = ve(lambda v: v.memset(zbuf[:, :], 0.0))
    for t in range(2):
        dma('scalar', None, stv[t][:, :, 0:1], zbuf[:, :, None],
            waits=[(s_ve, vz)], slow=True)
    barrier()

    if STAGE == 'consts':
        return _finish(nc, q, ALL, w, on, stack)

    NSLOT = 6
    wslots = []
    wslot_free = [[] for _ in range(NSLOT)]
    wst = {'i': 0}

    def load_w(name, of, ni=8, sub=0):
        a = ap[name]
        si = wst['i']; wst['i'] = (si + 1) % NSLOT
        src = a[:, of, sub:sub + ni, :].rearrange('p a b -> p (a b)')
        lv = wl(wslots[si][:, 0:ni * 128], src, waits=wslot_free[si],
                stream=f'ws{si}')
        wslot_free[si] = []
        def sl(k, si=si):
            return wslots[si][:, 128 * k:128 * (k + 1)]
        def rel(si=si, ctr=s_pe):
            wslot_free[si] = [(ctr, ctr.n)]
        return sl, lv, rel

    def load_w_rhs(name, ci):
        a = ap[name]
        si = wst['i']; wst['i'] = (si + 1) % NSLOT
        lv = wl(wslots[si][:, 0:D], a[:, ci, :], waits=wslot_free[si],
                stream=f'ws{si}')
        wslot_free[si] = []
        def sl(c0, c1, si=si):
            return wslots[si][:, c0:c1]
        def rel(si=si):
            wslot_free[si] = [(s_pe, s_pe.n)]
        return sl, lv, rel

    def ln_stats_chain(srows_fn, nn):
        m1 = ve(lambda v: v.tensor_scalar_mul(out=srows_fn(0), in0=p4.t[0:1, 0:nn],
                                              scalar1=1.0 / D))
        m2 = ve(lambda v: v.tensor_scalar_mul(out=srows_fn(1), in0=p5.t[0:1, 0:nn],
                                              scalar1=1.0 / D))
        p4.consumed(s_ve, m1)
        p5.consumed(s_ve, m2)
        m3 = ac(lambda s: s.square(out=srows_fn(2), in_=srows_fn(0)), waits=[(s_ve, m2)])
        m4 = ve(lambda v: v.tensor_tensor(out=srows_fn(3), in0=srows_fn(1), in1=srows_fn(2),
                                          op=AluOpType.subtract), waits=[(s_ac, m3)])
        m5 = ac(lambda s: s.activation(out=srows_fn(4), in_=srows_fn(3), func=AF.Sqrt,
                                       bias=c_eps[0:1, 0:1]), waits=[(s_ve, m4)])
        m6 = ve(lambda v: v.reciprocal(out=srows_fn(5), in_=srows_fn(4)), waits=[(s_ac, m5)])
        m7 = ve(lambda v: v.tensor_tensor(out=srows_fn(6), in0=srows_fn(0), in1=srows_fn(5),
                                          op=AluOpType.mult))
        m8 = ve(lambda v: v.tensor_scalar_mul(out=srows_fn(7), in0=srows_fn(6), scalar1=-1.0))
        b1 = pe(lambda t: t.matmul(p6.t[:, 0:nn], c_onesf[0:1, :], srows_fn(5),
                                   start=True, stop=True), waits=[(s_ve, m8)] + p6.free)
        b2 = pe(lambda t: t.matmul(p7.t[:, 0:nn], c_onesf[0:1, :], srows_fn(7),
                                   start=True, stop=True), waits=p7.free)
        return b2


    ktd = nc.dram_tensor('ktd', [8, 128, S], BF16).ap()
    vtd = nc.dram_tensor('vtd', [32, 128, 1040], BF16).ap()
    tdd = nc.dram_tensor('tdd', [H, 128, 7040], BF16).ap()

    A.push()
    wkb = A.al(8192, BF16)
    A.push()
    wvb = A.al(8192, BF16)
    xsb = [A.al(4096, BF16), A.al(4096, BF16)]
    xsb_free = [[], []]
    kstg = [A.al(512, BF16), A.al(512, BF16)]
    kstg_free = [[], []]
    vstg_free = [[], []]
    stagB_free = []
    vstg = [A.al(1040, BF16), A.al(1040, BF16)]

    for h in range(H):
        wl(tdd[h], bass.AP(tensor=ap['toep'].tensor, offset=h * 8192 + 1024,
                           ap=[[1, 128], [1, 7040]]), stream='td')

    for of in range(8):
        wk_lv = wl(wkb[:, 1024 * of:1024 * (of + 1)],
                   ap['o_wk'][:, of, :, :].rearrange('p a b -> p (a b)'),
                   stream='wk')
    for ci in range(8):
        wv_lv = wl(wvb[:, 1024 * ci:1024 * (ci + 1)], ap['o_wv'][:, ci, :],
                   stream='wv')

    for nt_i, (o, n) in enumerate(NT_S):
        sl_ = nt_i % 2
        xv = wl(xsb[sl_][:, 0:8 * n].rearrange('p (c t) -> p c t', c=8),
                xTv[:, :, o:o + n],
                waits=xsb_free[sl_] + [wk_lv], stream=f'xs{sl_}')
        for of in range(8):
            bank = pA if of % 2 == 0 else pB
            ks_ = kstg[of % 2]
            def kmm(t, sl_=sl_, n=n, of=of, bank=bank):
                last = None
                for ci in range(8):
                    last = t.matmul(bank.t[:, 0:n],
                                    wkb[:, of * 1024 + ci * 128:of * 1024 + (ci + 1) * 128],
                                    xsb[sl_][:, ci * n:ci * n + n],
                                    start=(ci == 0), stop=(ci == 7))
                return last
            pv = pe(kmm, waits=[xv] + bank.free)
            av = ac(lambda s, n=n, bank=bank, ks_=ks_: s.copy(out=ks_[:, 0:n],
                                                              in_=bank.t[:, 0:n]),
                    waits=[(s_pe, pv)] + kstg_free[of % 2])
            bank.consumed(s_ac, av)
            kstg_free[of % 2] = []
            kv = sto(ktd[of][:, o:o + n], ks_[:, 0:n], waits=[(s_ac, av)],
                     stream=f'kst{of % 2}')
            kstg_free[of % 2] = [kv]
        for tb in range(0, n, 128):
            kc = (o + tb) // 128
            vs_ = vstg[kc % 2]
            ev2 = 0
            for half in range(2):
                bank = p4 if half == 0 else p5
                def vmm(t, sl_=sl_, n=n, tb=tb, half=half, bank=bank):
                    last = None
                    for ci in range(8):
                        last = t.matmul(bank.t[:, 0:512],
                                        xsb[sl_][:, ci * n + tb:ci * n + tb + 128],
                                        wvb[:, ci * 1024 + half * 512:ci * 1024 + half * 512 + 512],
                                        start=(ci == 0), stop=(ci == 7))
                    return last
                pv = pe(vmm, waits=[xv, wv_lv] + bank.free)
                gs = c_g[:, kc * H + half * 8:kc * H + half * 8 + 8]
                e1 = ve(lambda v, half=half, bank=bank, vs_=vs_, gs=gs:
                        v.tensor_tensor(
                            out=vs_[:, half * 520:half * 520 + 520]
                                .rearrange('p (h e) -> p h e', h=8)[:, :, 0:64],
                            in0=bank.t[:, 0:512].rearrange('p (h e) -> p h e', h=8),
                            in1=gs[:, :, None].broadcast_to((128, 8, 64)),
                            op=AluOpType.mult),
                        waits=[(s_pe, pv)] + (vstg_free[kc % 2] if half == 0 else []))
                ev2 = ve(lambda v, half=half, vs_=vs_, gs=gs:
                         v.tensor_copy(
                             out=vs_[:, half * 520:half * 520 + 520]
                                 .rearrange('p (h e) -> p h e', e=65)[:, :, 64:65],
                             in_=gs[:, :, None]))
                bank.consumed(s_ve, e1)
            vstg_free[kc % 2] = []
            vv_ = sto(vtd[kc], vs_[:, 0:1040], waits=[(s_ve, ev2)], stream=f'vst{kc % 2}')
            vstg_free[kc % 2] = [vv_]
        xsb_free[sl_] = [(s_pe, s_pe.n)]

    if STAGE.startswith('o1'):
        A.pop()
        A.pop()
        wslots.extend(A.al(1024, BF16) for _ in range(NSLOT))
        barrier()
        return _finish(nc, q, ALL, w, on, stack)

    A.pop()
    A.push()
    wqb = wkb
    NO2 = 256
    NT_O2 = _tiles(3072, NO2)
    QTf = A.al(8 * 3072, BF16)
    xsq = [A.al(8 * NO2, BF16), A.al(8 * NO2, BF16)]
    xsq_free = [[], []]
    for of in range(8):
        wq_lv = wl(wqb[:, 1024 * of:1024 * (of + 1)],
                   ap['o_wq'][:, of, :, :].rearrange('p a b -> p (a b)'),
                   waits=[(s_ac, s_ac.n), (s_pe, s_pe.n)], stream='wk')
    Pq = [Bank(pA.t[:, 0:256]), Bank(pA.t[:, 512:768]),
          Bank(pB.t[:, 0:256]), Bank(pB.t[:, 512:768])]
    for nt_i, (o, n) in enumerate(NT_O2):
        sl_ = nt_i % 2
        xv = wl(xsq[sl_][:, 0:8 * n].rearrange('p (c t) -> p c t', c=8),
                xTfv[:, :, o:o + n],
                waits=xsq_free[sl_] + ([wq_lv] if nt_i == 0 else []),
                stream=f'xq{sl_}')
        for of in range(8):
            bank = Pq[(nt_i * 8 + of) % 4]
            def qmm(t, sl_=sl_, n=n, of=of, bank=bank):
                last = None
                for ci in range(8):
                    last = t.matmul(bank.t[:, 0:n],
                                    wqb[:, of * 1024 + ci * 128:of * 1024 + (ci + 1) * 128],
                                    xsq[sl_][:, ci * n:ci * n + n],
                                    start=(ci == 0), stop=(ci == 7))
                return last
            pv = pe(qmm, waits=[xv] + bank.free)
            qrdy = ac(lambda s, of=of, o=o, n=n, bank=bank:
                      s.copy(out=QTf[:, of * 3072 + o:of * 3072 + o + n],
                             in_=bank.t[:, 0:n]),
                      waits=[(s_pe, pv)])
            bank.consumed(s_ac, qrdy)
        xsq_free[sl_] = [(s_pe, s_pe.n)]
    qf_rdy = s_ac.n

    KTp = A.al(S, BF16)
    vtp = A.al(32 * 130, BF16)
    TDs = A.al(2 * 7040, BF16)
    prA = A.al(1024, BF16); prB = A.al(1024, BF16)
    aob = A.al(8 * 3072, BF16)
    stagB = A.al(3072, BF16)
    rA = A.al(NO2, F32); rB = A.al(NO2, F32)
    RA = A.al(NO2, F32); RB = A.al(NO2, F32)

    _td0 = TDs[:, :]

    def tdap(h2, kq, q0):
        return bass.AP(tensor=_td0.tensor,
                       offset=int(_td0.offset) + h2 * 7040 + 3071 - q0 + 512 * kq,
                       ap=[[int(_td0.ap[0][0]), 128], [128, 4], [-1, 256]])

    for hc in range(8):
        ktv = wl(KTp[:, :], ktd[hc], waits=[(s_pe, s_pe.n)], stream='ktp')
        vtv = wl(vtp[:, :].rearrange('p (a b) -> p a b', b=130),
                 vtd[:, :, 130 * hc:130 * hc + 130].rearrange('a p b -> p a b'),
                 waits=[(s_pe, s_pe.n)], stream='vtp')
        tdv = wl(TDs[:, 0:7040], tdd[2 * hc], waits=[(s_ve, s_ve.n)], stream='tds')
        tdv = wl(TDs[:, 7040:14080], tdd[2 * hc + 1], stream='tds')
        for nt_i, (q0, n) in enumerate(NT_O2):
            for kq in range(8):
                pvsc = 0
                for kcl in range(4):
                    kc = kq * 4 + kcl
                    def smm(t, kc=kc, kcl=kcl, q0=q0, n=n, hc=hc):
                        t.matmul(pA.t[:, kcl * 256:kcl * 256 + n],
                                 KTp[0:64, kc * 128:(kc + 1) * 128],
                                 QTf[0:64, hc * 3072 + q0:hc * 3072 + q0 + n],
                                 start=True, stop=True)
                        return t.matmul(pB.t[:, kcl * 256:kcl * 256 + n],
                                        KTp[64:128, kc * 128:(kc + 1) * 128],
                                        QTf[64:128, hc * 3072 + q0:hc * 3072 + q0 + n],
                                        start=True, stop=True, tile_position=(64, 0))
                    wt = ([ktv, (s_ac, qf_rdy)] + pA.free + pB.free) \
                        if (kq == 0 and kcl == 0) else []
                    pvsc = pe(smm, waits=wt)
                eA = ac(lambda s: s.activation(out=prA[:, 0:1024], in_=pA.t[:, 0:1024],
                                               func=AF.Exp, scale=1.0 / 32.0),
                        waits=[(s_pe, pvsc)])
                eB = ac(lambda s: s.activation(out=prB[:, 0:1024], in_=pB.t[:, 0:1024],
                                               func=AF.Exp, scale=1.0 / 32.0))
                pA.consumed(s_ac, eA)
                pB.consumed(s_ac, eB)
                tm1 = ve(lambda v, kq=kq, q0=q0: v.tensor_tensor(
                            out=prA[:, 0:1024].rearrange('p (a b) -> p a b', a=4),
                            in0=prA[:, 0:1024].rearrange('p (a b) -> p a b', a=4),
                            in1=tdap(0, kq, q0), op=AluOpType.mult),
                         waits=[(s_ac, eA)] + ([tdv] if (nt_i == 0 and kq == 0) else []))
                tm2 = ve(lambda v, kq=kq, q0=q0: v.tensor_tensor(
                            out=prB[:, 0:1024].rearrange('p (a b) -> p a b', a=4),
                            in0=prB[:, 0:1024].rearrange('p (a b) -> p a b', a=4),
                            in1=tdap(1, kq, q0), op=AluOpType.mult),
                         waits=[(s_ac, eB)])
                def avmm(t, kq=kq, n=n):
                    last = None
                    for kcl in range(4):
                        kc = kq * 4 + kcl
                        st_ = (kc == 0)
                        sp = (kc == 31)
                        t.matmul(p4.t[0:65, 0:n], vtp[:, kc * 130:kc * 130 + 65],
                                 prA[:, kcl * 256:kcl * 256 + n], start=st_, stop=sp)
                        last = t.matmul(p5.t[0:65, 0:n],
                                        vtp[:, kc * 130 + 65:kc * 130 + 130],
                                        prB[:, kcl * 256:kcl * 256 + n],
                                        start=st_, stop=sp)
                    return last
                wt = [(s_ve, tm2), vtv] + ((p4.free + p5.free) if kq == 0 else [])
                pvav = pe(avmm, waits=wt)
            r1 = ve(lambda v, n=n: v.reciprocal(out=rA[64:65, 0:n], in_=p4.t[64:65, 0:n]),
                    waits=[(s_pe, pvav)])
            r2 = ve(lambda v, n=n: v.reciprocal(out=rB[64:65, 0:n], in_=p5.t[64:65, 0:n]))
            b1 = pe(lambda t, n=n: t.matmul(p6.t[0:64, 0:n], c_onesf[64:65, 0:64],
                                            rA[64:65, 0:n], start=True, stop=True),
                    waits=[(s_ve, r1)] + p6.free)
            b2 = pe(lambda t, n=n: t.matmul(p7.t[0:64, 0:n], c_onesf[64:65, 0:64],
                                            rB[64:65, 0:n], start=True, stop=True),
                    waits=[(s_ve, r2)] + p7.free)
            a1 = ac(lambda s, n=n: s.copy(out=RA[0:64, 0:n], in_=p6.t[0:64, 0:n]),
                    waits=[(s_pe, b1)])
            a2 = ac(lambda s, n=n: s.copy(out=RB[0:64, 0:n], in_=p7.t[0:64, 0:n]),
                    waits=[(s_pe, b2)])
            p6.consumed(s_ac, a1)
            p7.consumed(s_ac, a2)
            v1 = ve(lambda v, hc=hc, q0=q0, n=n: v.tensor_tensor(
                        out=aob[0:64, hc * 3072 + q0:hc * 3072 + q0 + n],
                        in0=p4.t[0:64, 0:n], in1=RA[0:64, 0:n], op=AluOpType.mult),
                    waits=[(s_ac, a1)])
            v2 = ve(lambda v, q0=q0, n=n: v.tensor_tensor(
                        out=stagB[0:64, q0:q0 + n],
                        in0=p5.t[0:64, 0:n], in1=RB[0:64, 0:n], op=AluOpType.mult),
                    waits=[(s_ac, a2)] + (stagB_free if nt_i == 0 else []))
            p4.consumed(s_ve, v1)
            p5.consumed(s_ve, v2)
        pk = ld(aob[64:128, hc * 3072:hc * 3072 + 3072], stagB[0:64, 0:3072],
                waits=[(s_ve, s_ve.n)], stream='pk')
        stagB_free = [pk]
    ao_rdy = [pk]

    x2f = A.al(NO2, F32)
    sqf = A.al(NO2, F32)
    tf = A.al(8 * NO2, F32)
    srow = A.al(8 * NO2, F32)
    for nt_i, (o, n) in enumerate(NT_O2):
        pvs = 0
        t1l = 0
        for c in range(8):
            xv2 = ld(x2f[:, 0:n], xTfv[:, c, o:o + n],
                     waits=[(s_ve, s_ve.n)], stream='x2f')
            t1 = ve(lambda v, c=c, o=o, n=n: v.tensor_tensor(
                        out=tf[:, c * NO2:c * NO2 + n],
                        in0=aob[:, c * 3072 + o:c * 3072 + o + n],
                        in1=x2f[:, 0:n], op=AluOpType.add),
                    waits=[xv2] + (ao_rdy if (nt_i == 0 and c == 0) else [])
                          + ([(s_pe, s_pe.n)] if c == 0 else []))
            sq = ac(lambda s, c=c, n=n: s.square(out=sqf[:, 0:n],
                                                 in_=tf[:, c * NO2:c * NO2 + n]),
                    waits=[(s_ve, t1), (s_pe, s_pe.n)])
            def summ(t, c=c, n=n):
                t.matmul(p4.t[0:1, 0:n], c_onesf[:, 0:1], tf[:, c * NO2:c * NO2 + n],
                         start=(c == 0), stop=(c == 7))
                return t.matmul(p5.t[0:1, 0:n], c_onesf[:, 0:1], sqf[:, 0:n],
                                start=(c == 0), stop=(c == 7))
            wt = [(s_ac, sq)] + ((p4.free + p5.free) if c == 0 else [])
            pvs = pe(summ, waits=wt)
        on('vector', lambda v, pvs=pvs: v.wait_ge(s_pe.h, pvs))
        bc = ln_stats_chain(lambda j, n=n: srow[0:1, j * NO2:j * NO2 + n], n)
        lastv = 0
        for c in range(8):
            ve(lambda v, c=c, n=n: v.tensor_tensor(
                    out=tf[:, c * NO2:c * NO2 + n], in0=tf[:, c * NO2:c * NO2 + n],
                    in1=p6.t[:, 0:n], op=AluOpType.mult),
               waits=[(s_pe, bc)] if c == 0 else [])
            lastv = ve(lambda v, c=c, n=n: v.tensor_tensor(
                    out=tf[:, c * NO2:c * NO2 + n], in0=tf[:, c * NO2:c * NO2 + n],
                    in1=p7.t[:, 0:n], op=AluOpType.add))
        p6.consumed(s_ve, lastv)
        p7.consumed(s_ve, lastv)
        ns = min(n, FT - o)
        if ns > 0:
            sto(stv[0][:, :, 1 + o:1 + o + ns],
                tf[:, 0:8 * NO2].rearrange('p (c t) -> p c t', c=8)[:, :, 0:ns],
                waits=[(s_ve, lastv)])
            sto(x1kv[:, :, o:o + ns],
                tf[:, 0:8 * NO2].rearrange('p (c t) -> p c t', c=8)[:, :, 0:ns])
            if STAGE == 'outer':
                sto(outv[:, :, o:o + ns],
                    tf[:, 0:8 * NO2].rearrange('p (c t) -> p c t', c=8)[:, :, 0:ns])
        barrier()
    A.pop()
    A.pop()
    wslots.extend(A.al(1024, BF16) for _ in range(NSLOT))

    SLmax = 1280

    def emit_layer(i):
        st_in, st_mid = stv[0], stv[1]
        A.push()
        xsb_l = A.al(8 * SLmax, BF16)
        QT = A.al(8 * 1024, BF16)
        KTl = A.al(8 * SLmax, BF16)
        VTl = A.al(10 * 1040, BF16)
        rn = A.al(2304, F32)
        wvbl = A.al(8 * D, BF16)
        for ci in range(8):
            wv_rdy = wl(wvbl[:, ci * D:(ci + 1) * D], ap[f'wv{i}'][:, ci, :],
                        waits=[(s_pe, s_pe.n)] if ci == 0 else [], stream='wvb')

        def emit_mega(m0, mn):
            s0 = max(0, m0 - 256)
            s1 = m0 + mn
            SLn = s1 - s0
            qoff = m0 - s0
            nkc = (SLn + 127) // 128
            A.push()
            rtab = A.al(4 * SLmax, F32)
            rlo = A.al(SLmax, F32)
            At = A.al(SLmax, F32)
            Bt = A.al(SLmax, F32)
            raw = A.al(8 * SLmax, F32)
            sqb = [A.al(512, BF16), A.al(512, BF16)]
            ts1 = [A.al(512, F32), A.al(512, F32)]
            vbs = [A.al(512, BF16), A.al(512, BF16)]

            wl(xsb_l[:, 0:4 * SLn].rearrange('p (c t) -> p c t', c=4),
               st_in[:, 0:4, 1 + s0:1 + s1],
               waits=[(s_pe, s_pe.n), (s_ac, s_ac.n)], stream='xspan')
            xv1 = wl(xsb_l[:, 4 * SLn:8 * SLn].rearrange('p (c t) -> p c t', c=4),
                     st_in[:, 4:8, s0:s1], stream='xspan')
            rtv = 0
            for j, nm in enumerate(('rcq', 'rsq', 'rck', 'rsk')):
                rtv = ld(rtab[:, j * SLmax:j * SLmax + SLn], ap[nm][:, s0:s1],
                         waits=[(s_ve, s_ve.n)] if j == 0 else [], stream='rtab')
            LONM = ('rcql', 'rsql', 'rckl', 'rskl')

            def rslice(j, a, b):
                return rtab[:, j * SLmax + a:j * SLmax + b]

            def side(sb_, wname, dest, dcols, r0, rl, tabj):
                tv1 = ve(lambda v: v.tensor_scalar_mul(out=At[:, 0:rl], in0=rslice(tabj[0], r0, r0 + rl),
                                                       scalar1=lvec(i, sb_ // 8)),
                         waits=[rtv])
                lv1 = ld(rlo[:, 0:rl], ap[LONM[tabj[0]]][:, s0 + r0:s0 + r0 + rl],
                         waits=[(s_ve, s_ve.n)], stream='rlo')
                tvx = ve(lambda v: v.scalar_tensor_tensor(
                       out=At[:, 0:rl], in0=rlo[:, 0:rl],
                       scalar=lvec(i, sb_ // 8), in1=At[:, 0:rl],
                       op0=AluOpType.mult, op1=AluOpType.add), waits=[lv1])
                tv2 = ve(lambda v: v.tensor_scalar_mul(out=Bt[:, 0:rl], in0=rslice(tabj[1], r0, r0 + rl),
                                                       scalar1=lvec(i, sb_ // 8)))
                lv2 = ld(rlo[:, 0:rl], ap[LONM[tabj[1]]][:, s0 + r0:s0 + r0 + rl],
                         waits=[(s_ve, s_ve.n)], stream='rlo')
                tv2 = ve(lambda v: v.scalar_tensor_tensor(
                       out=Bt[:, 0:rl], in0=rlo[:, 0:rl],
                       scalar=lvec(i, sb_ // 8), in1=Bt[:, 0:rl],
                       op0=AluOpType.mult, op1=AluOpType.add), waits=[lv2])
                ntl = _tiles(rl, 512)
                Pb = [Bank(pA.t[:, 0:512]), Bank(pA.t[:, 512:1024]),
                      Bank(pB.t[:, 0:512]), Bank(pB.t[:, 512:1024])]
                for bk in Pb:
                    bk.free = pA.free + pB.free
                SSb = [p4, p5, p6][:len(ntl)]
                wpre = []
                for of in range(2):
                    wpre.append(load_w(wname, of))
                pvlast = 0
                for of in range(8):
                    wsl, wlv, wrel = wpre[of]
                    if of + 2 < 8:
                        wpre.append(load_w(wname, of + 2))
                    for tix, (to, tn) in enumerate(ntl):
                        bk = Pb[(of * len(ntl) + tix) % 4]
                        def pmm(t, wsl=wsl, to=to, tn=tn, bk=bk, r0=r0):
                            last = None
                            for ci in range(8):
                                last = t.matmul(bk.t[:, 0:tn], wsl(ci),
                                                xsb_l[:, ci * SLn + r0 + to:ci * SLn + r0 + to + tn],
                                                start=(ci == 0), stop=(ci == 7))
                            return last
                        pv = pe(pmm, waits=[wlv, xv1] + bk.free)
                        a1 = ac(lambda s, of=of, to=to, tn=tn, bk=bk:
                                s.copy(out=raw[:, of * SLn + to:of * SLn + to + tn],
                                       in_=bk.t[:, 0:tn]), waits=[(s_pe, pv)])
                        sl2 = (of * len(ntl) + tix) % 2
                        a2 = ac(lambda s, of=of, to=to, tn=tn, sl2=sl2:
                                s.square(out=sqb[sl2][:, 0:tn],
                                         in_=raw[:, of * SLn + to:of * SLn + to + tn]),
                                waits=[(s_pe, s_pe.n)])
                        bk.consumed(s_ac, a1)
                        def ssm(t, of=of, tn=tn, sl2=sl2, sb_=sb_, tix=tix):
                            return t.matmul(SSb[tix].t[0:16, 0:tn],
                                            c_indss[:, of * 16:of * 16 + 16],
                                            sqb[sl2][:, 0:tn],
                                            start=(of == 0), stop=(of == 7))
                        wt = [(s_ac, a2)] + (SSb[tix].free if of == 0 else [])
                        pvlast = pe(ssm, waits=wt)
                    wrel()
                rnv = 0
                for tix, (to, tn) in enumerate(ntl):
                    roff = (sb_ // 8) * 1024
                    c1 = ac(lambda s, to=to, tn=tn, tix=tix, roff=roff:
                            s.activation(out=rn[0:16, roff + to:roff + to + tn],
                                         in_=SSb[tix].t[0:16, 0:tn],
                                         func=AF.Sqrt), waits=[(s_pe, pvlast)])
                    SSb[tix].consumed(s_ac, c1)
                    c2 = ve(lambda v, to=to, tn=tn, roff=roff:
                            v.tensor_scalar_max(out=rn[0:16, roff + to:roff + to + tn],
                                                in0=rn[0:16, roff + to:roff + to + tn],
                                                scalar1=1e-12),
                            waits=[(s_ac, c1)])
                    rnv = ve(lambda v, to=to, tn=tn, roff=roff:
                             v.reciprocal(out=rn[0:16, roff + to:roff + to + tn],
                                          in_=rn[0:16, roff + to:roff + to + tn]))
                Rb2 = [Bank(pA.t[:, 0:512]), Bank(pA.t[:, 512:1024])]
                Pp = [Bank(pB.t[:, 0:512]), Bank(pB.t[:, 512:1024])]
                for bk in Rb2 + Pp:
                    bk.free = [(s_ac, s_ac.n), (s_ve, s_ve.n)]
                for of in range(8):
                    for tix, (to, tn) in enumerate(ntl):
                        k2 = (of * len(ntl) + tix) % 2
                        v1 = ve(lambda v, of=of, to=to, tn=tn, k2=k2:
                                v.tensor_tensor(out=ts1[k2][:, 0:tn],
                                                in0=raw[:, of * SLn + to:of * SLn + to + tn],
                                                in1=At[:, to:to + tn], op=AluOpType.mult),
                                waits=[(s_ve, rnv)] if (of == 0 and tix == 0) else [])
                        v2 = ve(lambda v, of=of, to=to, tn=tn, k2=k2:
                                v.tensor_tensor(out=vbs[k2][:, 0:tn],
                                                in0=raw[:, of * SLn + to:of * SLn + to + tn],
                                                in1=Bt[:, to:to + tn], op=AluOpType.mult),
                                waits=[(s_pe, s_pe.n)])
                        pp = Pp[k2]
                        pv1 = pe(lambda t, tn=tn, k2=k2, pp=pp:
                                 t.matmul(pp.t[:, 0:tn], c_perm[:, :], vbs[k2][:, 0:tn],
                                          start=True, stop=True),
                                 waits=[(s_ve, v2)] + pp.free)
                        v3 = ve(lambda v, tn=tn, k2=k2, pp=pp:
                                v.tensor_tensor(out=ts1[k2][:, 0:tn], in0=ts1[k2][:, 0:tn],
                                                in1=pp.t[:, 0:tn], op=AluOpType.add),
                                waits=[(s_pe, pv1)])
                        pp.consumed(s_ve, v3)
                        rb = Rb2[k2]
                        roff2 = (sb_ // 8) * 1024
                        pv2 = pe(lambda t, of=of, to=to, tn=tn, rb=rb, roff2=roff2:
                                 t.matmul(rb.t[0:128, 0:tn],
                                          c_indbcf[0:16, of * 128:of * 128 + 128],
                                          rn[0:16, roff2 + to:roff2 + to + tn],
                                          start=True, stop=True),
                                 waits=[(s_ve, rnv)] + rb.free)
                        v4 = ve(lambda v, of=of, to=to, tn=tn, k2=k2, rb=rb:
                                v.tensor_tensor(out=dest(of, to, tn), in0=ts1[k2][:, 0:tn],
                                                in1=rb.t[:, 0:tn], op=AluOpType.mult),
                                waits=[(s_pe, pv2)])
                        rb.consumed(s_ve, v4)
                pA.free = [(s_ve, s_ve.n)]
                pB.free = [(s_ve, s_ve.n)]

            side(0, f'wq{i}',
                 lambda of, to, tn: QT[:, of * mn + to:of * mn + to + tn],
                 mn, qoff, mn, (0, 1))
            side(8, f'wk{i}',
                 lambda of, to, tn: KTl[:, of * SLn + to:of * SLn + to + tn],
                 SLn, 0, SLn, (2, 3))

            for tb in range(0, SLn, 128):
                kcr = tb // 128
                kw = min(128, SLn - tb)
                for half in range(2):
                    bk = p4 if half == 0 else p5
                    def vmm(t, tb=tb, kw=kw, half=half, bk=bk):
                        last = None
                        for ci in range(8):
                            last = t.matmul(bk.t[0:kw, 0:512],
                                            xsb_l[:, ci * SLn + tb:ci * SLn + tb + kw],
                                            wvbl[:, ci * D + half * 512:ci * D + half * 512 + 512],
                                            start=(ci == 0), stop=(ci == 7))
                        return last
                    pv = pe(vmm, waits=[wv_rdy, xv1] + bk.free)
                    e1 = ve(lambda v, kcr=kcr, kw=kw, half=half, bk=bk:
                            v.tensor_copy(
                                out=VTl[0:kw, kcr * 1040 + half * 520:kcr * 1040 + half * 520 + 520]
                                    .rearrange('p (h e) -> p h e', h=8)[:, :, 0:64],
                                in_=bk.t[0:kw, 0:512].rearrange('p (h e) -> p h e', h=8)),
                            waits=[(s_pe, pv)])
                    bk.consumed(s_ve, e1)
                ve(lambda v, kcr=kcr, kw=kw:
                   v.memset(VTl[0:kw, kcr * 1040:kcr * 1040 + 1040]
                            .rearrange('p (h e) -> p h e', e=65)[:, :, 64:65], 1.0))
            A.pop()

            A.push()
            prA = A.al(1024, BF16); prB = A.al(1024, BF16)
            rA = A.al(256, F32); rB = A.al(256, F32)
            RA = A.al(256, F32); RB = A.al(256, F32)
            aob = A.al(8 * 1024, BF16)
            stagBb = A.al(8 * 1024, BF16)
            res2 = [A.al(512, F32), A.al(512, F32)]
            yout = [A.al(512, F32), A.al(512, F32)]
            last_y = [[], []]
            qts = _tiles(mn, 256)
            for hc in range(8):
                for (qo, qn) in qts:
                    q0 = m0 + qo
                    kcos = [k for k in range(4) if q0 - 256 + 128 * k >= 0
                            and q0 - 256 + 128 * k < s1]
                    pvsc = 0
                    for kco in kcos:
                        kabs = q0 - 256 + 128 * kco
                        kcr = (kabs - s0) // 128
                        kw = min(128, SLn - kcr * 128)
                        def smm(t, hc=hc, kco=kco, kcr=kcr, kw=kw, qo=qo, qn=qn):
                            t.matmul(pA.t[0:kw, kco * 256:kco * 256 + qn],
                                     KTl[0:64, hc * SLn + kcr * 128:hc * SLn + kcr * 128 + kw],
                                     QT[0:64, hc * mn + qo:hc * mn + qo + qn],
                                     start=True, stop=False)
                            t.matmul(pA.t[0:kw, kco * 256:kco * 256 + qn],
                                     c_i128[0:kw, 0:kw],
                                     c_masks[0:kw, kco * 256:kco * 256 + qn],
                                     start=False, stop=True)
                            t.matmul(pB.t[0:kw, kco * 256:kco * 256 + qn],
                                     KTl[64:128, hc * SLn + kcr * 128:hc * SLn + kcr * 128 + kw],
                                     QT[64:128, hc * mn + qo:hc * mn + qo + qn],
                                     start=True, stop=False, tile_position=(64, 0))
                            return t.matmul(pB.t[0:kw, kco * 256:kco * 256 + qn],
                                            c_i128[0:kw, 0:kw],
                                            c_masks[0:kw, kco * 256:kco * 256 + qn],
                                            start=False, stop=True)
                        wt = (pA.free + pB.free) if kco == kcos[0] else []
                        pvsc = pe(smm, waits=wt)
                    eA = ac(lambda s: s.activation(out=prA[:, 0:1024], in_=pA.t[:, 0:1024],
                                                   func=AF.Exp, scale=8.0),
                            waits=[(s_pe, pvsc)])
                    eB = ac(lambda s: s.activation(out=prB[:, 0:1024], in_=pB.t[:, 0:1024],
                                                   func=AF.Exp, scale=8.0))
                    pA.consumed(s_ac, eA)
                    pB.consumed(s_ac, eB)
                    def avmm(t, kcos=tuple(kcos), q0=q0, qn=qn, s0=s0, hc=hc):
                        last = None
                        for ix, kco in enumerate(kcos):
                            kabs = q0 - 256 + 128 * kco
                            kcr = (kabs - s0) // 128
                            kw = min(128, SLn - kcr * 128)
                            st_ = (ix == 0)
                            sp = (ix == len(kcos) - 1)
                            t.matmul(p4.t[0:65, 0:qn],
                                     VTl[0:kw, kcr * 1040 + 65 * 2 * hc:kcr * 1040 + 65 * 2 * hc + 65],
                                     prA[0:kw, kco * 256:kco * 256 + qn], start=st_, stop=sp)
                            last = t.matmul(p5.t[0:65, 0:qn],
                                            VTl[0:kw, kcr * 1040 + 65 * (2 * hc + 1):kcr * 1040 + 65 * (2 * hc + 1) + 65],
                                            prB[0:kw, kco * 256:kco * 256 + qn], start=st_, stop=sp)
                        return last
                    pvav = pe(avmm, waits=[(s_ac, eB)] + p4.free + p5.free)
                    r1 = ve(lambda v, qn=qn: v.reciprocal(out=rA[64:65, 0:qn],
                                                          in_=p4.t[64:65, 0:qn]),
                            waits=[(s_pe, pvav)])
                    r2 = ve(lambda v, qn=qn: v.reciprocal(out=rB[64:65, 0:qn],
                                                          in_=p5.t[64:65, 0:qn]))
                    b1 = pe(lambda t, qn=qn: t.matmul(p6.t[0:64, 0:qn], c_onesf[64:65, 0:64],
                                                      rA[64:65, 0:qn], start=True, stop=True),
                            waits=[(s_ve, r1)] + p6.free)
                    b2 = pe(lambda t, qn=qn: t.matmul(p7.t[0:64, 0:qn], c_onesf[64:65, 0:64],
                                                      rB[64:65, 0:qn], start=True, stop=True),
                            waits=[(s_ve, r2)] + p7.free)
                    a1 = ac(lambda s, qn=qn: s.copy(out=RA[0:64, 0:qn], in_=p6.t[0:64, 0:qn]),
                            waits=[(s_pe, b1)])
                    a2 = ac(lambda s, qn=qn: s.copy(out=RB[0:64, 0:qn], in_=p7.t[0:64, 0:qn]),
                            waits=[(s_pe, b2)])
                    p6.consumed(s_ac, a1)
                    p7.consumed(s_ac, a2)
                    v1 = ve(lambda v, hc=hc, qo=qo, qn=qn: v.tensor_tensor(
                                out=aob[0:64, hc * mn + qo:hc * mn + qo + qn],
                                in0=p4.t[0:64, 0:qn], in1=RA[0:64, 0:qn], op=AluOpType.mult),
                            waits=[(s_ac, a1)])
                    v2 = ve(lambda v, hc=hc, qo=qo, qn=qn: v.tensor_tensor(
                                out=stagBb[0:64, hc * mn + qo:hc * mn + qo + qn],
                                in0=p5.t[0:64, 0:qn], in1=RB[0:64, 0:qn], op=AluOpType.mult),
                            waits=[(s_ac, a2)])
                    p4.consumed(s_ve, v1)
                    p5.consumed(s_ve, v2)
            pk = ld(aob[64:128, 0:8 * mn], stagBb[0:64, 0:8 * mn],
                    waits=[(s_ve, s_ve.n)], stream='pk')
            wpre = [load_w(f'wo{i}', 0), load_w(f'wo{i}', 1)]
            ntm = _tiles(mn, 512)
            for of in range(8):
                wsl, wlv, wrel = wpre[of]
                if of + 2 < 8:
                    wpre.append(load_w(f'wo{i}', of + 2))
                for tix, (to, tn) in enumerate(ntm):
                    bk = p6 if (of * len(ntm) + tix) % 2 == 0 else p7
                    slr = (of * len(ntm) + tix) % 2
                    xr = ld(res2[slr][:, 0:tn], st_in[:, of, 1 + m0 + to:1 + m0 + to + tn],
                            waits=[(s_ve, s_ve.n)], stream=f'res{slr}')
                    def womm(t, wsl=wsl, to=to, tn=tn, bk=bk):
                        last = None
                        for ci in range(8):
                            last = t.matmul(bk.t[:, 0:tn], wsl(ci),
                                            aob[:, ci * mn + to:ci * mn + to + tn],
                                            start=(ci == 0), stop=(ci == 7))
                        return last
                    pv = pe(womm, waits=[wlv, pk] + bk.free)
                    vv = ve(lambda v, tn=tn, bk=bk, slr=slr: v.tensor_tensor(
                                out=yout[slr][:, 0:tn], in0=bk.t[:, 0:tn],
                                in1=res2[slr][:, 0:tn], op=AluOpType.add),
                            waits=[(s_pe, pv), xr] + last_y[slr])
                    bk.consumed(s_ve, vv)
                    last_y[slr] = [sto(st_mid[:, of, 1 + m0 + to:1 + m0 + to + tn],
                                       yout[slr][:, 0:tn],
                                       waits=[(s_ve, vv)], stream=f'sy{slr}')]
                wrel()
            A.pop()
            barrier()

        for (m0_, mn_) in MEGAS:
            emit_mega(m0_, mn_)
        A.pop()

        A.push()
        xsf = A.al(8 * 512, F32)
        sqf = [A.al(512, F32), A.al(512, F32)]
        t1s = [A.al(512, F32), A.al(512, F32)]
        xnb = A.al(8 * 512, BF16)
        gels = [A.al(512, F32), A.al(512, F32)]
        prodb = A.al(W1C * 512, BF16)
        resf = [A.al(512, F32), A.al(512, F32)]
        zf = [A.al(512, F32), A.al(512, F32)]
        last_z = [[], []]
        srow2 = A.al(8 * 512, F32)


        for (o, n) in NT_OV:
            ld(xsf[:, 0:4 * n].rearrange('p (c t) -> p c t', c=4),
               st_mid[:, 0:4, 1 + o:1 + o + n],
               waits=[(s_ve, s_ve.n), (s_ac, s_ac.n)], stream='ffnx')
            xv1 = ld(xsf[:, 4 * n:8 * n].rearrange('p (c t) -> p c t', c=4),
                     st_mid[:, 4:8, o:o + n], stream='ffnx')
            pvs = 0
            for c in range(8):
                slc = c % 2
                sq = ac(lambda s, c=c, n=n, slc=slc: s.square(out=sqf[slc][:, 0:n],
                                                              in_=xsf[:, c * n:c * n + n]),
                        waits=([xv1] if c == 0 else []) + [(s_pe, s_pe.n)])
                def summ(t, c=c, n=n, slc=slc):
                    t.matmul(p4.t[0:1, 0:n], c_onesf[:, 0:1], xsf[:, c * n:c * n + n],
                             start=(c == 0), stop=(c == 7))
                    return t.matmul(p5.t[0:1, 0:n], c_onesf[:, 0:1], sqf[slc][:, 0:n],
                                    start=(c == 0), stop=(c == 7))
                wt = [(s_ac, sq)] + ((p4.free + p5.free) if c == 0 else [])
                pvs = pe(summ, waits=wt)
            on('vector', lambda v, pvs=pvs: v.wait_ge(s_pe.h, pvs))
            bc = ln_stats_chain(lambda j, n=n: srow2[0:1, j * 512:j * 512 + n], n)
            lnv = 0
            for c in range(8):
                slc = c % 2
                va = ve(lambda v, c=c, n=n, slc=slc: v.tensor_tensor(
                            out=t1s[slc][:, 0:n], in0=xsf[:, c * n:c * n + n],
                            in1=p6.t[:, 0:n], op=AluOpType.mult),
                        waits=[(s_pe, bc)] if c == 0 else [])
                vb = ve(lambda v, c=c, n=n, slc=slc: v.tensor_tensor(
                            out=t1s[slc][:, 0:n], in0=t1s[slc][:, 0:n],
                            in1=p7.t[:, 0:n], op=AluOpType.add))
                lnv = ve(lambda v, c=c, n=n, slc=slc: v.tensor_scalar(
                            out=xnb[:, c * n:c * n + n], in0=t1s[slc][:, 0:n],
                            scalar1=lvec(i, 2 + c), scalar2=lvec(i, 10 + c),
                            op0=AluOpType.mult, op1=AluOpType.add))
            p6.consumed(s_ve, lnv)
            p7.consumed(s_ve, lnv)
            wpre = [load_w(f'w1g{i}', 0), load_w(f'w1a{i}', 0),
                    load_w(f'w1g{i}', 1), load_w(f'w1a{i}', 1)]
            for af in range(W1C):
                wg, wglv, wgrel = wpre[2 * af]
                wa, walv, warel = wpre[2 * af + 1]
                if af + 2 < W1C:
                    wpre.append(load_w(f'w1g{i}', af + 2))
                    wpre.append(load_w(f'w1a{i}', af + 2))
                bkg = Bank(pA.t[:, 0:512]) if af % 2 == 0 else Bank(pB.t[:, 0:512])
                bka = Bank(pA.t[:, 512:1024]) if af % 2 == 0 else Bank(pB.t[:, 512:1024])
                src = pA if af % 2 == 0 else pB
                bkg.free = src.free
                bka.free = src.free
                def w1mm(t, wg=wg, wa=wa, n=n, bkg=bkg, bka=bka):
                    for ci in range(8):
                        t.matmul(bkg.t[:, 0:n], wg(ci), xnb[:, ci * n:ci * n + n],
                                 start=(ci == 0), stop=(ci == 7))
                    last = None
                    for ci in range(8):
                        last = t.matmul(bka.t[:, 0:n], wa(ci), xnb[:, ci * n:ci * n + n],
                                        start=(ci == 0), stop=(ci == 7))
                    return last
                pv = pe(w1mm, waits=[wglv, walv, (s_ve, lnv)] + bkg.free)
                slg = af % 2
                ga = ac(lambda s, af=af, n=n, slg=slg, bkg=bkg: s.activation(
                            out=gels[slg][:, 0:n], in_=bkg.t[:, 0:n], func=AF.Gelu,
                            bias=lvec(i, 40 + af)),
                        waits=[(s_pe, pv), (s_ve, s_ve.n)])
                pr = ve(lambda v, af=af, n=n, slg=slg, bka=bka: v.scalar_tensor_tensor(
                            out=prodb[:, af * n:af * n + n], in0=bka.t[:, 0:n],
                            scalar=lvec(i, 18 + af), in1=gels[slg][:, 0:n],
                            op0=AluOpType.add, op1=AluOpType.mult),
                        waits=[(s_ac, ga)])
                src.consumed(s_ve, pr)
                src.also_consumed(s_ac, ga)
                wgrel(); warel()
            w2pre = [load_w(f'w2{i}', 0, ni=8, sub=0), load_w(f'w2{i}', 0, ni=8, sub=8),
                     load_w(f'w2{i}', 0, ni=6, sub=16)]
            for of in range(8):
                if of + 1 < 8:
                    w2pre.append(load_w(f'w2{i}', of + 1, ni=8, sub=0))
                    w2pre.append(load_w(f'w2{i}', of + 1, ni=8, sub=8))
                    w2pre.append(load_w(f'w2{i}', of + 1, ni=6, sub=16))
                g0, g1, g2 = w2pre[3 * of], w2pre[3 * of + 1], w2pre[3 * of + 2]
                bk = p6 if of % 2 == 0 else p7
                slr = of % 2
                xr = ld(resf[slr][:, 0:n], st_mid[:, of, 1 + o:1 + o + n],
                        waits=[(s_ve, s_ve.n)], stream=f'fr{slr}')
                def w2mm(t, g0=g0, g1=g1, g2=g2, n=n, bk=bk):
                    last = None
                    for af in range(W1C):
                        grp, k = (g0, af) if af < 8 else ((g1, af - 8) if af < 16 else (g2, af - 16))
                        last = t.matmul(bk.t[:, 0:n], grp[0](k),
                                        prodb[:, af * n:af * n + n],
                                        start=(af == 0), stop=(af == W1C - 1))
                    return last
                pv = pe(w2mm, waits=[g0[1], g1[1], g2[1],
                                     (s_ve, s_ve.n)] + bk.free)
                vv = ve(lambda v, of=of, n=n, bk=bk, slr=slr: v.scalar_tensor_tensor(
                            out=zf[slr][:, 0:n], in0=bk.t[:, 0:n], scalar=lvec(i, 62 + of),
                            in1=resf[slr][:, 0:n], op0=AluOpType.add, op1=AluOpType.add),
                        waits=[(s_pe, pv), xr] + last_z[slr])
                bk.consumed(s_ve, vv)
                last_z[slr] = [sto(st_in[:, of, 1 + o:1 + o + n], zf[slr][:, 0:n],
                                   waits=[(s_ve, vv)], stream=f'sz{slr}')]
                g0[2](); g1[2](); g2[2]()
            barrier()
        A.pop()

    def emit_final():
        A.push()
        tb = A.al(8 * 512, F32)
        x1b = A.al(8 * 512, F32)
        sqf = [A.al(512, F32), A.al(512, F32)]
        srow3 = A.al(8 * 512, F32)
        outb = A.al(8 * 512, F32)


        for (o, n) in NT_OV:
            zv = ld(tb[:, 0:8 * n].rearrange('p (c t) -> p c t', c=8),
                    stv[0][:, :, 1 + o:1 + o + n], waits=[(s_ve, s_ve.n)], stream='fz')
            xv = ld(x1b[:, 0:8 * n].rearrange('p (c t) -> p c t', c=8),
                    x1kv[:, :, o:o + n], stream='fx')
            pvs = 0
            for c in range(8):
                slc = c % 2
                va = ve(lambda v, c=c, n=n: v.tensor_tensor(
                            out=tb[:, c * n:c * n + n], in0=tb[:, c * n:c * n + n],
                            in1=x1b[:, c * n:c * n + n], op=AluOpType.add),
                        waits=[zv, xv] if c == 0 else [])
                sq = ac(lambda s, c=c, n=n, slc=slc: s.square(out=sqf[slc][:, 0:n],
                                                              in_=tb[:, c * n:c * n + n]),
                        waits=[(s_ve, va), (s_pe, s_pe.n)])
                def summ(t, c=c, n=n, slc=slc):
                    t.matmul(p4.t[0:1, 0:n], c_onesf[:, 0:1], tb[:, c * n:c * n + n],
                             start=(c == 0), stop=(c == 7))
                    return t.matmul(p5.t[0:1, 0:n], c_onesf[:, 0:1], sqf[slc][:, 0:n],
                                    start=(c == 0), stop=(c == 7))
                wt = [(s_ac, sq)] + ((p4.free + p5.free) if c == 0 else [])
                pvs = pe(summ, waits=wt)
            on('vector', lambda v, pvs=pvs: v.wait_ge(s_pe.h, pvs))
            bc = ln_stats_chain(lambda j, n=n: srow3[0:1, j * 512:j * 512 + n], n)
            lastv = 0
            for c in range(8):
                ve(lambda v, c=c, n=n: v.tensor_tensor(
                        out=outb[:, c * n:c * n + n], in0=tb[:, c * n:c * n + n],
                        in1=p6.t[:, 0:n], op=AluOpType.mult),
                   waits=[(s_pe, bc)] if c == 0 else [])
                lastv = ve(lambda v, c=c, n=n: v.tensor_tensor(
                        out=outb[:, c * n:c * n + n], in0=outb[:, c * n:c * n + n],
                        in1=p7.t[:, 0:n], op=AluOpType.add))
            p6.consumed(s_ve, lastv)
            p7.consumed(s_ve, lastv)
            sto(outv[:, :, o:o + n],
                outb[:, 0:8 * n].rearrange('p (c t) -> p c t', c=8),
                waits=[(s_ve, lastv)])
            barrier()
        A.pop()

    if STAGE == 'full':
        for i in range(NLAYERS):
            emit_layer(i)
        emit_final()

    return _finish(nc, q, ALL, w, on, stack)


def _finish(nc, q, ALL, w, on, stack):
    final = tuple((c, c.n) for c in ALL)
    for e in q:
        def f(eng, final=final):
            w(eng, final)
        on(e, f)
    with nc.Block() as block:
        @block.sync
        def _(sy):
            for f in q['sync']:
                f(sy)
        @block.vector
        def _(v):
            for f in q['vector']:
                f(v)
        @block.scalar
        def _(s):
            for f in q['scalar']:
                f(s)
        @block.tensor
        def _(t):
            for f in q['tensor']:
                f(t)
        @block.gpsimd
        def _(g):
            for f in q['gpsimd']:
                f(g)
    stack.close()
    return nc


_CACHE = {}


def kernel(**inputs):
    in_maps = prep_inputs(inputs)
    if 'nc' not in _CACHE:
        _CACHE['nc'] = build_nc()
    res = run_bass_kernel_spmd(_CACHE['nc'], in_maps, core_ids=list(range(8)))
    full = np.zeros((B_, S, D), np.float32)
    for c in range(8):
        o = res.results[c]['out']
        b = c // 2
        if c % 2 == 0:
            full[b, 0:SPLIT, :] = o[:, 0:SPLIT].T
        else:
            full[b, SPLIT:S, :] = o[:, SPLIT - (S - FT):FT].T
    return full


# version 07
# speedup vs baseline: 1.0341x; 1.0235x over previous
import sys
sys.path.insert(0, '/opt/trn_rl_repo')

import numpy as np
import concourse.bass as bass
import concourse.mybir as mybir
from concourse.alu_op_type import AluOpType
from concourse.bass_utils import run_bass_kernel_spmd

F32 = mybir.dt.float32
BF16 = mybir.dt.bfloat16
I8 = mybir.dt.int8
AF = mybir.ActivationFunctionType

B_ = 4
S = 4096
D = 1024
H = 16
FT = 2848
SPLIT = 2816
DEPTH = 6
W1OF = 2816
W1C = 22
NEG = -1.0e30
STAGE = 'full'
NLAYERS = DEPTH


def _tiles(total, step, start=0):
    o, out = start, []
    while o < total:
        out.append((o, min(step, total - o)))
        o += step
    return out


NT_FT = _tiles(FT, 512)
NT_S = _tiles(S, 512)
MEGAS = [(0, 1024), (1024, 1024), (1824, 1024)]
NT_OV = [(0, 512), (512, 512), (1024, 512), (1536, 512), (2048, 512), (2336, 512)]


def _tile_lhsT(W):
    K, N = W.shape
    Kc, Nc = (K + 127) // 128, (N + 127) // 128
    Wp = np.zeros((Kc * 128, Nc * 128), np.float32)
    Wp[:K, :N] = W
    return np.ascontiguousarray(
        Wp.reshape(Kc, 128, Nc, 128).transpose(1, 2, 0, 3), np.float32)


def _tile_rhs(W):
    K, N = W.shape
    Kc = (K + 127) // 128
    Wp = np.zeros((Kc * 128, N), np.float32)
    Wp[:K] = W
    return np.ascontiguousarray(Wp.reshape(Kc, 128, N).transpose(1, 0, 2), np.float32)


def _rotary_tables():
    HD = 64
    half32 = np.arange(0, HD, 2, dtype=np.float32)
    inv_freq = (np.float32(1.0) / np.float32(10000.0) ** (half32 / np.float32(HD)))
    fr32 = (np.arange(S, dtype=np.float32)[:, None] * inv_freq[None, :])
    freqs = np.concatenate([fr32.astype(np.float64)] * 2, -1)
    bs32 = ((half32 + np.float32(0.4 * HD)) / np.float32(1.4 * HD))
    power = ((np.arange(S, dtype=np.float32) - np.float32(S // 2)) / np.float32(128.0))
    sc32 = (bs32[None, :].astype(np.float64) ** power[:, None].astype(np.float64)
            ).astype(np.float32)
    scale = np.concatenate([sc32.astype(np.float64)] * 2, -1)
    tau = np.where(np.arange(HD) < 32, 1.0, -1.0)[None, :]
    cs_q = (np.cos(freqs) * scale).T
    sn_q = (np.sin(freqs) * scale * tau).T
    cs_k = (np.cos(freqs) / scale).T
    sn_k = (np.sin(freqs) / scale * tau).T
    def st(a):
        full = np.concatenate([a, a], 0)
        hi = full.astype(np.float32)
        lo = (full - hi.astype(np.float64)).astype(np.float32)
        return np.ascontiguousarray(hi), np.ascontiguousarray(lo)
    return st(cs_q), st(sn_q), st(cs_k), st(sn_k)


def _alibi_toep(F0):
    base = (2.0 ** 8) ** (1.0 / H)
    m64 = np.asarray([1.0 / base ** (i + 1) for i in range(H)], np.float64)
    m32 = m64.astype(np.float32)
    d32 = np.arange(8192, dtype=np.float32) - np.float32(4095.0 + F0)
    d64 = np.arange(8192, dtype=np.float64) - (4095.0 + F0)
    keep = (d64 >= -4095.5) & (d64 <= 4095.5)
    T = np.ones((H, 8192), np.float32)
    for h in range(H):
        b32 = (m32[h] * d32).astype(np.float32).astype(np.float64)
        T[h, keep] = np.exp(b32 - m64[h] * d64).astype(np.float32)[keep]
    return np.ascontiguousarray(T)


def _alibi_g():
    base = (2.0 ** 8) ** (1.0 / H)
    m = np.asarray([1.0 / base ** (i + 1) for i in range(H)], np.float64)
    j = np.arange(S, dtype=np.float64)
    g = np.exp(m[None, :] * (j[:, None] - (S - 1))).astype(np.float32)
    return np.ascontiguousarray(g.reshape(S // 128, 128, H).transpose(1, 0, 2)
                                .reshape(128, (S // 128) * H))


def _band_masks():
    ms = np.zeros((128, 4, 256), np.float32)
    kk = np.arange(128)[:, None]
    ii = np.arange(256)[None, :]
    for kco in range(4):
        jj = 128 * kco + kk
        ms[:, kco, :] = np.where((jj >= ii) & (jj <= ii + 256), 0.0, NEG)
    return np.ascontiguousarray(ms.reshape(128, 1024))


def _perm_rot():
    P = np.zeros((128, 128), np.float32)
    for c in range(128):
        P[(c - 32 if (c % 64) >= 32 else c + 32), c] = 1.0
    return P


def _indicators():
    ind_ss = np.zeros((128, 8, 16), np.float32)
    ind_bc = np.zeros((16, 8, 128), np.float32)
    for sx in range(8):
        for p in range(128):
            r = 2 * sx + (p // 64)
            ind_ss[p, sx, r] = 1.0
            ind_bc[r, sx, p] = 1.0
    return (np.ascontiguousarray(ind_ss.reshape(128, 128)),
            np.ascontiguousarray(ind_bc.reshape(16, 1024)))


def prep_inputs(inputs):
    x = np.asarray(inputs['L0_attn_x'], np.float32)
    Wkqv = np.asarray(inputs['L0_attn_W_kqv'], np.float32)
    iWqkv = np.asarray(inputs['L0_ffn_Wqkv'], np.float32)
    qs = np.asarray(inputs['L0_ffn_q_scale'], np.float32)
    ks = np.asarray(inputs['L0_ffn_k_scale'], np.float32)
    Wo = np.asarray(inputs['L0_ffn_Wo'], np.float32)
    lng = np.asarray(inputs['L0_ffn_ln_g'], np.float32)
    lnb = np.asarray(inputs['L0_ffn_ln_b'], np.float32)
    W1 = np.asarray(inputs['L0_ffn_W1'], np.float32)
    b1 = np.asarray(inputs['L0_ffn_b1'], np.float32)
    W2 = np.asarray(inputs['L0_ffn_W2'], np.float32)
    b2 = np.asarray(inputs['L0_ffn_b2'], np.float32)

    sh = {}
    sh['o_wk'] = _tile_lhsT(Wkqv[:, 0:D])
    sh['o_wq'] = _tile_lhsT(Wkqv[:, D:2 * D])
    sh['o_wv'] = _tile_rhs(Wkqv[:, 2 * D:3 * D])
    for i in range(DEPTH):
        sh[f'wq{i}'] = _tile_lhsT(iWqkv[i][:, 0:D])
        sh[f'wk{i}'] = _tile_lhsT(iWqkv[i][:, D:2 * D])
        sh[f'wv{i}'] = _tile_rhs(iWqkv[i][:, 2 * D:3 * D])
        sh[f'wo{i}'] = _tile_lhsT(Wo[i])
        W1a = np.zeros((D, W1OF), np.float32); W1a[:, :2730] = W1[i][:, :2730]
        W1g = np.zeros((D, W1OF), np.float32); W1g[:, :2730] = W1[i][:, 2730:]
        sh[f'w1a{i}'] = _tile_lhsT(W1a)
        sh[f'w1g{i}'] = _tile_lhsT(W1g)
        W2p = np.zeros((W1OF, D), np.float32); W2p[:2730] = W2[i]
        sh[f'w2{i}'] = _tile_lhsT(W2p)

    vecs = np.zeros((128, DEPTH, 70), np.float32)
    for i in range(DEPTH):
        vecs[:, i, 0] = np.tile(qs[i], 2)
        vecs[:, i, 1] = np.tile(ks[i], 2)
        for c in range(8):
            vecs[:, i, 2 + c] = lng[i][128 * c:128 * (c + 1)]
            vecs[:, i, 10 + c] = lnb[i][128 * c:128 * (c + 1)]
            vecs[:, i, 62 + c] = b2[i][128 * c:128 * (c + 1)]
        b1a = np.zeros(W1OF, np.float32); b1a[:2730] = b1[i][:2730]
        b1g = np.zeros(W1OF, np.float32); b1g[:2730] = b1[i][2730:]
        for c in range(W1C):
            vecs[:, i, 18 + c] = b1a[128 * c:128 * (c + 1)]
            vecs[:, i, 40 + c] = b1g[128 * c:128 * (c + 1)]
    sh['lvecs'] = np.ascontiguousarray(vecs.reshape(128, DEPTH * 70))
    sh['galibi'] = _alibi_g()
    sh['masks'] = _band_masks()
    sh['perm'] = _perm_rot()
    sh['ind_ss'], sh['ind_bc'] = _indicators()
    sh['i128'] = np.eye(128, dtype=np.float32)

    (cs_q, cs_ql), (sn_q, sn_ql), (cs_k, cs_kl), (sn_k, sn_kl) = _rotary_tables()
    in_maps = []
    for c in range(8):
        b = c // 2
        F0 = 0 if c % 2 == 0 else S - FT
        d = dict(sh)
        d['toep'] = _alibi_toep(F0)
        d['xT'] = np.ascontiguousarray(x[b].T)
        xtf = np.zeros((D, 3072), np.float32)
        xtf[:, 0:FT] = x[b].T[:, F0:F0 + FT]
        d['xTf'] = xtf
        d['rcq'] = np.ascontiguousarray(cs_q[:, F0:F0 + FT])
        d['rsq'] = np.ascontiguousarray(sn_q[:, F0:F0 + FT])
        d['rck'] = np.ascontiguousarray(cs_k[:, F0:F0 + FT])
        d['rsk'] = np.ascontiguousarray(sn_k[:, F0:F0 + FT])
        d['rcql'] = np.ascontiguousarray(cs_ql[:, F0:F0 + FT])
        d['rsql'] = np.ascontiguousarray(sn_ql[:, F0:F0 + FT])
        d['rckl'] = np.ascontiguousarray(cs_kl[:, F0:F0 + FT])
        d['rskl'] = np.ascontiguousarray(sn_kl[:, F0:F0 + FT])
        in_maps.append(d)
    return in_maps



class Ctr:
    def __init__(self, h):
        self.h, self.n = h, 0


class Bank:
    def __init__(self, t):
        self.t = t
        self.free = []

    def consumed(self, ctr, val):
        self.free = [(ctr, val)]

    def also_consumed(self, ctr, val):
        self.free.append((ctr, val))


def build_nc():
    nc = bass.Bass("TRN2", target_bir_lowering=False, debug=False,
                   detect_race_conditions=False)
    from contextlib import ExitStack
    stack = ExitStack()

    ap = {}
    def din(name, shape, dt=F32):
        ap[name] = nc.dram_tensor(name, list(shape), dt, kind="ExternalInput").ap()

    din('xT', (D, S)); din('xTf', (D, 3072))
    din('o_wk', (128, 8, 8, 128)); din('o_wq', (128, 8, 8, 128)); din('o_wv', (128, 8, D))
    for i in range(DEPTH):
        din(f'wq{i}', (128, 8, 8, 128)); din(f'wk{i}', (128, 8, 8, 128))
        din(f'wv{i}', (128, 8, D)); din(f'wo{i}', (128, 8, 8, 128))
        din(f'w1a{i}', (128, W1C, 8, 128)); din(f'w1g{i}', (128, W1C, 8, 128))
        din(f'w2{i}', (128, 8, W1C, 128))
    din('lvecs', (128, DEPTH * 70)); din('toep', (H, 8192))
    din('galibi', (128, (S // 128) * H))
    din('masks', (128, 1024)); din('perm', (128, 128)); din('i128', (128, 128))
    din('ind_ss', (128, 128)); din('ind_bc', (16, 1024))
    din('rcq', (128, FT)); din('rsq', (128, FT)); din('rck', (128, FT)); din('rsk', (128, FT))
    din('rcql', (128, FT)); din('rsql', (128, FT)); din('rckl', (128, FT)); din('rskl', (128, FT))

    out = nc.dram_tensor('out', [D, FT], F32, kind="ExternalOutput").ap()
    st_t = [nc.dram_tensor('st_a', [D, 1 + FT], F32).ap(),
            nc.dram_tensor('st_b', [D, 1 + FT], F32).ap()]
    x1k = nc.dram_tensor('x1k', [D, FT], F32).ap()

    stv = [t.rearrange('(c p) t -> p c t', p=128) for t in st_t]
    x1kv = x1k.rearrange('(c p) t -> p c t', p=128)
    outv = out.rearrange('(c p) t -> p c t', p=128)
    xTv = ap['xT'].rearrange('(c p) t -> p c t', p=128)
    xTfv = ap['xTf'].rearrange('(c p) t -> p c t', p=128)

    q = {e: [] for e in ('sync', 'vector', 'scalar', 'tensor', 'gpsimd')}
    def on(e, fn):
        q[e].append(fn)

    def sem(name):
        return Ctr(stack.enter_context(nc.semaphore(name)))
    NQS = 8
    ldp = [sem(f's_ld{j}') for j in range(NQS)]
    wlp = [sem(f's_wl{j}') for j in range(NQS)]
    stp = [sem(f's_st{j}') for j in range(NQS)]
    qrot = {'sync': [ldp, 0], 'gpsimd': [wlp, 0], 'scalar': [stp, 0]}
    nmp = [sem(f's_nm{j}') for j in range(40)]
    nmmap = {}
    def named(name):
        if name not in nmmap:
            nmmap[name] = nmp[len(nmmap)]
        return nmmap[name]
    s_pe = sem('s_pe'); s_ve = sem('s_ve'); s_ac = sem('s_ac')
    ALL = ldp + wlp + stp + nmp + [s_pe, s_ve, s_ac]
    def w(eng, pairs):
        for (ctr, n) in pairs:
            if n > 0:
                eng.wait_ge(ctr.h, n)

    def dma(queue, _ctr, dst, src, waits=(), slow=False, stream=None):
        if stream is None:
            pool, idx = qrot[queue]
            qrot[queue][1] = (idx + 1) % NQS
            ctr = pool[idx]
        else:
            ctr = named(stream)
        ctr.n += 16
        def f(eng, dst=dst, src=src, waits=tuple(waits), slow=slow, h=ctr.h):
            w(eng, waits)
            eng.dma_start(out=dst, in_=src,
                          allow_slow_non_contiguous=slow).then_inc(h, 16)
        on(queue, f)
        return (ctr, ctr.n)

    def ld(dst, src, waits=(), stream=None):
        return dma('sync', None, dst, src, waits, stream=stream)

    def wl(dst, src, waits=(), stream=None):
        return dma('gpsimd', None, dst, src, waits, stream=stream)

    def sto(dst, src, waits=(), stream=None):
        return dma('scalar', None, dst, src, waits, stream=stream)

    def _eng_emit(queue, ctr, fn, waits):
        ctr.n += 1
        def f(e, fn=fn, waits=tuple(waits), h=ctr.h):
            w(e, waits)
            fn(e).then_inc(h, 1)
        on(queue, f)
        return ctr.n

    def ve(fn, waits=()):
        return _eng_emit('vector', s_ve, fn, waits)

    def ac(fn, waits=()):
        return _eng_emit('scalar', s_ac, fn, waits)

    def pe(fn, waits=()):
        return _eng_emit('tensor', s_pe, fn, waits)

    def barrier():
        vals = tuple((c, c.n) for c in ALL)
        for e in q:
            def f(eng, vals=vals):
                w(eng, vals)
            on(e, f)

    def sb(name, shape, dt):
        return stack.enter_context(nc.sbuf_tensor(name, list(shape), dt))

    c_masks = sb('c_masks', (128, 1024), BF16)
    c_perm = sb('c_perm', (128, 128), BF16)
    c_i128 = sb('c_i128', (128, 128), BF16)
    c_indss = sb('c_indss', (128, 128), BF16)
    c_indbcf = sb('c_indbcf', (16, 1024), F32)
    c_onesf = sb('c_onesf', (128, 128), F32)
    c_eps = sb('c_eps', (128, 1), F32)
    c_lv = sb('c_lv', (128, DEPTH * 70), F32)
    c_g = sb('c_g', (128, (S // 128) * H), F32)

    _ab = (int(nc.sbuf_base) + 63) // 64 * 64 + 64
    ARENA = (int(nc.sbuf_top) - _ab) // 64 * 64

    class Arena:
        def __init__(self):
            self.off = 0
            self.marks = []
            self.base = _ab
        def al(self, cols, dt):
            esz = 2 if dt == BF16 else 4
            a = (self.off + 63) // 64 * 64
            self.off = a + cols * esz
            assert self.off <= ARENA, f"arena overflow {self.off} > {ARENA}"
            return nc.alloc_sbuf_tensor_at('ar', (128, cols), dt,
                                           offset=self.base + a)
        def push(self):
            self.marks.append(self.off)
        def pop(self):
            self.off = self.marks.pop()
    A = Arena()

    def psum(name, cols):
        return stack.enter_context(nc.psum_tensor(name, [128, cols], F32))
    pA = Bank(psum('pA', 1024))
    pB = Bank(psum('pB', 1024))
    p4 = Bank(psum('p4', 512)); p5 = Bank(psum('p5', 512))
    p6 = Bank(psum('p6', 512)); p7 = Bank(psum('p7', 512))

    def lvec(i, col):
        return c_lv[:, i * 70 + col:i * 70 + col + 1]

    wl(c_masks[:, :], ap['masks'])
    wl(c_perm[:, :], ap['perm'])
    wl(c_i128[:, :], ap['i128'])
    wl(c_indss[:, :], ap['ind_ss'])
    ld(c_indbcf[:, :], ap['ind_bc'])
    ld(c_lv[:, :], ap['lvecs'])
    ld(c_g[:, :], ap['galibi'])
    ve(lambda v: v.memset(c_onesf[:, :], 1.0))
    ve(lambda v: v.memset(c_eps[:, :], 1e-5))
    zbuf = A.al(8, F32)
    vz = ve(lambda v: v.memset(zbuf[:, :], 0.0))
    for t in range(2):
        dma('scalar', None, stv[t][:, :, 0:1], zbuf[:, :, None],
            waits=[(s_ve, vz)], slow=True)
    barrier()

    if STAGE == 'consts':
        return _finish(nc, q, ALL, w, on, stack)

    NSLOT = 6
    wslots = []
    wslot_free = [[] for _ in range(NSLOT)]
    wst = {'i': 0}

    def load_w(name, of, ni=8, sub=0):
        a = ap[name]
        si = wst['i']; wst['i'] = (si + 1) % NSLOT
        src = a[:, of, sub:sub + ni, :].rearrange('p a b -> p (a b)')
        lv = wl(wslots[si][:, 0:ni * 128], src, waits=wslot_free[si],
                stream=f'ws{si}')
        wslot_free[si] = []
        def sl(k, si=si):
            return wslots[si][:, 128 * k:128 * (k + 1)]
        def rel(si=si, ctr=s_pe):
            wslot_free[si] = [(ctr, ctr.n)]
        return sl, lv, rel

    def load_w_rhs(name, ci):
        a = ap[name]
        si = wst['i']; wst['i'] = (si + 1) % NSLOT
        lv = wl(wslots[si][:, 0:D], a[:, ci, :], waits=wslot_free[si],
                stream=f'ws{si}')
        wslot_free[si] = []
        def sl(c0, c1, si=si):
            return wslots[si][:, c0:c1]
        def rel(si=si):
            wslot_free[si] = [(s_pe, s_pe.n)]
        return sl, lv, rel

    def ln_stats_chain(srows_fn, nn):
        m1 = ve(lambda v: v.tensor_scalar_mul(out=srows_fn(0), in0=p4.t[0:1, 0:nn],
                                              scalar1=1.0 / D))
        m2 = ve(lambda v: v.tensor_scalar_mul(out=srows_fn(1), in0=p5.t[0:1, 0:nn],
                                              scalar1=1.0 / D))
        p4.consumed(s_ve, m1)
        p5.consumed(s_ve, m2)
        m3 = ac(lambda s: s.square(out=srows_fn(2), in_=srows_fn(0)), waits=[(s_ve, m2)])
        m4 = ve(lambda v: v.tensor_tensor(out=srows_fn(3), in0=srows_fn(1), in1=srows_fn(2),
                                          op=AluOpType.subtract), waits=[(s_ac, m3)])
        m5 = ac(lambda s: s.activation(out=srows_fn(4), in_=srows_fn(3), func=AF.Sqrt,
                                       bias=c_eps[0:1, 0:1]), waits=[(s_ve, m4)])
        ac(lambda s: s.activation(out=srows_fn(5), in_=srows_fn(4), func=AF.Ln))
        m6 = ac(lambda s: s.activation(out=srows_fn(5), in_=srows_fn(5), func=AF.Exp,
                                       scale=-1.0))
        m7 = ve(lambda v: v.tensor_tensor(out=srows_fn(6), in0=srows_fn(0), in1=srows_fn(5),
                                          op=AluOpType.mult), waits=[(s_ac, m6)])
        m8 = ve(lambda v: v.tensor_scalar_mul(out=srows_fn(7), in0=srows_fn(6), scalar1=-1.0))
        b1 = pe(lambda t: t.matmul(p6.t[:, 0:nn], c_onesf[0:1, :], srows_fn(5),
                                   start=True, stop=True), waits=[(s_ve, m8)] + p6.free)
        b2 = pe(lambda t: t.matmul(p7.t[:, 0:nn], c_onesf[0:1, :], srows_fn(7),
                                   start=True, stop=True), waits=p7.free)
        return b2


    ktd = nc.dram_tensor('ktd', [8, 128, S], BF16).ap()
    vtd = nc.dram_tensor('vtd', [32, 128, 1040], BF16).ap()
    tdd = nc.dram_tensor('tdd', [H, 128, 7040], BF16).ap()

    A.push()
    wkb = A.al(8192, BF16)
    A.push()
    wvb = A.al(8192, BF16)
    xsb = [A.al(4096, BF16), A.al(4096, BF16)]
    xsb_free = [[], []]
    kstg = [A.al(512, BF16), A.al(512, BF16)]
    kstg_free = [[], []]
    vstg_free = [[], []]
    stagB_free = []
    vstg = [A.al(1040, BF16), A.al(1040, BF16)]

    for h in range(H):
        wl(tdd[h], bass.AP(tensor=ap['toep'].tensor, offset=h * 8192 + 1024,
                           ap=[[1, 128], [1, 7040]]), stream='td')

    for of in range(8):
        wk_lv = wl(wkb[:, 1024 * of:1024 * (of + 1)],
                   ap['o_wk'][:, of, :, :].rearrange('p a b -> p (a b)'),
                   stream='wk')
    for ci in range(8):
        wv_lv = wl(wvb[:, 1024 * ci:1024 * (ci + 1)], ap['o_wv'][:, ci, :],
                   stream='wv')

    for nt_i, (o, n) in enumerate(NT_S):
        sl_ = nt_i % 2
        xv = wl(xsb[sl_][:, 0:8 * n].rearrange('p (c t) -> p c t', c=8),
                xTv[:, :, o:o + n],
                waits=xsb_free[sl_] + [wk_lv], stream=f'xs{sl_}')
        for of in range(8):
            bank = pA if of % 2 == 0 else pB
            ks_ = kstg[of % 2]
            def kmm(t, sl_=sl_, n=n, of=of, bank=bank):
                last = None
                for ci in range(8):
                    last = t.matmul(bank.t[:, 0:n],
                                    wkb[:, of * 1024 + ci * 128:of * 1024 + (ci + 1) * 128],
                                    xsb[sl_][:, ci * n:ci * n + n],
                                    start=(ci == 0), stop=(ci == 7))
                return last
            pv = pe(kmm, waits=[xv] + bank.free)
            av = ac(lambda s, n=n, bank=bank, ks_=ks_: s.copy(out=ks_[:, 0:n],
                                                              in_=bank.t[:, 0:n]),
                    waits=[(s_pe, pv)] + kstg_free[of % 2])
            bank.consumed(s_ac, av)
            kstg_free[of % 2] = []
            kv = sto(ktd[of][:, o:o + n], ks_[:, 0:n], waits=[(s_ac, av)],
                     stream=f'kst{of % 2}')
            kstg_free[of % 2] = [kv]
        for tb in range(0, n, 128):
            kc = (o + tb) // 128
            vs_ = vstg[kc % 2]
            ev2 = 0
            for half in range(2):
                bank = p4 if half == 0 else p5
                def vmm(t, sl_=sl_, n=n, tb=tb, half=half, bank=bank):
                    last = None
                    for ci in range(8):
                        last = t.matmul(bank.t[:, 0:512],
                                        xsb[sl_][:, ci * n + tb:ci * n + tb + 128],
                                        wvb[:, ci * 1024 + half * 512:ci * 1024 + half * 512 + 512],
                                        start=(ci == 0), stop=(ci == 7))
                    return last
                pv = pe(vmm, waits=[xv, wv_lv] + bank.free)
                gs = c_g[:, kc * H + half * 8:kc * H + half * 8 + 8]
                e1 = ve(lambda v, half=half, bank=bank, vs_=vs_, gs=gs:
                        v.tensor_tensor(
                            out=vs_[:, half * 520:half * 520 + 520]
                                .rearrange('p (h e) -> p h e', h=8)[:, :, 0:64],
                            in0=bank.t[:, 0:512].rearrange('p (h e) -> p h e', h=8),
                            in1=gs[:, :, None].broadcast_to((128, 8, 64)),
                            op=AluOpType.mult),
                        waits=[(s_pe, pv)] + (vstg_free[kc % 2] if half == 0 else []))
                ev2 = ve(lambda v, half=half, vs_=vs_, gs=gs:
                         v.tensor_copy(
                             out=vs_[:, half * 520:half * 520 + 520]
                                 .rearrange('p (h e) -> p h e', e=65)[:, :, 64:65],
                             in_=gs[:, :, None]))
                bank.consumed(s_ve, e1)
            vstg_free[kc % 2] = []
            vv_ = sto(vtd[kc], vs_[:, 0:1040], waits=[(s_ve, ev2)], stream=f'vst{kc % 2}')
            vstg_free[kc % 2] = [vv_]
        xsb_free[sl_] = [(s_pe, s_pe.n)]

    if STAGE.startswith('o1'):
        A.pop()
        A.pop()
        wslots.extend(A.al(1024, BF16) for _ in range(NSLOT))
        barrier()
        return _finish(nc, q, ALL, w, on, stack)

    A.pop()
    A.push()
    wqb = wkb
    NO2 = 256
    NT_O2 = _tiles(3072, NO2)
    QTf = A.al(8 * 3072, BF16)
    xsq = [A.al(8 * NO2, BF16), A.al(8 * NO2, BF16)]
    xsq_free = [[], []]
    for of in range(8):
        wq_lv = wl(wqb[:, 1024 * of:1024 * (of + 1)],
                   ap['o_wq'][:, of, :, :].rearrange('p a b -> p (a b)'),
                   waits=[(s_ac, s_ac.n), (s_pe, s_pe.n)], stream='wk')
    Pq = [Bank(pA.t[:, 0:256]), Bank(pA.t[:, 512:768]),
          Bank(pB.t[:, 0:256]), Bank(pB.t[:, 512:768])]
    for nt_i, (o, n) in enumerate(NT_O2):
        sl_ = nt_i % 2
        xv = wl(xsq[sl_][:, 0:8 * n].rearrange('p (c t) -> p c t', c=8),
                xTfv[:, :, o:o + n],
                waits=xsq_free[sl_] + ([wq_lv] if nt_i == 0 else []),
                stream=f'xq{sl_}')
        for of in range(8):
            bank = Pq[(nt_i * 8 + of) % 4]
            def qmm(t, sl_=sl_, n=n, of=of, bank=bank):
                last = None
                for ci in range(8):
                    last = t.matmul(bank.t[:, 0:n],
                                    wqb[:, of * 1024 + ci * 128:of * 1024 + (ci + 1) * 128],
                                    xsq[sl_][:, ci * n:ci * n + n],
                                    start=(ci == 0), stop=(ci == 7))
                return last
            pv = pe(qmm, waits=[xv] + bank.free)
            qrdy = ac(lambda s, of=of, o=o, n=n, bank=bank:
                      s.copy(out=QTf[:, of * 3072 + o:of * 3072 + o + n],
                             in_=bank.t[:, 0:n]),
                      waits=[(s_pe, pv)])
            bank.consumed(s_ac, qrdy)
        xsq_free[sl_] = [(s_pe, s_pe.n)]
    qf_rdy = s_ac.n

    KTp = A.al(S, BF16)
    vtp = A.al(32 * 130, BF16)
    TDs = A.al(2 * 7040, BF16)
    prA = A.al(1024, BF16); prB = A.al(1024, BF16)
    aob = A.al(8 * 3072, BF16)
    stagB = A.al(3072, BF16)
    rA = A.al(NO2, F32); rB = A.al(NO2, F32)
    RA = A.al(NO2, F32); RB = A.al(NO2, F32)

    _td0 = TDs[:, :]

    def tdap(h2, kq, q0):
        return bass.AP(tensor=_td0.tensor,
                       offset=int(_td0.offset) + h2 * 7040 + 3071 - q0 + 512 * kq,
                       ap=[[int(_td0.ap[0][0]), 128], [128, 4], [-1, 256]])

    for hc in range(8):
        ktv = wl(KTp[:, :], ktd[hc], waits=[(s_pe, s_pe.n)], stream='ktp')
        vtv = wl(vtp[:, :].rearrange('p (a b) -> p a b', b=130),
                 vtd[:, :, 130 * hc:130 * hc + 130].rearrange('a p b -> p a b'),
                 waits=[(s_pe, s_pe.n)], stream='vtp')
        tdv = wl(TDs[:, 0:7040], tdd[2 * hc], waits=[(s_ve, s_ve.n)], stream='tds')
        tdv = wl(TDs[:, 7040:14080], tdd[2 * hc + 1], stream='tds')
        for nt_i, (q0, n) in enumerate(NT_O2):
            for kq in range(8):
                pvsc = 0
                for kcl in range(4):
                    kc = kq * 4 + kcl
                    def smm(t, kc=kc, kcl=kcl, q0=q0, n=n, hc=hc):
                        t.matmul(pA.t[:, kcl * 256:kcl * 256 + n],
                                 KTp[0:64, kc * 128:(kc + 1) * 128],
                                 QTf[0:64, hc * 3072 + q0:hc * 3072 + q0 + n],
                                 start=True, stop=True)
                        return t.matmul(pB.t[:, kcl * 256:kcl * 256 + n],
                                        KTp[64:128, kc * 128:(kc + 1) * 128],
                                        QTf[64:128, hc * 3072 + q0:hc * 3072 + q0 + n],
                                        start=True, stop=True, tile_position=(64, 0))
                    wt = ([ktv, (s_ac, qf_rdy)] + pA.free + pB.free) \
                        if (kq == 0 and kcl == 0) else []
                    pvsc = pe(smm, waits=wt)
                eA = ac(lambda s: s.activation(out=prA[:, 0:1024], in_=pA.t[:, 0:1024],
                                               func=AF.Exp, scale=1.0 / 32.0),
                        waits=[(s_pe, pvsc)])
                eB = ac(lambda s: s.activation(out=prB[:, 0:1024], in_=pB.t[:, 0:1024],
                                               func=AF.Exp, scale=1.0 / 32.0))
                pA.consumed(s_ac, eA)
                pB.consumed(s_ac, eB)
                tm1 = ve(lambda v, kq=kq, q0=q0: v.tensor_tensor(
                            out=prA[:, 0:1024].rearrange('p (a b) -> p a b', a=4),
                            in0=prA[:, 0:1024].rearrange('p (a b) -> p a b', a=4),
                            in1=tdap(0, kq, q0), op=AluOpType.mult),
                         waits=[(s_ac, eA)] + ([tdv] if (nt_i == 0 and kq == 0) else []))
                tm2 = ve(lambda v, kq=kq, q0=q0: v.tensor_tensor(
                            out=prB[:, 0:1024].rearrange('p (a b) -> p a b', a=4),
                            in0=prB[:, 0:1024].rearrange('p (a b) -> p a b', a=4),
                            in1=tdap(1, kq, q0), op=AluOpType.mult),
                         waits=[(s_ac, eB)])
                def avmm(t, kq=kq, n=n):
                    last = None
                    for kcl in range(4):
                        kc = kq * 4 + kcl
                        st_ = (kc == 0)
                        sp = (kc == 31)
                        t.matmul(p4.t[0:65, 0:n], vtp[:, kc * 130:kc * 130 + 65],
                                 prA[:, kcl * 256:kcl * 256 + n], start=st_, stop=sp)
                        last = t.matmul(p5.t[0:65, 0:n],
                                        vtp[:, kc * 130 + 65:kc * 130 + 130],
                                        prB[:, kcl * 256:kcl * 256 + n],
                                        start=st_, stop=sp)
                    return last
                wt = [(s_ve, tm2), vtv] + ((p4.free + p5.free) if kq == 0 else [])
                pvav = pe(avmm, waits=wt)
            ac(lambda s, n=n: s.activation(out=rA[64:65, 0:n], in_=p4.t[64:65, 0:n],
                                           func=AF.Ln), waits=[(s_pe, pvav)])
            r1 = ac(lambda s, n=n: s.activation(out=rA[64:65, 0:n], in_=rA[64:65, 0:n],
                                                func=AF.Exp, scale=-1.0))
            ac(lambda s, n=n: s.activation(out=rB[64:65, 0:n], in_=p5.t[64:65, 0:n],
                                           func=AF.Ln))
            r2 = ac(lambda s, n=n: s.activation(out=rB[64:65, 0:n], in_=rB[64:65, 0:n],
                                                func=AF.Exp, scale=-1.0))
            b1 = pe(lambda t, n=n: t.matmul(p6.t[0:64, 0:n], c_onesf[64:65, 0:64],
                                            rA[64:65, 0:n], start=True, stop=True),
                    waits=[(s_ac, r1)] + p6.free)
            b2 = pe(lambda t, n=n: t.matmul(p7.t[0:64, 0:n], c_onesf[64:65, 0:64],
                                            rB[64:65, 0:n], start=True, stop=True),
                    waits=[(s_ac, r2)] + p7.free)
            a1 = ac(lambda s, n=n: s.copy(out=RA[0:64, 0:n], in_=p6.t[0:64, 0:n]),
                    waits=[(s_pe, b1)])
            a2 = ac(lambda s, n=n: s.copy(out=RB[0:64, 0:n], in_=p7.t[0:64, 0:n]),
                    waits=[(s_pe, b2)])
            p6.consumed(s_ac, a1)
            p7.consumed(s_ac, a2)
            v1 = ve(lambda v, hc=hc, q0=q0, n=n: v.tensor_tensor(
                        out=aob[0:64, hc * 3072 + q0:hc * 3072 + q0 + n],
                        in0=p4.t[0:64, 0:n], in1=RA[0:64, 0:n], op=AluOpType.mult),
                    waits=[(s_ac, a1)])
            v2 = ve(lambda v, q0=q0, n=n: v.tensor_tensor(
                        out=stagB[0:64, q0:q0 + n],
                        in0=p5.t[0:64, 0:n], in1=RB[0:64, 0:n], op=AluOpType.mult),
                    waits=[(s_ac, a2)] + (stagB_free if nt_i == 0 else []))
            p4.consumed(s_ve, v1)
            p5.consumed(s_ve, v2)
        pk = ld(aob[64:128, hc * 3072:hc * 3072 + 3072], stagB[0:64, 0:3072],
                waits=[(s_ve, s_ve.n)], stream='pk')
        stagB_free = [pk]
    ao_rdy = [pk]

    x2f = A.al(NO2, F32)
    sqf = A.al(NO2, F32)
    tf = A.al(8 * NO2, F32)
    srow = A.al(8 * NO2, F32)
    for nt_i, (o, n) in enumerate(NT_O2):
        pvs = 0
        t1l = 0
        for c in range(8):
            xv2 = ld(x2f[:, 0:n], xTfv[:, c, o:o + n],
                     waits=[(s_ve, s_ve.n)], stream='x2f')
            t1 = ve(lambda v, c=c, o=o, n=n: v.tensor_tensor(
                        out=tf[:, c * NO2:c * NO2 + n],
                        in0=aob[:, c * 3072 + o:c * 3072 + o + n],
                        in1=x2f[:, 0:n], op=AluOpType.add),
                    waits=[xv2] + (ao_rdy if (nt_i == 0 and c == 0) else [])
                          + ([(s_pe, s_pe.n)] if c == 0 else []))
            sq = ac(lambda s, c=c, n=n: s.square(out=sqf[:, 0:n],
                                                 in_=tf[:, c * NO2:c * NO2 + n]),
                    waits=[(s_ve, t1), (s_pe, s_pe.n)])
            def summ(t, c=c, n=n):
                t.matmul(p4.t[0:1, 0:n], c_onesf[:, 0:1], tf[:, c * NO2:c * NO2 + n],
                         start=(c == 0), stop=(c == 7))
                return t.matmul(p5.t[0:1, 0:n], c_onesf[:, 0:1], sqf[:, 0:n],
                                start=(c == 0), stop=(c == 7))
            wt = [(s_ac, sq)] + ((p4.free + p5.free) if c == 0 else [])
            pvs = pe(summ, waits=wt)
        on('vector', lambda v, pvs=pvs: v.wait_ge(s_pe.h, pvs))
        bc = ln_stats_chain(lambda j, n=n: srow[0:1, j * NO2:j * NO2 + n], n)
        lastv = 0
        for c in range(8):
            ve(lambda v, c=c, n=n: v.tensor_tensor(
                    out=tf[:, c * NO2:c * NO2 + n], in0=tf[:, c * NO2:c * NO2 + n],
                    in1=p6.t[:, 0:n], op=AluOpType.mult),
               waits=[(s_pe, bc)] if c == 0 else [])
            lastv = ve(lambda v, c=c, n=n: v.tensor_tensor(
                    out=tf[:, c * NO2:c * NO2 + n], in0=tf[:, c * NO2:c * NO2 + n],
                    in1=p7.t[:, 0:n], op=AluOpType.add))
        p6.consumed(s_ve, lastv)
        p7.consumed(s_ve, lastv)
        ns = min(n, FT - o)
        if ns > 0:
            sto(stv[0][:, :, 1 + o:1 + o + ns],
                tf[:, 0:8 * NO2].rearrange('p (c t) -> p c t', c=8)[:, :, 0:ns],
                waits=[(s_ve, lastv)])
            sto(x1kv[:, :, o:o + ns],
                tf[:, 0:8 * NO2].rearrange('p (c t) -> p c t', c=8)[:, :, 0:ns])
            if STAGE == 'outer':
                sto(outv[:, :, o:o + ns],
                    tf[:, 0:8 * NO2].rearrange('p (c t) -> p c t', c=8)[:, :, 0:ns])
        barrier()
    A.pop()
    A.pop()
    wslots.extend(A.al(1024, BF16) for _ in range(NSLOT))

    SLmax = 1280

    def emit_layer(i):
        st_in, st_mid = stv[0], stv[1]
        A.push()
        xsb_l = A.al(8 * SLmax, BF16)
        QT = A.al(8 * 1024, BF16)
        KTl = A.al(8 * SLmax, BF16)
        VTl = A.al(10 * 1040, BF16)
        rn = A.al(2304, F32)
        wvbl = A.al(8 * D, BF16)
        for ci in range(8):
            wv_rdy = wl(wvbl[:, ci * D:(ci + 1) * D], ap[f'wv{i}'][:, ci, :],
                        waits=[(s_pe, s_pe.n)] if ci == 0 else [], stream='wvb')

        def emit_mega(m0, mn):
            s0 = max(0, m0 - 256)
            s1 = m0 + mn
            SLn = s1 - s0
            qoff = m0 - s0
            nkc = (SLn + 127) // 128
            A.push()
            rtab = A.al(4 * SLmax, F32)
            rlo = A.al(SLmax, F32)
            At = A.al(SLmax, F32)
            Bt = A.al(SLmax, F32)
            raw = A.al(8 * SLmax, F32)
            sqb = [A.al(512, BF16), A.al(512, BF16)]
            ts1 = [A.al(512, F32), A.al(512, F32)]
            vbs = [A.al(512, BF16), A.al(512, BF16)]

            wl(xsb_l[:, 0:4 * SLn].rearrange('p (c t) -> p c t', c=4),
               st_in[:, 0:4, 1 + s0:1 + s1],
               waits=[(s_pe, s_pe.n), (s_ac, s_ac.n)], stream='xspan')
            xv1 = wl(xsb_l[:, 4 * SLn:8 * SLn].rearrange('p (c t) -> p c t', c=4),
                     st_in[:, 4:8, s0:s1], stream='xspan')
            rtv = 0
            for j, nm in enumerate(('rcq', 'rsq', 'rck', 'rsk')):
                rtv = ld(rtab[:, j * SLmax:j * SLmax + SLn], ap[nm][:, s0:s1],
                         waits=[(s_ve, s_ve.n)] if j == 0 else [], stream='rtab')
            LONM = ('rcql', 'rsql', 'rckl', 'rskl')

            def rslice(j, a, b):
                return rtab[:, j * SLmax + a:j * SLmax + b]

            def side(sb_, wname, dest, dcols, r0, rl, tabj):
                tv1 = ve(lambda v: v.tensor_scalar_mul(out=At[:, 0:rl], in0=rslice(tabj[0], r0, r0 + rl),
                                                       scalar1=lvec(i, sb_ // 8)),
                         waits=[rtv])
                lv1 = ld(rlo[:, 0:rl], ap[LONM[tabj[0]]][:, s0 + r0:s0 + r0 + rl],
                         waits=[(s_ve, s_ve.n)], stream='rlo')
                tvx = ve(lambda v: v.scalar_tensor_tensor(
                       out=At[:, 0:rl], in0=rlo[:, 0:rl],
                       scalar=lvec(i, sb_ // 8), in1=At[:, 0:rl],
                       op0=AluOpType.mult, op1=AluOpType.add), waits=[lv1])
                tv2 = ve(lambda v: v.tensor_scalar_mul(out=Bt[:, 0:rl], in0=rslice(tabj[1], r0, r0 + rl),
                                                       scalar1=lvec(i, sb_ // 8)))
                lv2 = ld(rlo[:, 0:rl], ap[LONM[tabj[1]]][:, s0 + r0:s0 + r0 + rl],
                         waits=[(s_ve, s_ve.n)], stream='rlo')
                tv2 = ve(lambda v: v.scalar_tensor_tensor(
                       out=Bt[:, 0:rl], in0=rlo[:, 0:rl],
                       scalar=lvec(i, sb_ // 8), in1=Bt[:, 0:rl],
                       op0=AluOpType.mult, op1=AluOpType.add), waits=[lv2])
                ntl = _tiles(rl, 512)
                Pb = [Bank(pA.t[:, 0:512]), Bank(pA.t[:, 512:1024]),
                      Bank(pB.t[:, 0:512]), Bank(pB.t[:, 512:1024])]
                for bk in Pb:
                    bk.free = pA.free + pB.free
                SSb = [p4, p5, p6][:len(ntl)]
                wpre = []
                for of in range(2):
                    wpre.append(load_w(wname, of))
                pvlast = 0
                for of in range(8):
                    wsl, wlv, wrel = wpre[of]
                    if of + 2 < 8:
                        wpre.append(load_w(wname, of + 2))
                    for tix, (to, tn) in enumerate(ntl):
                        bk = Pb[(of * len(ntl) + tix) % 4]
                        def pmm(t, wsl=wsl, to=to, tn=tn, bk=bk, r0=r0):
                            last = None
                            for ci in range(8):
                                last = t.matmul(bk.t[:, 0:tn], wsl(ci),
                                                xsb_l[:, ci * SLn + r0 + to:ci * SLn + r0 + to + tn],
                                                start=(ci == 0), stop=(ci == 7))
                            return last
                        pv = pe(pmm, waits=[wlv, xv1] + bk.free)
                        a1 = ac(lambda s, of=of, to=to, tn=tn, bk=bk:
                                s.copy(out=raw[:, of * SLn + to:of * SLn + to + tn],
                                       in_=bk.t[:, 0:tn]), waits=[(s_pe, pv)])
                        sl2 = (of * len(ntl) + tix) % 2
                        a2 = ac(lambda s, of=of, to=to, tn=tn, sl2=sl2:
                                s.square(out=sqb[sl2][:, 0:tn],
                                         in_=raw[:, of * SLn + to:of * SLn + to + tn]),
                                waits=[(s_pe, s_pe.n)])
                        bk.consumed(s_ac, a1)
                        def ssm(t, of=of, tn=tn, sl2=sl2, sb_=sb_, tix=tix):
                            return t.matmul(SSb[tix].t[0:16, 0:tn],
                                            c_indss[:, of * 16:of * 16 + 16],
                                            sqb[sl2][:, 0:tn],
                                            start=(of == 0), stop=(of == 7))
                        wt = [(s_ac, a2)] + (SSb[tix].free if of == 0 else [])
                        pvlast = pe(ssm, waits=wt)
                    wrel()
                rnv = 0
                for tix, (to, tn) in enumerate(ntl):
                    roff = (sb_ // 8) * 1024
                    c1 = ac(lambda s, to=to, tn=tn, tix=tix, roff=roff:
                            s.activation(out=rn[0:16, roff + to:roff + to + tn],
                                         in_=SSb[tix].t[0:16, 0:tn],
                                         func=AF.Sqrt), waits=[(s_pe, pvlast)])
                    SSb[tix].consumed(s_ac, c1)
                    c2 = ve(lambda v, to=to, tn=tn, roff=roff:
                            v.tensor_scalar_max(out=rn[0:16, roff + to:roff + to + tn],
                                                in0=rn[0:16, roff + to:roff + to + tn],
                                                scalar1=1e-12),
                            waits=[(s_ac, c1)])
                    ac(lambda s, to=to, tn=tn, roff=roff:
                       s.activation(out=rn[0:16, roff + to:roff + to + tn],
                                    in_=rn[0:16, roff + to:roff + to + tn],
                                    func=AF.Ln), waits=[(s_ve, c2)])
                    rnv = ac(lambda s, to=to, tn=tn, roff=roff:
                             s.activation(out=rn[0:16, roff + to:roff + to + tn],
                                          in_=rn[0:16, roff + to:roff + to + tn],
                                          func=AF.Exp, scale=-1.0))
                Rb2 = [Bank(pA.t[:, 0:512]), Bank(pA.t[:, 512:1024])]
                Pp = [Bank(pB.t[:, 0:512]), Bank(pB.t[:, 512:1024])]
                for bk in Rb2 + Pp:
                    bk.free = [(s_ac, s_ac.n), (s_ve, s_ve.n)]
                for of in range(8):
                    for tix, (to, tn) in enumerate(ntl):
                        k2 = (of * len(ntl) + tix) % 2
                        v1 = ve(lambda v, of=of, to=to, tn=tn, k2=k2:
                                v.tensor_tensor(out=ts1[k2][:, 0:tn],
                                                in0=raw[:, of * SLn + to:of * SLn + to + tn],
                                                in1=At[:, to:to + tn], op=AluOpType.mult),
                                waits=[(s_ac, rnv)] if (of == 0 and tix == 0) else [])
                        v2 = ve(lambda v, of=of, to=to, tn=tn, k2=k2:
                                v.tensor_tensor(out=vbs[k2][:, 0:tn],
                                                in0=raw[:, of * SLn + to:of * SLn + to + tn],
                                                in1=Bt[:, to:to + tn], op=AluOpType.mult),
                                waits=[(s_pe, s_pe.n)])
                        pp = Pp[k2]
                        pv1 = pe(lambda t, tn=tn, k2=k2, pp=pp:
                                 t.matmul(pp.t[:, 0:tn], c_perm[:, :], vbs[k2][:, 0:tn],
                                          start=True, stop=True),
                                 waits=[(s_ve, v2)] + pp.free)
                        v3 = ve(lambda v, tn=tn, k2=k2, pp=pp:
                                v.tensor_tensor(out=ts1[k2][:, 0:tn], in0=ts1[k2][:, 0:tn],
                                                in1=pp.t[:, 0:tn], op=AluOpType.add),
                                waits=[(s_pe, pv1)])
                        pp.consumed(s_ve, v3)
                        rb = Rb2[k2]
                        roff2 = (sb_ // 8) * 1024
                        pv2 = pe(lambda t, of=of, to=to, tn=tn, rb=rb, roff2=roff2:
                                 t.matmul(rb.t[0:128, 0:tn],
                                          c_indbcf[0:16, of * 128:of * 128 + 128],
                                          rn[0:16, roff2 + to:roff2 + to + tn],
                                          start=True, stop=True),
                                 waits=[(s_ac, rnv)] + rb.free)
                        v4 = ve(lambda v, of=of, to=to, tn=tn, k2=k2, rb=rb:
                                v.tensor_tensor(out=dest(of, to, tn), in0=ts1[k2][:, 0:tn],
                                                in1=rb.t[:, 0:tn], op=AluOpType.mult),
                                waits=[(s_pe, pv2)])
                        rb.consumed(s_ve, v4)
                pA.free = [(s_ve, s_ve.n)]
                pB.free = [(s_ve, s_ve.n)]

            side(0, f'wq{i}',
                 lambda of, to, tn: QT[:, of * mn + to:of * mn + to + tn],
                 mn, qoff, mn, (0, 1))
            side(8, f'wk{i}',
                 lambda of, to, tn: KTl[:, of * SLn + to:of * SLn + to + tn],
                 SLn, 0, SLn, (2, 3))

            for tb in range(0, SLn, 128):
                kcr = tb // 128
                kw = min(128, SLn - tb)
                for half in range(2):
                    bk = p4 if half == 0 else p5
                    def vmm(t, tb=tb, kw=kw, half=half, bk=bk):
                        last = None
                        for ci in range(8):
                            last = t.matmul(bk.t[0:kw, 0:512],
                                            xsb_l[:, ci * SLn + tb:ci * SLn + tb + kw],
                                            wvbl[:, ci * D + half * 512:ci * D + half * 512 + 512],
                                            start=(ci == 0), stop=(ci == 7))
                        return last
                    pv = pe(vmm, waits=[wv_rdy, xv1] + bk.free)
                    e1 = ve(lambda v, kcr=kcr, kw=kw, half=half, bk=bk:
                            v.tensor_copy(
                                out=VTl[0:kw, kcr * 1040 + half * 520:kcr * 1040 + half * 520 + 520]
                                    .rearrange('p (h e) -> p h e', h=8)[:, :, 0:64],
                                in_=bk.t[0:kw, 0:512].rearrange('p (h e) -> p h e', h=8)),
                            waits=[(s_pe, pv)])
                    bk.consumed(s_ve, e1)
                ve(lambda v, kcr=kcr, kw=kw:
                   v.memset(VTl[0:kw, kcr * 1040:kcr * 1040 + 1040]
                            .rearrange('p (h e) -> p h e', e=65)[:, :, 64:65], 1.0))
            A.pop()

            A.push()
            prA = A.al(1024, BF16); prB = A.al(1024, BF16)
            rA = A.al(256, F32); rB = A.al(256, F32)
            RA = A.al(256, F32); RB = A.al(256, F32)
            aob = A.al(8 * 1024, BF16)
            stagBb = A.al(8 * 1024, BF16)
            res2 = [A.al(512, F32), A.al(512, F32)]
            yout = [A.al(512, F32), A.al(512, F32)]
            last_y = [[], []]
            qts = _tiles(mn, 256)
            for hc in range(8):
                for (qo, qn) in qts:
                    q0 = m0 + qo
                    kcos = [k for k in range(4) if q0 - 256 + 128 * k >= 0
                            and q0 - 256 + 128 * k < s1]
                    pvsc = 0
                    for kco in kcos:
                        kabs = q0 - 256 + 128 * kco
                        kcr = (kabs - s0) // 128
                        kw = min(128, SLn - kcr * 128)
                        def smm(t, hc=hc, kco=kco, kcr=kcr, kw=kw, qo=qo, qn=qn):
                            t.matmul(pA.t[0:kw, kco * 256:kco * 256 + qn],
                                     KTl[0:64, hc * SLn + kcr * 128:hc * SLn + kcr * 128 + kw],
                                     QT[0:64, hc * mn + qo:hc * mn + qo + qn],
                                     start=True, stop=False)
                            t.matmul(pA.t[0:kw, kco * 256:kco * 256 + qn],
                                     c_i128[0:kw, 0:kw],
                                     c_masks[0:kw, kco * 256:kco * 256 + qn],
                                     start=False, stop=True)
                            t.matmul(pB.t[0:kw, kco * 256:kco * 256 + qn],
                                     KTl[64:128, hc * SLn + kcr * 128:hc * SLn + kcr * 128 + kw],
                                     QT[64:128, hc * mn + qo:hc * mn + qo + qn],
                                     start=True, stop=False, tile_position=(64, 0))
                            return t.matmul(pB.t[0:kw, kco * 256:kco * 256 + qn],
                                            c_i128[0:kw, 0:kw],
                                            c_masks[0:kw, kco * 256:kco * 256 + qn],
                                            start=False, stop=True)
                        wt = (pA.free + pB.free) if kco == kcos[0] else []
                        pvsc = pe(smm, waits=wt)
                    eA = ac(lambda s: s.activation(out=prA[:, 0:1024], in_=pA.t[:, 0:1024],
                                                   func=AF.Exp, scale=8.0),
                            waits=[(s_pe, pvsc)])
                    eB = ac(lambda s: s.activation(out=prB[:, 0:1024], in_=pB.t[:, 0:1024],
                                                   func=AF.Exp, scale=8.0))
                    pA.consumed(s_ac, eA)
                    pB.consumed(s_ac, eB)
                    def avmm(t, kcos=tuple(kcos), q0=q0, qn=qn, s0=s0, hc=hc):
                        last = None
                        for ix, kco in enumerate(kcos):
                            kabs = q0 - 256 + 128 * kco
                            kcr = (kabs - s0) // 128
                            kw = min(128, SLn - kcr * 128)
                            st_ = (ix == 0)
                            sp = (ix == len(kcos) - 1)
                            t.matmul(p4.t[0:65, 0:qn],
                                     VTl[0:kw, kcr * 1040 + 65 * 2 * hc:kcr * 1040 + 65 * 2 * hc + 65],
                                     prA[0:kw, kco * 256:kco * 256 + qn], start=st_, stop=sp)
                            last = t.matmul(p5.t[0:65, 0:qn],
                                            VTl[0:kw, kcr * 1040 + 65 * (2 * hc + 1):kcr * 1040 + 65 * (2 * hc + 1) + 65],
                                            prB[0:kw, kco * 256:kco * 256 + qn], start=st_, stop=sp)
                        return last
                    pvav = pe(avmm, waits=[(s_ac, eB)] + p4.free + p5.free)
                    ac(lambda s, qn=qn: s.activation(out=rA[64:65, 0:qn],
                                                     in_=p4.t[64:65, 0:qn], func=AF.Ln),
                       waits=[(s_pe, pvav)])
                    r1 = ac(lambda s, qn=qn: s.activation(out=rA[64:65, 0:qn],
                                                          in_=rA[64:65, 0:qn],
                                                          func=AF.Exp, scale=-1.0))
                    ac(lambda s, qn=qn: s.activation(out=rB[64:65, 0:qn],
                                                     in_=p5.t[64:65, 0:qn], func=AF.Ln))
                    r2 = ac(lambda s, qn=qn: s.activation(out=rB[64:65, 0:qn],
                                                          in_=rB[64:65, 0:qn],
                                                          func=AF.Exp, scale=-1.0))
                    b1 = pe(lambda t, qn=qn: t.matmul(p6.t[0:64, 0:qn], c_onesf[64:65, 0:64],
                                                      rA[64:65, 0:qn], start=True, stop=True),
                            waits=[(s_ac, r1)] + p6.free)
                    b2 = pe(lambda t, qn=qn: t.matmul(p7.t[0:64, 0:qn], c_onesf[64:65, 0:64],
                                                      rB[64:65, 0:qn], start=True, stop=True),
                            waits=[(s_ac, r2)] + p7.free)
                    a1 = ac(lambda s, qn=qn: s.copy(out=RA[0:64, 0:qn], in_=p6.t[0:64, 0:qn]),
                            waits=[(s_pe, b1)])
                    a2 = ac(lambda s, qn=qn: s.copy(out=RB[0:64, 0:qn], in_=p7.t[0:64, 0:qn]),
                            waits=[(s_pe, b2)])
                    p6.consumed(s_ac, a1)
                    p7.consumed(s_ac, a2)
                    v1 = ve(lambda v, hc=hc, qo=qo, qn=qn: v.tensor_tensor(
                                out=aob[0:64, hc * mn + qo:hc * mn + qo + qn],
                                in0=p4.t[0:64, 0:qn], in1=RA[0:64, 0:qn], op=AluOpType.mult),
                            waits=[(s_ac, a1)])
                    v2 = ve(lambda v, hc=hc, qo=qo, qn=qn: v.tensor_tensor(
                                out=stagBb[0:64, hc * mn + qo:hc * mn + qo + qn],
                                in0=p5.t[0:64, 0:qn], in1=RB[0:64, 0:qn], op=AluOpType.mult),
                            waits=[(s_ac, a2)])
                    p4.consumed(s_ve, v1)
                    p5.consumed(s_ve, v2)
            pk = ld(aob[64:128, 0:8 * mn], stagBb[0:64, 0:8 * mn],
                    waits=[(s_ve, s_ve.n)], stream='pk')
            wpre = [load_w(f'wo{i}', 0), load_w(f'wo{i}', 1)]
            ntm = _tiles(mn, 512)
            for of in range(8):
                wsl, wlv, wrel = wpre[of]
                if of + 2 < 8:
                    wpre.append(load_w(f'wo{i}', of + 2))
                for tix, (to, tn) in enumerate(ntm):
                    bk = p6 if (of * len(ntm) + tix) % 2 == 0 else p7
                    slr = (of * len(ntm) + tix) % 2
                    xr = ld(res2[slr][:, 0:tn], st_in[:, of, 1 + m0 + to:1 + m0 + to + tn],
                            waits=[(s_ve, s_ve.n)], stream=f'res{slr}')
                    def womm(t, wsl=wsl, to=to, tn=tn, bk=bk):
                        last = None
                        for ci in range(8):
                            last = t.matmul(bk.t[:, 0:tn], wsl(ci),
                                            aob[:, ci * mn + to:ci * mn + to + tn],
                                            start=(ci == 0), stop=(ci == 7))
                        return last
                    pv = pe(womm, waits=[wlv, pk] + bk.free)
                    vv = ve(lambda v, tn=tn, bk=bk, slr=slr: v.tensor_tensor(
                                out=yout[slr][:, 0:tn], in0=bk.t[:, 0:tn],
                                in1=res2[slr][:, 0:tn], op=AluOpType.add),
                            waits=[(s_pe, pv), xr] + last_y[slr])
                    bk.consumed(s_ve, vv)
                    last_y[slr] = [sto(st_mid[:, of, 1 + m0 + to:1 + m0 + to + tn],
                                       yout[slr][:, 0:tn],
                                       waits=[(s_ve, vv)], stream=f'sy{slr}')]
                wrel()
            A.pop()
            barrier()

        for (m0_, mn_) in MEGAS:
            emit_mega(m0_, mn_)
        A.pop()

        A.push()
        xsf = A.al(8 * 512, F32)
        sqf = [A.al(512, F32), A.al(512, F32)]
        t1s = [A.al(512, F32), A.al(512, F32)]
        xnb = A.al(8 * 512, BF16)
        gels = [A.al(512, F32), A.al(512, F32)]
        prodb = A.al(W1C * 512, BF16)
        resf = [A.al(512, F32), A.al(512, F32)]
        zf = [A.al(512, F32), A.al(512, F32)]
        last_z = [[], []]
        srow2 = A.al(8 * 512, F32)


        for (o, n) in NT_OV:
            ld(xsf[:, 0:4 * n].rearrange('p (c t) -> p c t', c=4),
               st_mid[:, 0:4, 1 + o:1 + o + n],
               waits=[(s_ve, s_ve.n), (s_ac, s_ac.n)], stream='ffnx')
            xv1 = ld(xsf[:, 4 * n:8 * n].rearrange('p (c t) -> p c t', c=4),
                     st_mid[:, 4:8, o:o + n], stream='ffnx')
            pvs = 0
            for c in range(8):
                slc = c % 2
                sq = ac(lambda s, c=c, n=n, slc=slc: s.square(out=sqf[slc][:, 0:n],
                                                              in_=xsf[:, c * n:c * n + n]),
                        waits=([xv1] if c == 0 else []) + [(s_pe, s_pe.n)])
                def summ(t, c=c, n=n, slc=slc):
                    t.matmul(p4.t[0:1, 0:n], c_onesf[:, 0:1], xsf[:, c * n:c * n + n],
                             start=(c == 0), stop=(c == 7))
                    return t.matmul(p5.t[0:1, 0:n], c_onesf[:, 0:1], sqf[slc][:, 0:n],
                                    start=(c == 0), stop=(c == 7))
                wt = [(s_ac, sq)] + ((p4.free + p5.free) if c == 0 else [])
                pvs = pe(summ, waits=wt)
            on('vector', lambda v, pvs=pvs: v.wait_ge(s_pe.h, pvs))
            bc = ln_stats_chain(lambda j, n=n: srow2[0:1, j * 512:j * 512 + n], n)
            lnv = 0
            for c in range(8):
                slc = c % 2
                va = ve(lambda v, c=c, n=n, slc=slc: v.tensor_tensor(
                            out=t1s[slc][:, 0:n], in0=xsf[:, c * n:c * n + n],
                            in1=p6.t[:, 0:n], op=AluOpType.mult),
                        waits=[(s_pe, bc)] if c == 0 else [])
                vb = ve(lambda v, c=c, n=n, slc=slc: v.tensor_tensor(
                            out=t1s[slc][:, 0:n], in0=t1s[slc][:, 0:n],
                            in1=p7.t[:, 0:n], op=AluOpType.add))
                lnv = ve(lambda v, c=c, n=n, slc=slc: v.tensor_scalar(
                            out=xnb[:, c * n:c * n + n], in0=t1s[slc][:, 0:n],
                            scalar1=lvec(i, 2 + c), scalar2=lvec(i, 10 + c),
                            op0=AluOpType.mult, op1=AluOpType.add))
            p6.consumed(s_ve, lnv)
            p7.consumed(s_ve, lnv)
            wpre = [load_w(f'w1g{i}', 0), load_w(f'w1a{i}', 0),
                    load_w(f'w1g{i}', 1), load_w(f'w1a{i}', 1)]
            for af in range(W1C):
                wg, wglv, wgrel = wpre[2 * af]
                wa, walv, warel = wpre[2 * af + 1]
                if af + 2 < W1C:
                    wpre.append(load_w(f'w1g{i}', af + 2))
                    wpre.append(load_w(f'w1a{i}', af + 2))
                bkg = Bank(pA.t[:, 0:512]) if af % 2 == 0 else Bank(pB.t[:, 0:512])
                bka = Bank(pA.t[:, 512:1024]) if af % 2 == 0 else Bank(pB.t[:, 512:1024])
                src = pA if af % 2 == 0 else pB
                bkg.free = src.free
                bka.free = src.free
                def w1mm(t, wg=wg, wa=wa, n=n, bkg=bkg, bka=bka):
                    for ci in range(8):
                        t.matmul(bkg.t[:, 0:n], wg(ci), xnb[:, ci * n:ci * n + n],
                                 start=(ci == 0), stop=(ci == 7))
                    last = None
                    for ci in range(8):
                        last = t.matmul(bka.t[:, 0:n], wa(ci), xnb[:, ci * n:ci * n + n],
                                        start=(ci == 0), stop=(ci == 7))
                    return last
                pv = pe(w1mm, waits=[wglv, walv, (s_ve, lnv)] + bkg.free)
                slg = af % 2
                ga = ac(lambda s, af=af, n=n, slg=slg, bkg=bkg: s.activation(
                            out=gels[slg][:, 0:n], in_=bkg.t[:, 0:n], func=AF.Gelu,
                            bias=lvec(i, 40 + af)),
                        waits=[(s_pe, pv), (s_ve, s_ve.n)])
                pr = ve(lambda v, af=af, n=n, slg=slg, bka=bka: v.scalar_tensor_tensor(
                            out=prodb[:, af * n:af * n + n], in0=bka.t[:, 0:n],
                            scalar=lvec(i, 18 + af), in1=gels[slg][:, 0:n],
                            op0=AluOpType.add, op1=AluOpType.mult),
                        waits=[(s_ac, ga)])
                src.consumed(s_ve, pr)
                src.also_consumed(s_ac, ga)
                wgrel(); warel()
            w2pre = [load_w(f'w2{i}', 0, ni=8, sub=0), load_w(f'w2{i}', 0, ni=8, sub=8),
                     load_w(f'w2{i}', 0, ni=6, sub=16)]
            for of in range(8):
                if of + 1 < 8:
                    w2pre.append(load_w(f'w2{i}', of + 1, ni=8, sub=0))
                    w2pre.append(load_w(f'w2{i}', of + 1, ni=8, sub=8))
                    w2pre.append(load_w(f'w2{i}', of + 1, ni=6, sub=16))
                g0, g1, g2 = w2pre[3 * of], w2pre[3 * of + 1], w2pre[3 * of + 2]
                bk = p6 if of % 2 == 0 else p7
                slr = of % 2
                xr = ld(resf[slr][:, 0:n], st_mid[:, of, 1 + o:1 + o + n],
                        waits=[(s_ve, s_ve.n)], stream=f'fr{slr}')
                def w2mm(t, g0=g0, g1=g1, g2=g2, n=n, bk=bk):
                    last = None
                    for af in range(W1C):
                        grp, k = (g0, af) if af < 8 else ((g1, af - 8) if af < 16 else (g2, af - 16))
                        last = t.matmul(bk.t[:, 0:n], grp[0](k),
                                        prodb[:, af * n:af * n + n],
                                        start=(af == 0), stop=(af == W1C - 1))
                    return last
                pv = pe(w2mm, waits=[g0[1], g1[1], g2[1],
                                     (s_ve, s_ve.n)] + bk.free)
                vv = ve(lambda v, of=of, n=n, bk=bk, slr=slr: v.scalar_tensor_tensor(
                            out=zf[slr][:, 0:n], in0=bk.t[:, 0:n], scalar=lvec(i, 62 + of),
                            in1=resf[slr][:, 0:n], op0=AluOpType.add, op1=AluOpType.add),
                        waits=[(s_pe, pv), xr] + last_z[slr])
                bk.consumed(s_ve, vv)
                last_z[slr] = [sto(st_in[:, of, 1 + o:1 + o + n], zf[slr][:, 0:n],
                                   waits=[(s_ve, vv)], stream=f'sz{slr}')]
                g0[2](); g1[2](); g2[2]()
            barrier()
        A.pop()

    def emit_final():
        A.push()
        tb = A.al(8 * 512, F32)
        x1b = A.al(8 * 512, F32)
        sqf = [A.al(512, F32), A.al(512, F32)]
        srow3 = A.al(8 * 512, F32)
        outb = A.al(8 * 512, F32)


        for (o, n) in NT_OV:
            zv = ld(tb[:, 0:8 * n].rearrange('p (c t) -> p c t', c=8),
                    stv[0][:, :, 1 + o:1 + o + n], waits=[(s_ve, s_ve.n)], stream='fz')
            xv = ld(x1b[:, 0:8 * n].rearrange('p (c t) -> p c t', c=8),
                    x1kv[:, :, o:o + n], stream='fx')
            pvs = 0
            for c in range(8):
                slc = c % 2
                va = ve(lambda v, c=c, n=n: v.tensor_tensor(
                            out=tb[:, c * n:c * n + n], in0=tb[:, c * n:c * n + n],
                            in1=x1b[:, c * n:c * n + n], op=AluOpType.add),
                        waits=[zv, xv] if c == 0 else [])
                sq = ac(lambda s, c=c, n=n, slc=slc: s.square(out=sqf[slc][:, 0:n],
                                                              in_=tb[:, c * n:c * n + n]),
                        waits=[(s_ve, va), (s_pe, s_pe.n)])
                def summ(t, c=c, n=n, slc=slc):
                    t.matmul(p4.t[0:1, 0:n], c_onesf[:, 0:1], tb[:, c * n:c * n + n],
                             start=(c == 0), stop=(c == 7))
                    return t.matmul(p5.t[0:1, 0:n], c_onesf[:, 0:1], sqf[slc][:, 0:n],
                                    start=(c == 0), stop=(c == 7))
                wt = [(s_ac, sq)] + ((p4.free + p5.free) if c == 0 else [])
                pvs = pe(summ, waits=wt)
            on('vector', lambda v, pvs=pvs: v.wait_ge(s_pe.h, pvs))
            bc = ln_stats_chain(lambda j, n=n: srow3[0:1, j * 512:j * 512 + n], n)
            lastv = 0
            for c in range(8):
                ve(lambda v, c=c, n=n: v.tensor_tensor(
                        out=outb[:, c * n:c * n + n], in0=tb[:, c * n:c * n + n],
                        in1=p6.t[:, 0:n], op=AluOpType.mult),
                   waits=[(s_pe, bc)] if c == 0 else [])
                lastv = ve(lambda v, c=c, n=n: v.tensor_tensor(
                        out=outb[:, c * n:c * n + n], in0=outb[:, c * n:c * n + n],
                        in1=p7.t[:, 0:n], op=AluOpType.add))
            p6.consumed(s_ve, lastv)
            p7.consumed(s_ve, lastv)
            sto(outv[:, :, o:o + n],
                outb[:, 0:8 * n].rearrange('p (c t) -> p c t', c=8),
                waits=[(s_ve, lastv)])
            barrier()
        A.pop()

    if STAGE == 'full':
        for i in range(NLAYERS):
            emit_layer(i)
        emit_final()

    return _finish(nc, q, ALL, w, on, stack)


def _finish(nc, q, ALL, w, on, stack):
    final = tuple((c, c.n) for c in ALL)
    for e in q:
        def f(eng, final=final):
            w(eng, final)
        on(e, f)
    with nc.Block() as block:
        @block.sync
        def _(sy):
            for f in q['sync']:
                f(sy)
        @block.vector
        def _(v):
            for f in q['vector']:
                f(v)
        @block.scalar
        def _(s):
            for f in q['scalar']:
                f(s)
        @block.tensor
        def _(t):
            for f in q['tensor']:
                f(t)
        @block.gpsimd
        def _(g):
            for f in q['gpsimd']:
                f(g)
    stack.close()
    return nc


_CACHE = {}


def kernel(**inputs):
    in_maps = prep_inputs(inputs)
    if 'nc' not in _CACHE:
        _CACHE['nc'] = build_nc()
    res = run_bass_kernel_spmd(_CACHE['nc'], in_maps, core_ids=list(range(8)))
    full = np.zeros((B_, S, D), np.float32)
    for c in range(8):
        o = res.results[c]['out']
        b = c // 2
        if c % 2 == 0:
            full[b, 0:SPLIT, :] = o[:, 0:SPLIT].T
        else:
            full[b, SPLIT:S, :] = o[:, SPLIT - (S - FT):FT].T
    return full


# version 08 (final)
# speedup vs baseline: 1.0818x; 1.0461x over previous
import sys
sys.path.insert(0, '/opt/trn_rl_repo')

import numpy as np
import concourse.bass as bass
import concourse.mybir as mybir
from concourse.alu_op_type import AluOpType
from concourse.bass_utils import run_bass_kernel_spmd

F32 = mybir.dt.float32
BF16 = mybir.dt.bfloat16
I8 = mybir.dt.int8
AF = mybir.ActivationFunctionType

B_ = 4
S = 4096
D = 1024
H = 16
FT = 2848
SPLIT = 2816
DEPTH = 6
W1OF = 2816
W1C = 22
NEG = -1.0e30
STAGE = 'full'
NLAYERS = DEPTH


def _tiles(total, step, start=0):
    o, out = start, []
    while o < total:
        out.append((o, min(step, total - o)))
        o += step
    return out


NT_FT = _tiles(FT, 512)
NT_S = _tiles(S, 512)
MEGAS = [(0, 1024), (1024, 1024), (1824, 1024)]
NT_OV = [(0, 512), (512, 512), (1024, 512), (1536, 512), (2048, 512), (2336, 512)]


def _tile_lhsT(W):
    K, N = W.shape
    Kc, Nc = (K + 127) // 128, (N + 127) // 128
    Wp = np.zeros((Kc * 128, Nc * 128), np.float32)
    Wp[:K, :N] = W
    return np.ascontiguousarray(
        Wp.reshape(Kc, 128, Nc, 128).transpose(1, 2, 0, 3), np.float32)


def _tile_rhs(W):
    K, N = W.shape
    Kc = (K + 127) // 128
    Wp = np.zeros((Kc * 128, N), np.float32)
    Wp[:K] = W
    return np.ascontiguousarray(Wp.reshape(Kc, 128, N).transpose(1, 0, 2), np.float32)


def _rotary_tables():
    HD = 64
    half32 = np.arange(0, HD, 2, dtype=np.float32)
    inv_freq = (np.float32(1.0) / np.float32(10000.0) ** (half32 / np.float32(HD)))
    fr32 = (np.arange(S, dtype=np.float32)[:, None] * inv_freq[None, :])
    freqs = np.concatenate([fr32.astype(np.float64)] * 2, -1)
    bs32 = ((half32 + np.float32(0.4 * HD)) / np.float32(1.4 * HD))
    power = ((np.arange(S, dtype=np.float32) - np.float32(S // 2)) / np.float32(128.0))
    sc32 = (bs32[None, :].astype(np.float64) ** power[:, None].astype(np.float64)
            ).astype(np.float32)
    scale = np.concatenate([sc32.astype(np.float64)] * 2, -1)
    tau = np.where(np.arange(HD) < 32, 1.0, -1.0)[None, :]
    cs_q = (np.cos(freqs) * scale).T
    sn_q = (np.sin(freqs) * scale * tau).T
    cs_k = (np.cos(freqs) / scale).T
    sn_k = (np.sin(freqs) / scale * tau).T
    def st(a):
        full = np.concatenate([a, a], 0)
        hi = full.astype(np.float32)
        lo = (full - hi.astype(np.float64)).astype(np.float32)
        return np.ascontiguousarray(hi), np.ascontiguousarray(lo)
    return st(cs_q), st(sn_q), st(cs_k), st(sn_k)


def _alibi_toep(F0):
    base = (2.0 ** 8) ** (1.0 / H)
    m64 = np.asarray([1.0 / base ** (i + 1) for i in range(H)], np.float64)
    m32 = m64.astype(np.float32)
    d32 = np.arange(8192, dtype=np.float32) - np.float32(4095.0 + F0)
    d64 = np.arange(8192, dtype=np.float64) - (4095.0 + F0)
    keep = (d64 >= -4095.5) & (d64 <= 4095.5)
    T = np.ones((H, 8192), np.float32)
    for h in range(H):
        b32 = (m32[h] * d32).astype(np.float32).astype(np.float64)
        T[h, keep] = np.exp(b32 - m64[h] * d64).astype(np.float32)[keep]
    return np.ascontiguousarray(T)


def _alibi_g():
    base = (2.0 ** 8) ** (1.0 / H)
    m = np.asarray([1.0 / base ** (i + 1) for i in range(H)], np.float64)
    j = np.arange(S, dtype=np.float64)
    g = np.exp(m[None, :] * (j[:, None] - (S - 1))).astype(np.float32)
    return np.ascontiguousarray(g.reshape(S // 128, 128, H).transpose(1, 0, 2)
                                .reshape(128, (S // 128) * H))


def _band_masks():
    ms = np.zeros((128, 4, 256), np.float32)
    kk = np.arange(128)[:, None]
    ii = np.arange(256)[None, :]
    for kco in range(4):
        jj = 128 * kco + kk
        ms[:, kco, :] = np.where((jj >= ii) & (jj <= ii + 256), 0.0, NEG)
    return np.ascontiguousarray(ms.reshape(128, 1024))


def _perm_rot():
    P = np.zeros((128, 128), np.float32)
    for c in range(128):
        P[(c - 32 if (c % 64) >= 32 else c + 32), c] = 1.0
    return P


def _indicators():
    ind_ss = np.zeros((128, 8, 16), np.float32)
    ind_bc = np.zeros((16, 8, 128), np.float32)
    for sx in range(8):
        for p in range(128):
            r = 2 * sx + (p // 64)
            ind_ss[p, sx, r] = 1.0
            ind_bc[r, sx, p] = 1.0
    return (np.ascontiguousarray(ind_ss.reshape(128, 128)),
            np.ascontiguousarray(ind_bc.reshape(16, 1024)))


def prep_inputs(inputs):
    x = np.asarray(inputs['L0_attn_x'], np.float32)
    Wkqv = np.asarray(inputs['L0_attn_W_kqv'], np.float32)
    iWqkv = np.asarray(inputs['L0_ffn_Wqkv'], np.float32)
    qs = np.asarray(inputs['L0_ffn_q_scale'], np.float32)
    ks = np.asarray(inputs['L0_ffn_k_scale'], np.float32)
    Wo = np.asarray(inputs['L0_ffn_Wo'], np.float32)
    lng = np.asarray(inputs['L0_ffn_ln_g'], np.float32)
    lnb = np.asarray(inputs['L0_ffn_ln_b'], np.float32)
    W1 = np.asarray(inputs['L0_ffn_W1'], np.float32)
    b1 = np.asarray(inputs['L0_ffn_b1'], np.float32)
    W2 = np.asarray(inputs['L0_ffn_W2'], np.float32)
    b2 = np.asarray(inputs['L0_ffn_b2'], np.float32)

    sh = {}
    sh['o_wk'] = _tile_lhsT(Wkqv[:, 0:D])
    sh['o_wq'] = _tile_lhsT(Wkqv[:, D:2 * D])
    sh['o_wv'] = _tile_rhs(Wkqv[:, 2 * D:3 * D])
    for i in range(DEPTH):
        sh[f'wq{i}'] = _tile_lhsT(iWqkv[i][:, 0:D])
        sh[f'wk{i}'] = _tile_lhsT(iWqkv[i][:, D:2 * D])
        sh[f'wv{i}'] = _tile_rhs(iWqkv[i][:, 2 * D:3 * D])
        sh[f'wo{i}'] = _tile_lhsT(Wo[i])
        W1a = np.zeros((D, W1OF), np.float32); W1a[:, :2730] = W1[i][:, :2730]
        W1g = np.zeros((D, W1OF), np.float32); W1g[:, :2730] = W1[i][:, 2730:]
        sh[f'w1a{i}'] = _tile_lhsT(W1a)
        sh[f'w1g{i}'] = _tile_lhsT(W1g)
        W2p = np.zeros((W1OF, D), np.float32); W2p[:2730] = W2[i]
        sh[f'w2{i}'] = _tile_lhsT(W2p)

    vecs = np.zeros((128, DEPTH, 70), np.float32)
    for i in range(DEPTH):
        vecs[:, i, 0] = np.tile(qs[i], 2)
        vecs[:, i, 1] = np.tile(ks[i], 2)
        for c in range(8):
            vecs[:, i, 2 + c] = lng[i][128 * c:128 * (c + 1)]
            vecs[:, i, 10 + c] = lnb[i][128 * c:128 * (c + 1)]
            vecs[:, i, 62 + c] = b2[i][128 * c:128 * (c + 1)]
        b1a = np.zeros(W1OF, np.float32); b1a[:2730] = b1[i][:2730]
        b1g = np.zeros(W1OF, np.float32); b1g[:2730] = b1[i][2730:]
        for c in range(W1C):
            vecs[:, i, 18 + c] = b1a[128 * c:128 * (c + 1)]
            vecs[:, i, 40 + c] = b1g[128 * c:128 * (c + 1)]
    sh['lvecs'] = np.ascontiguousarray(vecs.reshape(128, DEPTH * 70))
    sh['galibi'] = _alibi_g()
    sh['masks'] = _band_masks()
    sh['perm'] = _perm_rot()
    sh['ind_ss'], sh['ind_bc'] = _indicators()
    sh['i128'] = np.eye(128, dtype=np.float32)

    (cs_q, cs_ql), (sn_q, sn_ql), (cs_k, cs_kl), (sn_k, sn_kl) = _rotary_tables()
    in_maps = []
    for c in range(8):
        b = c // 2
        F0 = 0 if c % 2 == 0 else S - FT
        d = dict(sh)
        d['toep'] = _alibi_toep(F0)
        d['xT'] = np.ascontiguousarray(x[b].T)
        xtf = np.zeros((D, 3072), np.float32)
        xtf[:, 0:FT] = x[b].T[:, F0:F0 + FT]
        d['xTf'] = xtf
        d['rcq'] = np.ascontiguousarray(cs_q[:, F0:F0 + FT])
        d['rsq'] = np.ascontiguousarray(sn_q[:, F0:F0 + FT])
        d['rck'] = np.ascontiguousarray(cs_k[:, F0:F0 + FT])
        d['rsk'] = np.ascontiguousarray(sn_k[:, F0:F0 + FT])
        d['rcql'] = np.ascontiguousarray(cs_ql[:, F0:F0 + FT])
        d['rsql'] = np.ascontiguousarray(sn_ql[:, F0:F0 + FT])
        d['rckl'] = np.ascontiguousarray(cs_kl[:, F0:F0 + FT])
        d['rskl'] = np.ascontiguousarray(sn_kl[:, F0:F0 + FT])
        in_maps.append(d)
    return in_maps



class Ctr:
    def __init__(self, h):
        self.h, self.n = h, 0


class Bank:
    def __init__(self, t):
        self.t = t
        self.free = []

    def consumed(self, ctr, val):
        self.free = [(ctr, val)]

    def also_consumed(self, ctr, val):
        self.free.append((ctr, val))


def build_nc():
    nc = bass.Bass("TRN2", target_bir_lowering=False, debug=False,
                   detect_race_conditions=False)
    from contextlib import ExitStack
    stack = ExitStack()

    ap = {}
    def din(name, shape, dt=F32):
        ap[name] = nc.dram_tensor(name, list(shape), dt, kind="ExternalInput").ap()

    din('xT', (D, S)); din('xTf', (D, 3072))
    din('o_wk', (128, 8, 8, 128)); din('o_wq', (128, 8, 8, 128)); din('o_wv', (128, 8, D))
    for i in range(DEPTH):
        din(f'wq{i}', (128, 8, 8, 128)); din(f'wk{i}', (128, 8, 8, 128))
        din(f'wv{i}', (128, 8, D)); din(f'wo{i}', (128, 8, 8, 128))
        din(f'w1a{i}', (128, W1C, 8, 128)); din(f'w1g{i}', (128, W1C, 8, 128))
        din(f'w2{i}', (128, 8, W1C, 128))
    din('lvecs', (128, DEPTH * 70)); din('toep', (H, 8192))
    din('galibi', (128, (S // 128) * H))
    din('masks', (128, 1024)); din('perm', (128, 128)); din('i128', (128, 128))
    din('ind_ss', (128, 128)); din('ind_bc', (16, 1024))
    din('rcq', (128, FT)); din('rsq', (128, FT)); din('rck', (128, FT)); din('rsk', (128, FT))
    din('rcql', (128, FT)); din('rsql', (128, FT)); din('rckl', (128, FT)); din('rskl', (128, FT))

    out = nc.dram_tensor('out', [D, FT], F32, kind="ExternalOutput").ap()
    st_t = [nc.dram_tensor('st_a', [D, 1 + FT], F32).ap(),
            nc.dram_tensor('st_b', [D, 1 + FT], F32).ap()]
    x1k = nc.dram_tensor('x1k', [D, FT], F32).ap()

    stv = [t.rearrange('(c p) t -> p c t', p=128) for t in st_t]
    x1kv = x1k.rearrange('(c p) t -> p c t', p=128)
    outv = out.rearrange('(c p) t -> p c t', p=128)
    xTv = ap['xT'].rearrange('(c p) t -> p c t', p=128)
    xTfv = ap['xTf'].rearrange('(c p) t -> p c t', p=128)

    q = {e: [] for e in ('sync', 'vector', 'scalar', 'tensor', 'gpsimd')}
    def on(e, fn):
        q[e].append(fn)

    def sem(name):
        return Ctr(stack.enter_context(nc.semaphore(name)))
    NQS = 8
    ldp = [sem(f's_ld{j}') for j in range(NQS)]
    wlp = [sem(f's_wl{j}') for j in range(NQS)]
    stp = [sem(f's_st{j}') for j in range(NQS)]
    qrot = {'sync': [ldp, 0], 'gpsimd': [wlp, 0], 'scalar': [stp, 0]}
    nmp = [sem(f's_nm{j}') for j in range(40)]
    nmmap = {}
    def named(name):
        if name not in nmmap:
            nmmap[name] = nmp[len(nmmap)]
        return nmmap[name]
    s_pe = sem('s_pe'); s_ve = sem('s_ve'); s_ac = sem('s_ac')
    ALL = ldp + wlp + stp + nmp + [s_pe, s_ve, s_ac]
    def w(eng, pairs):
        for (ctr, n) in pairs:
            if n > 0:
                eng.wait_ge(ctr.h, n)

    def dma(queue, _ctr, dst, src, waits=(), slow=False, stream=None):
        if stream is None:
            pool, idx = qrot[queue]
            qrot[queue][1] = (idx + 1) % NQS
            ctr = pool[idx]
        else:
            ctr = named(stream)
        ctr.n += 16
        def f(eng, dst=dst, src=src, waits=tuple(waits), slow=slow, h=ctr.h):
            w(eng, waits)
            eng.dma_start(out=dst, in_=src,
                          allow_slow_non_contiguous=slow).then_inc(h, 16)
        on(queue, f)
        return (ctr, ctr.n)

    def ld(dst, src, waits=(), stream=None):
        return dma('sync', None, dst, src, waits, stream=stream)

    def wl(dst, src, waits=(), stream=None):
        return dma('gpsimd', None, dst, src, waits, stream=stream)

    def sto(dst, src, waits=(), stream=None):
        return dma('scalar', None, dst, src, waits, stream=stream)

    def _eng_emit(queue, ctr, fn, waits):
        ctr.n += 1
        def f(e, fn=fn, waits=tuple(waits), h=ctr.h):
            w(e, waits)
            fn(e).then_inc(h, 1)
        on(queue, f)
        return ctr.n

    def ve(fn, waits=()):
        return _eng_emit('vector', s_ve, fn, waits)

    def ac(fn, waits=()):
        return _eng_emit('scalar', s_ac, fn, waits)

    def pe(fn, waits=()):
        return _eng_emit('tensor', s_pe, fn, waits)

    def barrier():
        vals = tuple((c, c.n) for c in ALL)
        for e in q:
            def f(eng, vals=vals):
                w(eng, vals)
            on(e, f)

    def sb(name, shape, dt):
        return stack.enter_context(nc.sbuf_tensor(name, list(shape), dt))

    c_masks = sb('c_masks', (128, 1024), BF16)
    c_perm = sb('c_perm', (128, 128), BF16)
    c_i128 = sb('c_i128', (128, 128), BF16)
    c_indss = sb('c_indss', (128, 128), BF16)
    c_indbcf = sb('c_indbcf', (16, 1024), BF16)
    c_onesf = sb('c_onesf', (128, 128), F32)
    c_onesb = sb('c_onesb', (128, 128), BF16)
    c_eps = sb('c_eps', (128, 1), F32)
    c_lv = sb('c_lv', (128, DEPTH * 70), F32)
    c_g = sb('c_g', (128, (S // 128) * H), F32)

    _ab = (int(nc.sbuf_base) + 63) // 64 * 64 + 64
    ARENA = (int(nc.sbuf_top) - _ab) // 64 * 64

    class Arena:
        def __init__(self):
            self.off = 0
            self.marks = []
            self.base = _ab
        def al(self, cols, dt):
            esz = 2 if dt == BF16 else 4
            a = (self.off + 63) // 64 * 64
            self.off = a + cols * esz
            assert self.off <= ARENA, f"arena overflow {self.off} > {ARENA}"
            return nc.alloc_sbuf_tensor_at('ar', (128, cols), dt,
                                           offset=self.base + a)
        def push(self):
            self.marks.append(self.off)
        def pop(self):
            self.off = self.marks.pop()
    A = Arena()

    def psum(name, cols):
        return stack.enter_context(nc.psum_tensor(name, [128, cols], F32))
    pA = Bank(psum('pA', 1024))
    pB = Bank(psum('pB', 1024))
    p4 = Bank(psum('p4', 512)); p5 = Bank(psum('p5', 512))
    p6 = Bank(psum('p6', 512)); p7 = Bank(psum('p7', 512))

    def lvec(i, col):
        return c_lv[:, i * 70 + col:i * 70 + col + 1]

    wl(c_masks[:, :], ap['masks'])
    wl(c_perm[:, :], ap['perm'])
    wl(c_i128[:, :], ap['i128'])
    wl(c_indss[:, :], ap['ind_ss'])
    wl(c_indbcf[:, :], ap['ind_bc'])
    ld(c_lv[:, :], ap['lvecs'])
    ld(c_g[:, :], ap['galibi'])
    ve(lambda v: v.memset(c_onesf[:, :], 1.0))
    ve(lambda v: v.memset(c_onesb[:, :], 1.0))
    ve(lambda v: v.memset(c_eps[:, :], 1e-5))
    zbuf = A.al(8, F32)
    vz = ve(lambda v: v.memset(zbuf[:, :], 0.0))
    for t in range(2):
        dma('scalar', None, stv[t][:, :, 0:1], zbuf[:, :, None],
            waits=[(s_ve, vz)], slow=True)
    barrier()

    if STAGE == 'consts':
        return _finish(nc, q, ALL, w, on, stack)

    NSLOT = 6
    wslots = []
    wslot_free = [[] for _ in range(NSLOT)]
    wst = {'i': 0}

    def load_w(name, of, ni=8, sub=0):
        a = ap[name]
        si = wst['i']; wst['i'] = (si + 1) % NSLOT
        src = a[:, of, sub:sub + ni, :].rearrange('p a b -> p (a b)')
        lv = wl(wslots[si][:, 0:ni * 128], src, waits=wslot_free[si],
                stream=f'ws{si}')
        wslot_free[si] = []
        def sl(k, si=si):
            return wslots[si][:, 128 * k:128 * (k + 1)]
        def rel(si=si, ctr=s_pe):
            wslot_free[si] = [(ctr, ctr.n)]
        return sl, lv, rel

    def load_w_rhs(name, ci):
        a = ap[name]
        si = wst['i']; wst['i'] = (si + 1) % NSLOT
        lv = wl(wslots[si][:, 0:D], a[:, ci, :], waits=wslot_free[si],
                stream=f'ws{si}')
        wslot_free[si] = []
        def sl(c0, c1, si=si):
            return wslots[si][:, c0:c1]
        def rel(si=si):
            wslot_free[si] = [(s_pe, s_pe.n)]
        return sl, lv, rel

    def ln_stats_chain(srows_fn, nn):
        m1 = ve(lambda v: v.tensor_scalar_mul(out=srows_fn(0), in0=p4.t[0:1, 0:nn],
                                              scalar1=1.0 / D))
        m2 = ve(lambda v: v.tensor_scalar_mul(out=srows_fn(1), in0=p5.t[0:1, 0:nn],
                                              scalar1=1.0 / D))
        p4.consumed(s_ve, m1)
        p5.consumed(s_ve, m2)
        m3 = ac(lambda s: s.square(out=srows_fn(2), in_=srows_fn(0)), waits=[(s_ve, m2)])
        m4 = ve(lambda v: v.tensor_tensor(out=srows_fn(3), in0=srows_fn(1), in1=srows_fn(2),
                                          op=AluOpType.subtract), waits=[(s_ac, m3)])
        m5 = ac(lambda s: s.activation(out=srows_fn(4), in_=srows_fn(3), func=AF.Sqrt,
                                       bias=c_eps[0:1, 0:1]), waits=[(s_ve, m4)])
        ac(lambda s: s.activation(out=srows_fn(5), in_=srows_fn(4), func=AF.Ln))
        m6 = ac(lambda s: s.activation(out=srows_fn(5), in_=srows_fn(5), func=AF.Exp,
                                       scale=-1.0))
        m7 = ve(lambda v: v.tensor_tensor(out=srows_fn(6), in0=srows_fn(0), in1=srows_fn(5),
                                          op=AluOpType.mult), waits=[(s_ac, m6)])
        m8 = ve(lambda v: v.tensor_scalar_mul(out=srows_fn(7), in0=srows_fn(6), scalar1=-1.0))
        b1 = pe(lambda t: t.matmul(p6.t[:, 0:nn], c_onesf[0:1, :], srows_fn(5),
                                   start=True, stop=True), waits=[(s_ve, m8)] + p6.free)
        b2 = pe(lambda t: t.matmul(p7.t[:, 0:nn], c_onesf[0:1, :], srows_fn(7),
                                   start=True, stop=True), waits=p7.free)
        return b2


    ktd = nc.dram_tensor('ktd', [8, 128, S], BF16).ap()
    vtd = nc.dram_tensor('vtd', [32, 128, 1040], BF16).ap()
    tdd = nc.dram_tensor('tdd', [H, 128, 7040], BF16).ap()

    A.push()
    wkb = A.al(8192, BF16)
    A.push()
    wvb = A.al(8192, BF16)
    xsb = [A.al(4096, BF16), A.al(4096, BF16)]
    xsb_free = [[], []]
    kstg = [A.al(512, BF16), A.al(512, BF16)]
    kstg_free = [[], []]
    vstg_free = [[], []]
    stagB_free = []
    vstg = [A.al(1040, BF16), A.al(1040, BF16)]

    for h in range(H):
        wl(tdd[h], bass.AP(tensor=ap['toep'].tensor, offset=h * 8192 + 1024,
                           ap=[[1, 128], [1, 7040]]), stream='td')

    for of in range(8):
        wk_lv = wl(wkb[:, 1024 * of:1024 * (of + 1)],
                   ap['o_wk'][:, of, :, :].rearrange('p a b -> p (a b)'),
                   stream='wk')
    for ci in range(8):
        wv_lv = wl(wvb[:, 1024 * ci:1024 * (ci + 1)], ap['o_wv'][:, ci, :],
                   stream='wv')

    for nt_i, (o, n) in enumerate(NT_S):
        sl_ = nt_i % 2
        xv = wl(xsb[sl_][:, 0:8 * n].rearrange('p (c t) -> p c t', c=8),
                xTv[:, :, o:o + n],
                waits=xsb_free[sl_] + [wk_lv], stream=f'xs{sl_}')
        for of in range(8):
            bank = pA if of % 2 == 0 else pB
            ks_ = kstg[of % 2]
            def kmm(t, sl_=sl_, n=n, of=of, bank=bank):
                last = None
                for ci in range(8):
                    last = t.matmul(bank.t[:, 0:n],
                                    wkb[:, of * 1024 + ci * 128:of * 1024 + (ci + 1) * 128],
                                    xsb[sl_][:, ci * n:ci * n + n],
                                    start=(ci == 0), stop=(ci == 7))
                return last
            pv = pe(kmm, waits=[xv] + bank.free)
            av = ac(lambda s, n=n, bank=bank, ks_=ks_: s.copy(out=ks_[:, 0:n],
                                                              in_=bank.t[:, 0:n]),
                    waits=[(s_pe, pv)] + kstg_free[of % 2])
            bank.consumed(s_ac, av)
            kstg_free[of % 2] = []
            kv = sto(ktd[of][:, o:o + n], ks_[:, 0:n], waits=[(s_ac, av)],
                     stream=f'kst{of % 2}')
            kstg_free[of % 2] = [kv]
        for tb in range(0, n, 128):
            kc = (o + tb) // 128
            vs_ = vstg[kc % 2]
            ev2 = 0
            for half in range(2):
                bank = p4 if half == 0 else p5
                def vmm(t, sl_=sl_, n=n, tb=tb, half=half, bank=bank):
                    last = None
                    for ci in range(8):
                        last = t.matmul(bank.t[:, 0:512],
                                        xsb[sl_][:, ci * n + tb:ci * n + tb + 128],
                                        wvb[:, ci * 1024 + half * 512:ci * 1024 + half * 512 + 512],
                                        start=(ci == 0), stop=(ci == 7))
                    return last
                pv = pe(vmm, waits=[xv, wv_lv] + bank.free)
                gs = c_g[:, kc * H + half * 8:kc * H + half * 8 + 8]
                e1 = ve(lambda v, half=half, bank=bank, vs_=vs_, gs=gs:
                        v.tensor_tensor(
                            out=vs_[:, half * 520:half * 520 + 520]
                                .rearrange('p (h e) -> p h e', h=8)[:, :, 0:64],
                            in0=bank.t[:, 0:512].rearrange('p (h e) -> p h e', h=8),
                            in1=gs[:, :, None].broadcast_to((128, 8, 64)),
                            op=AluOpType.mult),
                        waits=[(s_pe, pv)] + (vstg_free[kc % 2] if half == 0 else []))
                ev2 = ve(lambda v, half=half, vs_=vs_, gs=gs:
                         v.tensor_copy(
                             out=vs_[:, half * 520:half * 520 + 520]
                                 .rearrange('p (h e) -> p h e', e=65)[:, :, 64:65],
                             in_=gs[:, :, None]))
                bank.consumed(s_ve, e1)
            vstg_free[kc % 2] = []
            vv_ = sto(vtd[kc], vs_[:, 0:1040], waits=[(s_ve, ev2)], stream=f'vst{kc % 2}')
            vstg_free[kc % 2] = [vv_]
        xsb_free[sl_] = [(s_pe, s_pe.n)]

    if STAGE.startswith('o1'):
        A.pop()
        A.pop()
        wslots.extend(A.al(1024, BF16) for _ in range(NSLOT))
        barrier()
        return _finish(nc, q, ALL, w, on, stack)

    A.pop()
    A.push()
    wqb = wkb
    NO2 = 256
    NT_O2 = _tiles(3072, NO2)
    QTf = A.al(8 * 3072, BF16)
    xsq = [A.al(8 * NO2, BF16), A.al(8 * NO2, BF16)]
    xsq_free = [[], []]
    for of in range(8):
        wq_lv = wl(wqb[:, 1024 * of:1024 * (of + 1)],
                   ap['o_wq'][:, of, :, :].rearrange('p a b -> p (a b)'),
                   waits=[(s_ac, s_ac.n), (s_pe, s_pe.n)], stream='wk')
    Pq = [Bank(pA.t[:, 0:256]), Bank(pA.t[:, 512:768]),
          Bank(pB.t[:, 0:256]), Bank(pB.t[:, 512:768])]
    for nt_i, (o, n) in enumerate(NT_O2):
        sl_ = nt_i % 2
        xv = wl(xsq[sl_][:, 0:8 * n].rearrange('p (c t) -> p c t', c=8),
                xTfv[:, :, o:o + n],
                waits=xsq_free[sl_] + ([wq_lv] if nt_i == 0 else []),
                stream=f'xq{sl_}')
        for of in range(8):
            bank = Pq[(nt_i * 8 + of) % 4]
            def qmm(t, sl_=sl_, n=n, of=of, bank=bank):
                last = None
                for ci in range(8):
                    last = t.matmul(bank.t[:, 0:n],
                                    wqb[:, of * 1024 + ci * 128:of * 1024 + (ci + 1) * 128],
                                    xsq[sl_][:, ci * n:ci * n + n],
                                    start=(ci == 0), stop=(ci == 7))
                return last
            pv = pe(qmm, waits=[xv] + bank.free)
            qrdy = ac(lambda s, of=of, o=o, n=n, bank=bank:
                      s.copy(out=QTf[:, of * 3072 + o:of * 3072 + o + n],
                             in_=bank.t[:, 0:n]),
                      waits=[(s_pe, pv)])
            bank.consumed(s_ac, qrdy)
        xsq_free[sl_] = [(s_pe, s_pe.n)]
    qf_rdy = s_ac.n

    KTp = A.al(S, BF16)
    vtp = A.al(32 * 130, BF16)
    TDs = A.al(2 * 7040, BF16)
    prA = A.al(1024, BF16); prB = A.al(1024, BF16)
    aob = A.al(8 * 3072, BF16)
    stagB = A.al(3072, BF16)
    rA = A.al(NO2, F32); rB = A.al(NO2, F32)
    rAb = A.al(NO2, BF16); rBb = A.al(NO2, BF16)
    RA = A.al(NO2, F32); RB = A.al(NO2, F32)

    _td0 = TDs[:, :]

    def tdap(h2, kq, q0):
        return bass.AP(tensor=_td0.tensor,
                       offset=int(_td0.offset) + h2 * 7040 + 3071 - q0 + 512 * kq,
                       ap=[[int(_td0.ap[0][0]), 128], [128, 4], [-1, 256]])

    for hc in range(8):
        ktv = wl(KTp[:, :], ktd[hc], waits=[(s_pe, s_pe.n)], stream='ktp')
        vtv = wl(vtp[:, :].rearrange('p (a b) -> p a b', b=130),
                 vtd[:, :, 130 * hc:130 * hc + 130].rearrange('a p b -> p a b'),
                 waits=[(s_pe, s_pe.n)], stream='vtp')
        tdv = wl(TDs[:, 0:7040], tdd[2 * hc], waits=[(s_ve, s_ve.n)], stream='tds')
        tdv = wl(TDs[:, 7040:14080], tdd[2 * hc + 1], stream='tds')
        for nt_i, (q0, n) in enumerate(NT_O2):
            for kq in range(8):
                pvsc = 0
                for kcl in range(4):
                    kc = kq * 4 + kcl
                    def smm(t, kc=kc, kcl=kcl, q0=q0, n=n, hc=hc):
                        t.matmul(pA.t[:, kcl * 256:kcl * 256 + n],
                                 KTp[0:64, kc * 128:(kc + 1) * 128],
                                 QTf[0:64, hc * 3072 + q0:hc * 3072 + q0 + n],
                                 start=True, stop=True)
                        return t.matmul(pB.t[:, kcl * 256:kcl * 256 + n],
                                        KTp[64:128, kc * 128:(kc + 1) * 128],
                                        QTf[64:128, hc * 3072 + q0:hc * 3072 + q0 + n],
                                        start=True, stop=True, tile_position=(64, 0))
                    wt = ([ktv, (s_ac, qf_rdy)] + pA.free + pB.free) \
                        if (kq == 0 and kcl == 0) else []
                    pvsc = pe(smm, waits=wt)
                eA = ac(lambda s: s.activation(out=prA[:, 0:1024], in_=pA.t[:, 0:1024],
                                               func=AF.Exp, scale=1.0 / 32.0),
                        waits=[(s_pe, pvsc)])
                eB = ac(lambda s: s.activation(out=prB[:, 0:1024], in_=pB.t[:, 0:1024],
                                               func=AF.Exp, scale=1.0 / 32.0))
                pA.consumed(s_ac, eA)
                pB.consumed(s_ac, eB)
                tm1 = ve(lambda v, kq=kq, q0=q0: v.tensor_tensor(
                            out=prA[:, 0:1024].rearrange('p (a b) -> p a b', a=4),
                            in0=prA[:, 0:1024].rearrange('p (a b) -> p a b', a=4),
                            in1=tdap(0, kq, q0), op=AluOpType.mult),
                         waits=[(s_ac, eA)] + ([tdv] if (nt_i == 0 and kq == 0) else []))
                tm2 = ve(lambda v, kq=kq, q0=q0: v.tensor_tensor(
                            out=prB[:, 0:1024].rearrange('p (a b) -> p a b', a=4),
                            in0=prB[:, 0:1024].rearrange('p (a b) -> p a b', a=4),
                            in1=tdap(1, kq, q0), op=AluOpType.mult),
                         waits=[(s_ac, eB)])
                def avmm(t, kq=kq, n=n):
                    last = None
                    for kcl in range(4):
                        kc = kq * 4 + kcl
                        st_ = (kc == 0)
                        sp = (kc == 31)
                        t.matmul(p4.t[0:65, 0:n], vtp[:, kc * 130:kc * 130 + 65],
                                 prA[:, kcl * 256:kcl * 256 + n], start=st_, stop=sp)
                        last = t.matmul(p5.t[0:65, 0:n],
                                        vtp[:, kc * 130 + 65:kc * 130 + 130],
                                        prB[:, kcl * 256:kcl * 256 + n],
                                        start=st_, stop=sp)
                    return last
                wt = [(s_ve, tm2), vtv] + ((p4.free + p5.free) if kq == 0 else [])
                pvav = pe(avmm, waits=wt)
            ac(lambda s, n=n: s.activation(out=rA[64:65, 0:n], in_=p4.t[64:65, 0:n],
                                           func=AF.Ln), waits=[(s_pe, pvav)])
            r1 = ac(lambda s, n=n: s.activation(out=rAb[64:65, 0:n], in_=rA[64:65, 0:n],
                                                func=AF.Exp, scale=-1.0))
            ac(lambda s, n=n: s.activation(out=rB[64:65, 0:n], in_=p5.t[64:65, 0:n],
                                           func=AF.Ln))
            r2 = ac(lambda s, n=n: s.activation(out=rBb[64:65, 0:n], in_=rB[64:65, 0:n],
                                                func=AF.Exp, scale=-1.0))
            b1 = pe(lambda t, n=n: t.matmul(p6.t[0:64, 0:n], c_onesb[64:65, 0:64],
                                            rAb[64:65, 0:n], start=True, stop=True),
                    waits=[(s_ac, r1)] + p6.free)
            b2 = pe(lambda t, n=n: t.matmul(p7.t[0:64, 0:n], c_onesb[64:65, 0:64],
                                            rBb[64:65, 0:n], start=True, stop=True),
                    waits=[(s_ac, r2)] + p7.free)
            a1 = ac(lambda s, n=n: s.copy(out=RA[0:64, 0:n], in_=p6.t[0:64, 0:n]),
                    waits=[(s_pe, b1)])
            a2 = ac(lambda s, n=n: s.copy(out=RB[0:64, 0:n], in_=p7.t[0:64, 0:n]),
                    waits=[(s_pe, b2)])
            p6.consumed(s_ac, a1)
            p7.consumed(s_ac, a2)
            v1 = ve(lambda v, hc=hc, q0=q0, n=n: v.tensor_tensor(
                        out=aob[0:64, hc * 3072 + q0:hc * 3072 + q0 + n],
                        in0=p4.t[0:64, 0:n], in1=RA[0:64, 0:n], op=AluOpType.mult),
                    waits=[(s_ac, a1)])
            v2 = ve(lambda v, q0=q0, n=n: v.tensor_tensor(
                        out=stagB[0:64, q0:q0 + n],
                        in0=p5.t[0:64, 0:n], in1=RB[0:64, 0:n], op=AluOpType.mult),
                    waits=[(s_ac, a2)] + (stagB_free if nt_i == 0 else []))
            p4.consumed(s_ve, v1)
            p5.consumed(s_ve, v2)
        pk = ld(aob[64:128, hc * 3072:hc * 3072 + 3072], stagB[0:64, 0:3072],
                waits=[(s_ve, s_ve.n)], stream='pk')
        stagB_free = [pk]
    ao_rdy = [pk]

    x2f = A.al(NO2, F32)
    sqf = A.al(NO2, F32)
    tf = A.al(8 * NO2, F32)
    srow = A.al(8 * NO2, F32)
    for nt_i, (o, n) in enumerate(NT_O2):
        pvs = 0
        t1l = 0
        for c in range(8):
            xv2 = ld(x2f[:, 0:n], xTfv[:, c, o:o + n],
                     waits=[(s_ve, s_ve.n)], stream='x2f')
            t1 = ve(lambda v, c=c, o=o, n=n: v.tensor_tensor(
                        out=tf[:, c * NO2:c * NO2 + n],
                        in0=aob[:, c * 3072 + o:c * 3072 + o + n],
                        in1=x2f[:, 0:n], op=AluOpType.add),
                    waits=[xv2] + (ao_rdy if (nt_i == 0 and c == 0) else [])
                          + ([(s_pe, s_pe.n)] if c == 0 else []))
            sq = ac(lambda s, c=c, n=n: s.square(out=sqf[:, 0:n],
                                                 in_=tf[:, c * NO2:c * NO2 + n]),
                    waits=[(s_ve, t1), (s_pe, s_pe.n)])
            def summ(t, c=c, n=n):
                t.matmul(p4.t[0:1, 0:n], c_onesf[:, 0:1], tf[:, c * NO2:c * NO2 + n],
                         start=(c == 0), stop=(c == 7))
                return t.matmul(p5.t[0:1, 0:n], c_onesf[:, 0:1], sqf[:, 0:n],
                                start=(c == 0), stop=(c == 7))
            wt = [(s_ac, sq)] + ((p4.free + p5.free) if c == 0 else [])
            pvs = pe(summ, waits=wt)
        on('vector', lambda v, pvs=pvs: v.wait_ge(s_pe.h, pvs))
        bc = ln_stats_chain(lambda j, n=n: srow[0:1, j * NO2:j * NO2 + n], n)
        lastv = 0
        for c in range(8):
            ve(lambda v, c=c, n=n: v.tensor_tensor(
                    out=tf[:, c * NO2:c * NO2 + n], in0=tf[:, c * NO2:c * NO2 + n],
                    in1=p6.t[:, 0:n], op=AluOpType.mult),
               waits=[(s_pe, bc)] if c == 0 else [])
            lastv = ve(lambda v, c=c, n=n: v.tensor_tensor(
                    out=tf[:, c * NO2:c * NO2 + n], in0=tf[:, c * NO2:c * NO2 + n],
                    in1=p7.t[:, 0:n], op=AluOpType.add))
        p6.consumed(s_ve, lastv)
        p7.consumed(s_ve, lastv)
        ns = min(n, FT - o)
        if ns > 0:
            sto(stv[0][:, :, 1 + o:1 + o + ns],
                tf[:, 0:8 * NO2].rearrange('p (c t) -> p c t', c=8)[:, :, 0:ns],
                waits=[(s_ve, lastv)])
            sto(x1kv[:, :, o:o + ns],
                tf[:, 0:8 * NO2].rearrange('p (c t) -> p c t', c=8)[:, :, 0:ns])
            if STAGE == 'outer':
                sto(outv[:, :, o:o + ns],
                    tf[:, 0:8 * NO2].rearrange('p (c t) -> p c t', c=8)[:, :, 0:ns])
        barrier()
    A.pop()
    A.pop()
    wslots.extend(A.al(1024, BF16) for _ in range(NSLOT))

    SLmax = 1280

    def emit_layer(i):
        st_in, st_mid = stv[0], stv[1]
        A.push()
        xsb_l = A.al(8 * SLmax, BF16)
        QT = A.al(8 * 1024, BF16)
        KTl = A.al(8 * SLmax, BF16)
        VTl = A.al(10 * 1040, BF16)
        rn = A.al(512, F32)
        rnb = A.al(2304, BF16)
        wvbl = A.al(8 * D, BF16)
        for ci in range(8):
            wv_rdy = wl(wvbl[:, ci * D:(ci + 1) * D], ap[f'wv{i}'][:, ci, :],
                        waits=[(s_pe, s_pe.n)] if ci == 0 else [], stream='wvb')

        def emit_mega(m0, mn):
            s0 = max(0, m0 - 256)
            s1 = m0 + mn
            SLn = s1 - s0
            qoff = m0 - s0
            nkc = (SLn + 127) // 128
            A.push()
            rtab = A.al(4 * SLmax, F32)
            rlo = A.al(SLmax, F32)
            At = A.al(SLmax, F32)
            Bt = A.al(SLmax, F32)
            raw = A.al(8 * SLmax, F32)
            sqb = [A.al(512, BF16), A.al(512, BF16)]
            ts1 = [A.al(512, F32), A.al(512, F32)]
            vbs = [A.al(512, BF16), A.al(512, BF16)]

            wl(xsb_l[:, 0:4 * SLn].rearrange('p (c t) -> p c t', c=4),
               st_in[:, 0:4, 1 + s0:1 + s1],
               waits=[(s_pe, s_pe.n), (s_ac, s_ac.n)], stream='xspan')
            xv1 = wl(xsb_l[:, 4 * SLn:8 * SLn].rearrange('p (c t) -> p c t', c=4),
                     st_in[:, 4:8, s0:s1], stream='xspan')
            rtv = 0
            for j, nm in enumerate(('rcq', 'rsq', 'rck', 'rsk')):
                rtv = ld(rtab[:, j * SLmax:j * SLmax + SLn], ap[nm][:, s0:s1],
                         waits=[(s_ve, s_ve.n)] if j == 0 else [], stream='rtab')
            LONM = ('rcql', 'rsql', 'rckl', 'rskl')

            def rslice(j, a, b):
                return rtab[:, j * SLmax + a:j * SLmax + b]

            def side(sb_, wname, dest, dcols, r0, rl, tabj):
                tv1 = ve(lambda v: v.tensor_scalar_mul(out=At[:, 0:rl], in0=rslice(tabj[0], r0, r0 + rl),
                                                       scalar1=lvec(i, sb_ // 8)),
                         waits=[rtv])
                lv1 = ld(rlo[:, 0:rl], ap[LONM[tabj[0]]][:, s0 + r0:s0 + r0 + rl],
                         waits=[(s_ve, s_ve.n)], stream='rlo')
                tvx = ve(lambda v: v.scalar_tensor_tensor(
                       out=At[:, 0:rl], in0=rlo[:, 0:rl],
                       scalar=lvec(i, sb_ // 8), in1=At[:, 0:rl],
                       op0=AluOpType.mult, op1=AluOpType.add), waits=[lv1])
                tv2 = ve(lambda v: v.tensor_scalar_mul(out=Bt[:, 0:rl], in0=rslice(tabj[1], r0, r0 + rl),
                                                       scalar1=lvec(i, sb_ // 8)))
                lv2 = ld(rlo[:, 0:rl], ap[LONM[tabj[1]]][:, s0 + r0:s0 + r0 + rl],
                         waits=[(s_ve, s_ve.n)], stream='rlo')
                tv2 = ve(lambda v: v.scalar_tensor_tensor(
                       out=Bt[:, 0:rl], in0=rlo[:, 0:rl],
                       scalar=lvec(i, sb_ // 8), in1=Bt[:, 0:rl],
                       op0=AluOpType.mult, op1=AluOpType.add), waits=[lv2])
                ntl = _tiles(rl, 512)
                Pb = [Bank(pA.t[:, 0:512]), Bank(pA.t[:, 512:1024]),
                      Bank(pB.t[:, 0:512]), Bank(pB.t[:, 512:1024])]
                for bk in Pb:
                    bk.free = pA.free + pB.free
                SSb = [p4, p5, p6][:len(ntl)]
                wpre = []
                for of in range(2):
                    wpre.append(load_w(wname, of))
                pvlast = 0
                for of in range(8):
                    wsl, wlv, wrel = wpre[of]
                    if of + 2 < 8:
                        wpre.append(load_w(wname, of + 2))
                    for tix, (to, tn) in enumerate(ntl):
                        bk = Pb[(of * len(ntl) + tix) % 4]
                        def pmm(t, wsl=wsl, to=to, tn=tn, bk=bk, r0=r0):
                            last = None
                            for ci in range(8):
                                last = t.matmul(bk.t[:, 0:tn], wsl(ci),
                                                xsb_l[:, ci * SLn + r0 + to:ci * SLn + r0 + to + tn],
                                                start=(ci == 0), stop=(ci == 7))
                            return last
                        pv = pe(pmm, waits=[wlv, xv1] + bk.free)
                        a1 = ac(lambda s, of=of, to=to, tn=tn, bk=bk:
                                s.copy(out=raw[:, of * SLn + to:of * SLn + to + tn],
                                       in_=bk.t[:, 0:tn]), waits=[(s_pe, pv)])
                        sl2 = (of * len(ntl) + tix) % 2
                        a2 = ac(lambda s, of=of, to=to, tn=tn, sl2=sl2:
                                s.square(out=sqb[sl2][:, 0:tn],
                                         in_=raw[:, of * SLn + to:of * SLn + to + tn]),
                                waits=[(s_pe, s_pe.n)])
                        bk.consumed(s_ac, a1)
                        def ssm(t, of=of, tn=tn, sl2=sl2, sb_=sb_, tix=tix):
                            return t.matmul(SSb[tix].t[0:16, 0:tn],
                                            c_indss[:, of * 16:of * 16 + 16],
                                            sqb[sl2][:, 0:tn],
                                            start=(of == 0), stop=(of == 7))
                        wt = [(s_ac, a2)] + (SSb[tix].free if of == 0 else [])
                        pvlast = pe(ssm, waits=wt)
                    wrel()
                rnv = 0
                for tix, (to, tn) in enumerate(ntl):
                    roff = (sb_ // 8) * 1024
                    c1 = ac(lambda s, tn=tn, tix=tix:
                            s.activation(out=rn[0:16, 0:tn],
                                         in_=SSb[tix].t[0:16, 0:tn],
                                         func=AF.Sqrt), waits=[(s_pe, pvlast)])
                    SSb[tix].consumed(s_ac, c1)
                    c2 = ve(lambda v, tn=tn:
                            v.tensor_scalar_max(out=rn[0:16, 0:tn],
                                                in0=rn[0:16, 0:tn],
                                                scalar1=1e-12),
                            waits=[(s_ac, c1)])
                    ac(lambda s, tn=tn:
                       s.activation(out=rn[0:16, 0:tn],
                                    in_=rn[0:16, 0:tn],
                                    func=AF.Ln), waits=[(s_ve, c2)])
                    rnv = ac(lambda s, to=to, tn=tn, roff=roff:
                             s.activation(out=rnb[0:16, roff + to:roff + to + tn],
                                          in_=rn[0:16, 0:tn],
                                          func=AF.Exp, scale=-1.0))
                Rb2 = [Bank(pA.t[:, 0:512]), Bank(pA.t[:, 512:1024])]
                Pp = [Bank(pB.t[:, 0:512]), Bank(pB.t[:, 512:1024])]
                for bk in Rb2 + Pp:
                    bk.free = [(s_ac, s_ac.n), (s_ve, s_ve.n)]
                for of in range(8):
                    for tix, (to, tn) in enumerate(ntl):
                        k2 = (of * len(ntl) + tix) % 2
                        v1 = ve(lambda v, of=of, to=to, tn=tn, k2=k2:
                                v.tensor_tensor(out=ts1[k2][:, 0:tn],
                                                in0=raw[:, of * SLn + to:of * SLn + to + tn],
                                                in1=At[:, to:to + tn], op=AluOpType.mult),
                                waits=[(s_ac, rnv)] if (of == 0 and tix == 0) else [])
                        v2 = ve(lambda v, of=of, to=to, tn=tn, k2=k2:
                                v.tensor_tensor(out=vbs[k2][:, 0:tn],
                                                in0=raw[:, of * SLn + to:of * SLn + to + tn],
                                                in1=Bt[:, to:to + tn], op=AluOpType.mult),
                                waits=[(s_pe, s_pe.n)])
                        pp = Pp[k2]
                        pv1 = pe(lambda t, tn=tn, k2=k2, pp=pp:
                                 t.matmul(pp.t[:, 0:tn], c_perm[:, :], vbs[k2][:, 0:tn],
                                          start=True, stop=True),
                                 waits=[(s_ve, v2)] + pp.free)
                        v3 = ve(lambda v, tn=tn, k2=k2, pp=pp:
                                v.tensor_tensor(out=ts1[k2][:, 0:tn], in0=ts1[k2][:, 0:tn],
                                                in1=pp.t[:, 0:tn], op=AluOpType.add),
                                waits=[(s_pe, pv1)])
                        pp.consumed(s_ve, v3)
                        rb = Rb2[k2]
                        roff2 = (sb_ // 8) * 1024
                        pv2 = pe(lambda t, of=of, to=to, tn=tn, rb=rb, roff2=roff2:
                                 t.matmul(rb.t[0:128, 0:tn],
                                          c_indbcf[0:16, of * 128:of * 128 + 128],
                                          rnb[0:16, roff2 + to:roff2 + to + tn],
                                          start=True, stop=True),
                                 waits=[(s_ac, rnv)] + rb.free)
                        v4 = ve(lambda v, of=of, to=to, tn=tn, k2=k2, rb=rb:
                                v.tensor_tensor(out=dest(of, to, tn), in0=ts1[k2][:, 0:tn],
                                                in1=rb.t[:, 0:tn], op=AluOpType.mult),
                                waits=[(s_pe, pv2)])
                        rb.consumed(s_ve, v4)
                pA.free = [(s_ve, s_ve.n)]
                pB.free = [(s_ve, s_ve.n)]

            side(0, f'wq{i}',
                 lambda of, to, tn: QT[:, of * mn + to:of * mn + to + tn],
                 mn, qoff, mn, (0, 1))
            side(8, f'wk{i}',
                 lambda of, to, tn: KTl[:, of * SLn + to:of * SLn + to + tn],
                 SLn, 0, SLn, (2, 3))

            for tb in range(0, SLn, 128):
                kcr = tb // 128
                kw = min(128, SLn - tb)
                for half in range(2):
                    bk = p4 if half == 0 else p5
                    def vmm(t, tb=tb, kw=kw, half=half, bk=bk):
                        last = None
                        for ci in range(8):
                            last = t.matmul(bk.t[0:kw, 0:512],
                                            xsb_l[:, ci * SLn + tb:ci * SLn + tb + kw],
                                            wvbl[:, ci * D + half * 512:ci * D + half * 512 + 512],
                                            start=(ci == 0), stop=(ci == 7))
                        return last
                    pv = pe(vmm, waits=[wv_rdy, xv1] + bk.free)
                    e1 = ve(lambda v, kcr=kcr, kw=kw, half=half, bk=bk:
                            v.tensor_copy(
                                out=VTl[0:kw, kcr * 1040 + half * 520:kcr * 1040 + half * 520 + 520]
                                    .rearrange('p (h e) -> p h e', h=8)[:, :, 0:64],
                                in_=bk.t[0:kw, 0:512].rearrange('p (h e) -> p h e', h=8)),
                            waits=[(s_pe, pv)])
                    bk.consumed(s_ve, e1)
                ve(lambda v, kcr=kcr, kw=kw:
                   v.memset(VTl[0:kw, kcr * 1040:kcr * 1040 + 1040]
                            .rearrange('p (h e) -> p h e', e=65)[:, :, 64:65], 1.0))
            A.pop()

            A.push()
            prA = A.al(1024, BF16); prB = A.al(1024, BF16)
            rA = A.al(256, F32); rB = A.al(256, F32)
            rAb = A.al(256, BF16); rBb = A.al(256, BF16)
            RA = A.al(256, F32); RB = A.al(256, F32)
            aob = A.al(8 * 1024, BF16)
            stagBb = A.al(8 * 1024, BF16)
            res2 = [A.al(512, F32), A.al(512, F32)]
            yout = [A.al(512, F32), A.al(512, F32)]
            last_y = [[], []]
            qts = _tiles(mn, 256)
            for hc in range(8):
                for (qo, qn) in qts:
                    q0 = m0 + qo
                    kcos = [k for k in range(4) if q0 - 256 + 128 * k >= 0
                            and q0 - 256 + 128 * k < s1]
                    pvsc = 0
                    for kco in kcos:
                        kabs = q0 - 256 + 128 * kco
                        kcr = (kabs - s0) // 128
                        kw = min(128, SLn - kcr * 128)
                        def smm(t, hc=hc, kco=kco, kcr=kcr, kw=kw, qo=qo, qn=qn):
                            t.matmul(pA.t[0:kw, kco * 256:kco * 256 + qn],
                                     KTl[0:64, hc * SLn + kcr * 128:hc * SLn + kcr * 128 + kw],
                                     QT[0:64, hc * mn + qo:hc * mn + qo + qn],
                                     start=True, stop=False)
                            t.matmul(pA.t[0:kw, kco * 256:kco * 256 + qn],
                                     c_i128[0:kw, 0:kw],
                                     c_masks[0:kw, kco * 256:kco * 256 + qn],
                                     start=False, stop=True)
                            t.matmul(pB.t[0:kw, kco * 256:kco * 256 + qn],
                                     KTl[64:128, hc * SLn + kcr * 128:hc * SLn + kcr * 128 + kw],
                                     QT[64:128, hc * mn + qo:hc * mn + qo + qn],
                                     start=True, stop=False, tile_position=(64, 0))
                            return t.matmul(pB.t[0:kw, kco * 256:kco * 256 + qn],
                                            c_i128[0:kw, 0:kw],
                                            c_masks[0:kw, kco * 256:kco * 256 + qn],
                                            start=False, stop=True)
                        wt = (pA.free + pB.free) if kco == kcos[0] else []
                        pvsc = pe(smm, waits=wt)
                    eA = ac(lambda s: s.activation(out=prA[:, 0:1024], in_=pA.t[:, 0:1024],
                                                   func=AF.Exp, scale=8.0),
                            waits=[(s_pe, pvsc)])
                    eB = ac(lambda s: s.activation(out=prB[:, 0:1024], in_=pB.t[:, 0:1024],
                                                   func=AF.Exp, scale=8.0))
                    pA.consumed(s_ac, eA)
                    pB.consumed(s_ac, eB)
                    def avmm(t, kcos=tuple(kcos), q0=q0, qn=qn, s0=s0, hc=hc):
                        last = None
                        for ix, kco in enumerate(kcos):
                            kabs = q0 - 256 + 128 * kco
                            kcr = (kabs - s0) // 128
                            kw = min(128, SLn - kcr * 128)
                            st_ = (ix == 0)
                            sp = (ix == len(kcos) - 1)
                            t.matmul(p4.t[0:65, 0:qn],
                                     VTl[0:kw, kcr * 1040 + 65 * 2 * hc:kcr * 1040 + 65 * 2 * hc + 65],
                                     prA[0:kw, kco * 256:kco * 256 + qn], start=st_, stop=sp)
                            last = t.matmul(p5.t[0:65, 0:qn],
                                            VTl[0:kw, kcr * 1040 + 65 * (2 * hc + 1):kcr * 1040 + 65 * (2 * hc + 1) + 65],
                                            prB[0:kw, kco * 256:kco * 256 + qn], start=st_, stop=sp)
                        return last
                    pvav = pe(avmm, waits=[(s_ac, eB)] + p4.free + p5.free)
                    ac(lambda s, qn=qn: s.activation(out=rA[64:65, 0:qn],
                                                     in_=p4.t[64:65, 0:qn], func=AF.Ln),
                       waits=[(s_pe, pvav)])
                    r1 = ac(lambda s, qn=qn: s.activation(out=rAb[64:65, 0:qn],
                                                          in_=rA[64:65, 0:qn],
                                                          func=AF.Exp, scale=-1.0))
                    ac(lambda s, qn=qn: s.activation(out=rB[64:65, 0:qn],
                                                     in_=p5.t[64:65, 0:qn], func=AF.Ln))
                    r2 = ac(lambda s, qn=qn: s.activation(out=rBb[64:65, 0:qn],
                                                          in_=rB[64:65, 0:qn],
                                                          func=AF.Exp, scale=-1.0))
                    b1 = pe(lambda t, qn=qn: t.matmul(p6.t[0:64, 0:qn], c_onesb[64:65, 0:64],
                                                      rAb[64:65, 0:qn], start=True, stop=True),
                            waits=[(s_ac, r1)] + p6.free)
                    b2 = pe(lambda t, qn=qn: t.matmul(p7.t[0:64, 0:qn], c_onesb[64:65, 0:64],
                                                      rBb[64:65, 0:qn], start=True, stop=True),
                            waits=[(s_ac, r2)] + p7.free)
                    a1 = ac(lambda s, qn=qn: s.copy(out=RA[0:64, 0:qn], in_=p6.t[0:64, 0:qn]),
                            waits=[(s_pe, b1)])
                    a2 = ac(lambda s, qn=qn: s.copy(out=RB[0:64, 0:qn], in_=p7.t[0:64, 0:qn]),
                            waits=[(s_pe, b2)])
                    p6.consumed(s_ac, a1)
                    p7.consumed(s_ac, a2)
                    v1 = ve(lambda v, hc=hc, qo=qo, qn=qn: v.tensor_tensor(
                                out=aob[0:64, hc * mn + qo:hc * mn + qo + qn],
                                in0=p4.t[0:64, 0:qn], in1=RA[0:64, 0:qn], op=AluOpType.mult),
                            waits=[(s_ac, a1)])
                    v2 = ve(lambda v, hc=hc, qo=qo, qn=qn: v.tensor_tensor(
                                out=stagBb[0:64, hc * mn + qo:hc * mn + qo + qn],
                                in0=p5.t[0:64, 0:qn], in1=RB[0:64, 0:qn], op=AluOpType.mult),
                            waits=[(s_ac, a2)])
                    p4.consumed(s_ve, v1)
                    p5.consumed(s_ve, v2)
            pk = ld(aob[64:128, 0:8 * mn], stagBb[0:64, 0:8 * mn],
                    waits=[(s_ve, s_ve.n)], stream='pk')
            wpre = [load_w(f'wo{i}', 0), load_w(f'wo{i}', 1)]
            ntm = _tiles(mn, 512)
            for of in range(8):
                wsl, wlv, wrel = wpre[of]
                if of + 2 < 8:
                    wpre.append(load_w(f'wo{i}', of + 2))
                for tix, (to, tn) in enumerate(ntm):
                    bk = p6 if (of * len(ntm) + tix) % 2 == 0 else p7
                    slr = (of * len(ntm) + tix) % 2
                    xr = ld(res2[slr][:, 0:tn], st_in[:, of, 1 + m0 + to:1 + m0 + to + tn],
                            waits=[(s_ve, s_ve.n)], stream=f'res{slr}')
                    def womm(t, wsl=wsl, to=to, tn=tn, bk=bk):
                        last = None
                        for ci in range(8):
                            last = t.matmul(bk.t[:, 0:tn], wsl(ci),
                                            aob[:, ci * mn + to:ci * mn + to + tn],
                                            start=(ci == 0), stop=(ci == 7))
                        return last
                    pv = pe(womm, waits=[wlv, pk] + bk.free)
                    vv = ve(lambda v, tn=tn, bk=bk, slr=slr: v.tensor_tensor(
                                out=yout[slr][:, 0:tn], in0=bk.t[:, 0:tn],
                                in1=res2[slr][:, 0:tn], op=AluOpType.add),
                            waits=[(s_pe, pv), xr] + last_y[slr])
                    bk.consumed(s_ve, vv)
                    last_y[slr] = [sto(st_mid[:, of, 1 + m0 + to:1 + m0 + to + tn],
                                       yout[slr][:, 0:tn],
                                       waits=[(s_ve, vv)], stream=f'sy{slr}')]
                wrel()
            A.pop()
            barrier()

        for (m0_, mn_) in MEGAS:
            emit_mega(m0_, mn_)
        A.pop()

        A.push()
        xsf = A.al(8 * 512, F32)
        sqf = [A.al(512, F32), A.al(512, F32)]
        t1s = [A.al(512, F32), A.al(512, F32)]
        xnb = A.al(8 * 512, BF16)
        gels = [A.al(512, F32), A.al(512, F32)]
        prodb = A.al(W1C * 512, BF16)
        resf = [A.al(512, F32), A.al(512, F32)]
        zf = [A.al(512, F32), A.al(512, F32)]
        last_z = [[], []]
        srow2 = A.al(8 * 512, F32)


        for (o, n) in NT_OV:
            ld(xsf[:, 0:4 * n].rearrange('p (c t) -> p c t', c=4),
               st_mid[:, 0:4, 1 + o:1 + o + n],
               waits=[(s_ve, s_ve.n), (s_ac, s_ac.n)], stream='ffnx')
            xv1 = ld(xsf[:, 4 * n:8 * n].rearrange('p (c t) -> p c t', c=4),
                     st_mid[:, 4:8, o:o + n], stream='ffnx')
            pvs = 0
            for c in range(8):
                slc = c % 2
                sq = ac(lambda s, c=c, n=n, slc=slc: s.square(out=sqf[slc][:, 0:n],
                                                              in_=xsf[:, c * n:c * n + n]),
                        waits=([xv1] if c == 0 else []) + [(s_pe, s_pe.n)])
                def summ(t, c=c, n=n, slc=slc):
                    t.matmul(p4.t[0:1, 0:n], c_onesf[:, 0:1], xsf[:, c * n:c * n + n],
                             start=(c == 0), stop=(c == 7))
                    return t.matmul(p5.t[0:1, 0:n], c_onesf[:, 0:1], sqf[slc][:, 0:n],
                                    start=(c == 0), stop=(c == 7))
                wt = [(s_ac, sq)] + ((p4.free + p5.free) if c == 0 else [])
                pvs = pe(summ, waits=wt)
            on('vector', lambda v, pvs=pvs: v.wait_ge(s_pe.h, pvs))
            bc = ln_stats_chain(lambda j, n=n: srow2[0:1, j * 512:j * 512 + n], n)
            lnv = 0
            for c in range(8):
                slc = c % 2
                va = ve(lambda v, c=c, n=n, slc=slc: v.tensor_tensor(
                            out=t1s[slc][:, 0:n], in0=xsf[:, c * n:c * n + n],
                            in1=p6.t[:, 0:n], op=AluOpType.mult),
                        waits=[(s_pe, bc)] if c == 0 else [])
                vb = ve(lambda v, c=c, n=n, slc=slc: v.tensor_tensor(
                            out=t1s[slc][:, 0:n], in0=t1s[slc][:, 0:n],
                            in1=p7.t[:, 0:n], op=AluOpType.add))
                lnv = ve(lambda v, c=c, n=n, slc=slc: v.tensor_scalar(
                            out=xnb[:, c * n:c * n + n], in0=t1s[slc][:, 0:n],
                            scalar1=lvec(i, 2 + c), scalar2=lvec(i, 10 + c),
                            op0=AluOpType.mult, op1=AluOpType.add))
            p6.consumed(s_ve, lnv)
            p7.consumed(s_ve, lnv)
            wpre = [load_w(f'w1g{i}', 0), load_w(f'w1a{i}', 0),
                    load_w(f'w1g{i}', 1), load_w(f'w1a{i}', 1)]
            for af in range(W1C):
                wg, wglv, wgrel = wpre[2 * af]
                wa, walv, warel = wpre[2 * af + 1]
                if af + 2 < W1C:
                    wpre.append(load_w(f'w1g{i}', af + 2))
                    wpre.append(load_w(f'w1a{i}', af + 2))
                bkg = Bank(pA.t[:, 0:512]) if af % 2 == 0 else Bank(pB.t[:, 0:512])
                bka = Bank(pA.t[:, 512:1024]) if af % 2 == 0 else Bank(pB.t[:, 512:1024])
                src = pA if af % 2 == 0 else pB
                bkg.free = src.free
                bka.free = src.free
                def w1mm(t, wg=wg, wa=wa, n=n, bkg=bkg, bka=bka):
                    for ci in range(8):
                        t.matmul(bkg.t[:, 0:n], wg(ci), xnb[:, ci * n:ci * n + n],
                                 start=(ci == 0), stop=(ci == 7))
                    last = None
                    for ci in range(8):
                        last = t.matmul(bka.t[:, 0:n], wa(ci), xnb[:, ci * n:ci * n + n],
                                        start=(ci == 0), stop=(ci == 7))
                    return last
                pv = pe(w1mm, waits=[wglv, walv, (s_ve, lnv)] + bkg.free)
                slg = af % 2
                ga = ac(lambda s, af=af, n=n, slg=slg, bkg=bkg: s.activation(
                            out=gels[slg][:, 0:n], in_=bkg.t[:, 0:n], func=AF.Gelu,
                            bias=lvec(i, 40 + af)),
                        waits=[(s_pe, pv), (s_ve, s_ve.n)])
                pr = ve(lambda v, af=af, n=n, slg=slg, bka=bka: v.scalar_tensor_tensor(
                            out=prodb[:, af * n:af * n + n], in0=bka.t[:, 0:n],
                            scalar=lvec(i, 18 + af), in1=gels[slg][:, 0:n],
                            op0=AluOpType.add, op1=AluOpType.mult),
                        waits=[(s_ac, ga)])
                src.consumed(s_ve, pr)
                src.also_consumed(s_ac, ga)
                wgrel(); warel()
            w2pre = [load_w(f'w2{i}', 0, ni=8, sub=0), load_w(f'w2{i}', 0, ni=8, sub=8),
                     load_w(f'w2{i}', 0, ni=6, sub=16)]
            for of in range(8):
                if of + 1 < 8:
                    w2pre.append(load_w(f'w2{i}', of + 1, ni=8, sub=0))
                    w2pre.append(load_w(f'w2{i}', of + 1, ni=8, sub=8))
                    w2pre.append(load_w(f'w2{i}', of + 1, ni=6, sub=16))
                g0, g1, g2 = w2pre[3 * of], w2pre[3 * of + 1], w2pre[3 * of + 2]
                bk = p6 if of % 2 == 0 else p7
                slr = of % 2
                xr = ld(resf[slr][:, 0:n], st_mid[:, of, 1 + o:1 + o + n],
                        waits=[(s_ve, s_ve.n)], stream=f'fr{slr}')
                def w2mm(t, g0=g0, g1=g1, g2=g2, n=n, bk=bk):
                    last = None
                    for af in range(W1C):
                        grp, k = (g0, af) if af < 8 else ((g1, af - 8) if af < 16 else (g2, af - 16))
                        last = t.matmul(bk.t[:, 0:n], grp[0](k),
                                        prodb[:, af * n:af * n + n],
                                        start=(af == 0), stop=(af == W1C - 1))
                    return last
                pv = pe(w2mm, waits=[g0[1], g1[1], g2[1],
                                     (s_ve, s_ve.n)] + bk.free)
                vv = ve(lambda v, of=of, n=n, bk=bk, slr=slr: v.scalar_tensor_tensor(
                            out=zf[slr][:, 0:n], in0=bk.t[:, 0:n], scalar=lvec(i, 62 + of),
                            in1=resf[slr][:, 0:n], op0=AluOpType.add, op1=AluOpType.add),
                        waits=[(s_pe, pv), xr] + last_z[slr])
                bk.consumed(s_ve, vv)
                last_z[slr] = [sto(st_in[:, of, 1 + o:1 + o + n], zf[slr][:, 0:n],
                                   waits=[(s_ve, vv)], stream=f'sz{slr}')]
                g0[2](); g1[2](); g2[2]()
            barrier()
        A.pop()

    def emit_final():
        A.push()
        tb = A.al(8 * 512, F32)
        x1b = A.al(8 * 512, F32)
        sqf = [A.al(512, F32), A.al(512, F32)]
        srow3 = A.al(8 * 512, F32)
        outb = A.al(8 * 512, F32)


        for (o, n) in NT_OV:
            zv = ld(tb[:, 0:8 * n].rearrange('p (c t) -> p c t', c=8),
                    stv[0][:, :, 1 + o:1 + o + n], waits=[(s_ve, s_ve.n)], stream='fz')
            xv = ld(x1b[:, 0:8 * n].rearrange('p (c t) -> p c t', c=8),
                    x1kv[:, :, o:o + n], stream='fx')
            pvs = 0
            for c in range(8):
                slc = c % 2
                va = ve(lambda v, c=c, n=n: v.tensor_tensor(
                            out=tb[:, c * n:c * n + n], in0=tb[:, c * n:c * n + n],
                            in1=x1b[:, c * n:c * n + n], op=AluOpType.add),
                        waits=[zv, xv] if c == 0 else [])
                sq = ac(lambda s, c=c, n=n, slc=slc: s.square(out=sqf[slc][:, 0:n],
                                                              in_=tb[:, c * n:c * n + n]),
                        waits=[(s_ve, va), (s_pe, s_pe.n)])
                def summ(t, c=c, n=n, slc=slc):
                    t.matmul(p4.t[0:1, 0:n], c_onesf[:, 0:1], tb[:, c * n:c * n + n],
                             start=(c == 0), stop=(c == 7))
                    return t.matmul(p5.t[0:1, 0:n], c_onesf[:, 0:1], sqf[slc][:, 0:n],
                                    start=(c == 0), stop=(c == 7))
                wt = [(s_ac, sq)] + ((p4.free + p5.free) if c == 0 else [])
                pvs = pe(summ, waits=wt)
            on('vector', lambda v, pvs=pvs: v.wait_ge(s_pe.h, pvs))
            bc = ln_stats_chain(lambda j, n=n: srow3[0:1, j * 512:j * 512 + n], n)
            lastv = 0
            for c in range(8):
                ve(lambda v, c=c, n=n: v.tensor_tensor(
                        out=outb[:, c * n:c * n + n], in0=tb[:, c * n:c * n + n],
                        in1=p6.t[:, 0:n], op=AluOpType.mult),
                   waits=[(s_pe, bc)] if c == 0 else [])
                lastv = ve(lambda v, c=c, n=n: v.tensor_tensor(
                        out=outb[:, c * n:c * n + n], in0=outb[:, c * n:c * n + n],
                        in1=p7.t[:, 0:n], op=AluOpType.add))
            p6.consumed(s_ve, lastv)
            p7.consumed(s_ve, lastv)
            sto(outv[:, :, o:o + n],
                outb[:, 0:8 * n].rearrange('p (c t) -> p c t', c=8),
                waits=[(s_ve, lastv)])
            barrier()
        A.pop()

    if STAGE == 'full':
        for i in range(NLAYERS):
            emit_layer(i)
        emit_final()

    return _finish(nc, q, ALL, w, on, stack)


def _finish(nc, q, ALL, w, on, stack):
    final = tuple((c, c.n) for c in ALL)
    for e in q:
        def f(eng, final=final):
            w(eng, final)
        on(e, f)
    with nc.Block() as block:
        @block.sync
        def _(sy):
            for f in q['sync']:
                f(sy)
        @block.vector
        def _(v):
            for f in q['vector']:
                f(v)
        @block.scalar
        def _(s):
            for f in q['scalar']:
                f(s)
        @block.tensor
        def _(t):
            for f in q['tensor']:
                f(t)
        @block.gpsimd
        def _(g):
            for f in q['gpsimd']:
                f(g)
    stack.close()
    return nc


_CACHE = {}


def kernel(**inputs):
    in_maps = prep_inputs(inputs)
    if 'nc' not in _CACHE:
        _CACHE['nc'] = build_nc()
    res = run_bass_kernel_spmd(_CACHE['nc'], in_maps, core_ids=list(range(8)))
    full = np.zeros((B_, S, D), np.float32)
    for c in range(8):
        o = res.results[c]['out']
        b = c // 2
        if c % 2 == 0:
            full[b, 0:SPLIT, :] = o[:, 0:SPLIT].T
        else:
            full[b, SPLIT:S, :] = o[:, SPLIT - (S - FT):FT].T
    return full
